# version 16
# baseline (speedup 1.0000x reference)
import numpy as np
from contextlib import ExitStack

import concourse.bass as bass
DMALOG = None
qx_ref = [None]
import concourse.mybir as mybir
from concourse.bass_utils import run_bass_kernel_spmd

F32 = mybir.dt.float32
BF16 = mybir.dt.bfloat16
AF = mybir.ActivationFunctionType
ALU = mybir.AluOpType
AX = mybir.AxisListType

H = 2048
T = 2048
NH = 16
DH = 128
NE = 8
FI = 1408
FS = 2816
HC = 16
TC = 512
NTC = 4
NSUB = 16
FC = 11
SFC = 22
NKC = NE * FC + SFC
EPS = 1e-5
ISQD = float(1.0 / np.sqrt(128.0))


def ceil_div(a, b):
    return (a + b - 1) // b


def build_program(n_blocks=4, debug=False):
    nc = bass.Bass("TRN2", target_bir_lowering=False)
    es = ExitStack()

    def din(name, shape, dtype=F32):
        return nc.dram_tensor(name, shape, dtype, kind="ExternalInput").ap()

    xT_in = din("xT_in", [H, T])
    wr_tl = din("wr_tl", [HC, 128, H]); wk_tl = din("wk_tl", [HC, 128, H])
    wv_tl = din("wv_tl", [HC, 128, H]); wo_tl = din("wo_tl", [HC, 128, H])
    td_pt = din("td_pt", [128, NH]); tf_pt = din("tf_pt", [128, NH])
    tmk_pt = din("tmk_pt", [128, NH]); tmv_pt = din("tmv_pt", [128, NH])
    tmr_pt = din("tmr_pt", [128, NH])
    wqh_tl = din("wqh_tl", [NH, 128, H]); wkh_tl = din("wkh_tl", [NH, 128, H])
    wvb_tl = din("wvb_tl", [HC, 128, H]); wob_tl = din("wob_tl", [HC, 128, H])
    bq_r = din("bq_r", [1, H]); bk_r = din("bk_r", [1, H])
    bv_r = din("bv_r", [1, H]); bo_r = din("bo_r", [1, H])
    gw_T = din("gw_T", [H, NE])
    gate_tl = din("gate_tl", [NE * FC, 128, H]); up_tl = din("up_tl", [NE * FC, 128, H])
    shg_tl = din("shg_tl", [SFC, 128, H]); shu_tl = din("shu_tl", [SFC, 128, H])
    dwn_tl = din("dwn_tl", [HC * 7, 128, H])
    ident_in = din("ident_in", [128, 128])
    masks_in = din("masks_in", [128, 4 * TC])

    xT_out = nc.dram_tensor("xT_out", [H, T], F32, kind="ExternalOutput").ap()

    kscr = "ExternalOutput" if debug else "Internal"
    x_res = nc.dram_tensor("x_res", [H, T], F32, kind="Internal").ap()
    kv_sp = nc.dram_tensor("kv_sp", [H, T], BF16, kind=kscr).ap()
    r_sp = nc.dram_tensor("r_sp", [H, T], BF16, kind=kscr).ap()
    wkv_sp = nc.dram_tensor("wkv_sp", [H, T], BF16, kind=kscr).ap()
    act_sp = nc.dram_tensor("act_sp", [NKC * 128, T], BF16, kind=kscr).ap()
    z_sp = nc.dram_tensor("z_sp", [H, T], BF16, kind=kscr).ap()
    h_dump = (nc.dram_tensor("h_dump", [4, H, T], BF16, kind="ExternalOutput").ap()
              if debug else None)
    M_dbg = {"ln": 0}
    wrow_sp = nc.dram_tensor("wrow_sp", [128, 128], F32, kind=kscr).ap()
    max_sp = nc.dram_tensor("max_sp", [NH * NSUB, 128], F32, kind="Internal").ap()
    dwb_sp = nc.dram_tensor("dwb_sp", [HC * 7, 128, 16 * 128], BF16,
                            kind="Internal").ap()
    wbc_sp = nc.dram_tensor("wbc_sp", [80, 128, 16 * 128], BF16,
                            kind="Internal").ap()

    def sb(name, shape, dt):
        return es.enter_context(nc.sbuf_tensor(name, shape, dt))

    h_full = sb("h_full", [128, HC, T], BF16)
    r1 = sb("r1", [128, 32768], BF16)
    r1_d = r1[:, 0:8192].rearrange("p (c t) -> p c t", c=HC)
    r1_mix = [r1[:, 8192:16384].rearrange("p (c t) -> p c t", c=HC),
              r1[:, 16384:24576].rearrange("p (c t) -> p c t", c=HC)]
    r1f = r1[:, 0:16384].bitcast(F32)
    r1_tbuf = r1f[:, 0:2049]
    r1_stmp = r1f[:, 2049:4097]
    r1_kvld = r1[:, 16384:20480].rearrange("p (b t) -> p b t", b=2)
    r1_rld = r1[:, 20480:24576].rearrange("p (b t) -> p b t", b=2)
    r1_pT = r1[:, 0:8192].rearrange("p (c t) -> p c t", c=16)
    r1_vg = r1[:, 8192:16384].rearrange("p (c t) -> p c t", c=16)
    r1_q = r1[:, 16384:18432]
    r1_k = r1[:, 18432:20480]
    r1_nm = r1[:, 20480:24576].rearrange("p (b t) -> p b t", b=2)
    r1_bias = r1[:, 24576:32768].rearrange("p (b t) -> p b t", b=4)
    r1_zb = r1[:, 0:8192].rearrange("p (c t) -> p c t", c=16)

    NSLOT = 4
    ring = [sb(f"ring{i}", [128, 16, 128], BF16) for i in range(NSLOT)]
    ones_col = sb("ones_col", [128, 1], BF16)
    ones_row = sb("ones_row", [1, TC], BF16)
    ones128b = sb("ones128b", [1, 128], BF16)
    ones128f = sb("ones128f", [1, 128], F32)
    ident = sb("ident", [128, 128], F32)
    masks = sb("masks", [128, 4, TC], BF16)
    tmk = sb("tmk", [128, NH], F32)
    tmv = sb("tmv", [128, NH], F32)
    tmr = sb("tmr", [128, NH], F32)
    td = sb("td", [128, NH], F32)
    w_dec = sb("w_dec", [128, NH], F32)
    u_fst = sb("u_fst", [128, NH], F32)
    e_tmp = sb("e_tmp", [128, NH], F32)
    gwt8 = sb("gwt8", [128, HC, NE], BF16)
    xt = sb("xt", [128, 2, TC], F32)
    xbf = sb("xbf", [128, 2, TC], BF16)
    xsq = sb("xsq", [128, 2, TC], BF16)
    lnt = sb("lnt", [128, TC], F32)
    rowA = sb("rowA", [1, TC], F32)
    rowB = sb("rowB", [1, TC], F32)
    rowC = sb("rowC", [1, TC], F32)
    rowD = sb("rowD", [1, TC], F32)
    nmrow = sb("nmrow", [1, NTC, TC], BF16)
    rsrow = sb("rsrow", [1, NTC, TC], BF16)
    xo = sb("xo", [128, 2, TC], F32)
    nx = sb("nx", [128, 2, TC], F32)
    sg = sb("sg", [128, 2, TC], BF16)
    gu = sb("gu", [128, 2, TC], BF16)
    stg = sb("stg", [128, 2, TC], BF16)
    wbv = sb("wbv", [128, NTC, TC], BF16)
    wrow = sb("wrow", [1, T], BF16)
    lgts = sb("lgts", [128, NSUB, NE], F32)
    rt_a = sb("rt_a", [128, NSUB, NE], F32)
    rt_b = sb("rt_b", [128, NSUB, NE], F32)
    rt_c = sb("rt_c", [128, NSUB, NE], F32)
    wmat = sb("wmat", [128, NSUB, NE], F32)
    rt_g = sb("rt_g", [128, NSUB, 4], F32)
    rt_h = sb("rt_h", [128, NSUB, 4], F32)
    rt_r1 = sb("rt_r1", [128, NSUB], F32)
    rt_r2 = sb("rt_r2", [128, NSUB], F32)
    rt_s1 = sb("rt_s1", [128, NSUB, 1], F32)
    rt_s2 = sb("rt_s2", [128, NSUB, 1], F32)
    rt_s3 = sb("rt_s3", [128, NSUB, 1], F32)
    rt_u = sb("rt_u", [128, NSUB, 4], F32)
    mxc = sb("mxc", [128, 8], F32)
    srow2 = rowD
    diag_t = sb("diag_t", [128, TC], F32)
    rb_t = sb("rb_t", [128, TC], F32)
    maxs_t = sb("maxs_t", [128, NSUB], F32)
    mT_t = sb("mT_t", [16, 128], F32)

    pb = [es.enter_context(nc.psum_tensor(f"pb{i}", [128, TC], F32)) for i in range(4)]
    pw = es.enter_context(nc.psum_tensor("pw", [128, 4 * TC], F32))
    pwb = [pw[:, i * TC:(i + 1) * TC] for i in range(4)]

    blk = es.enter_context(nc.Block())

    class Sem:
        def __init__(self, name):
            self.h = es.enter_context(nc.semaphore(name))
            self.n = 0

        def inc(self, ins, k=1):
            ins.then_inc(self.h, k)
            self.n += k
            return self.n

    pe, dv, ac = Sem("pe"), Sem("dv"), Sem("ac")
    lastwait = {}

    def W(eng, ename, sem, val):
        if val is None or val <= 0:
            return
        key = (ename, sem.h)
        if lastwait.get(key, -1) >= val:
            return
        eng.wait_ge(sem.h, val)
        lastwait[key] = val

    def WM(eng, ename, mark):
        if mark is None:
            return
        if isinstance(mark, tuple):
            W(eng, ename, mark[0], mark[1])
            return
        for m in mark:
            WM(eng, ename, m)

    class Pool:

        def __init__(self, name, n):
            self.sems = [Sem(f"{name}{i}") for i in range(n)]
            self.i = 0

        def pre(self, eng, ename):
            sm = self.sems[self.i]
            if sm.n:
                W(eng, ename, sm, sm.n)

        def fin(self, ins):
            sm = self.sems[self.i]
            idx = self.i
            self.i = (self.i + 1) % len(self.sems)
            sm.inc(ins, 16)
            import traceback
            if DMALOG is not None:
                fr = traceback.extract_stack()[-2]
                DMALOG.append((self.sems is qx_ref[0] and "qx" or "qw", idx, sm.n,
                               f"{fr.name}:{fr.lineno}"))
            return (sm, sm.n)

        def snap(self):
            return [(sm, sm.n) for sm in self.sems if sm.n]

    qw = Pool("qw", 12)
    qx = Pool("qx", 24)
    qx_ref[0] = qx.sems

    bankfree = {i: None for i in range(8)}

    def bank_ap(i):
        return pb[i][:, :] if i < 4 else pwb[i - 4][:, :]

    def TB(f): blk.tensor(f)
    def VB(f): blk.vector(f)
    def SB_(f): blk.scalar(f)
    def GB(f): blk.gpsimd(f)
    def YB(f): blk.sync(f)

    M = {}

    ring_state = [None] * NSLOT
    ring_next = [0]

    def ring_load(segs):
        s = ring_next[0]
        ring_next[0] = (s + 1) % NSLOT
        hold = []

        def g(g):
            WM(g, "g", ring_state[s])
            off = 0
            for ap, rows in segs:
                qw.pre(g, "g")
                ins = g.dma_start(out=ring[s][:, off // 128:(off + rows) // 128, :],
                                  in_=ap.rearrange("(c p) n -> p c n", p=128))
                if DMALOG is not None:
                    import traceback
                    DMALOG.append(("ringslot", s, traceback.extract_stack()[-4].name,
                                   traceback.extract_stack()[-4].lineno))
                hold.append(qw.fin(ins))
                off += rows
        GB(g)
        return s, hold

    def ring_load_flat(src2d, ncols, extra_wait):
        s = ring_next[0]
        ring_next[0] = (s + 1) % NSLOT
        hold = []

        def g(g):
            WM(g, "g", ring_state[s])
            WM(g, "g", extra_wait)
            qw.pre(g, "g")
            ins = g.dma_start(
                out=ring[s][:, 0:ncols // 128, :].rearrange("p c n -> p (c n)"),
                in_=src2d[:, 0:ncols])
            hold.append(qw.fin(ins))
        GB(g)
        return s, hold

    def ring_used(s, sem, val):
        ring_state[s] = [(sem, val)]

    def ring_also_used(s, mark):
        ring_state[s].append(mark)

    wbc_next = [0]
    wbc_blk = {}

    def ring_load_cached(key, src2d, ncols):
        if key in wbc_blk:
            blk_i, mk = wbc_blk[key]
            slot, rdy = ring_load_flat(wbc_sp[blk_i], ncols, mk)
            return slot, rdy, None
        blk_i = wbc_next[0]
        wbc_next[0] += 1
        slot, rdy = ring_load_flat(src2d, ncols, None)

        def sst(s_, slot=slot, rdy=list(rdy), blk_i=blk_i, ncols=ncols):
            WM(s_, "s", rdy)
            qx.pre(s_, "s")
            ins = s_.dma_start(
                out=wbc_sp[blk_i][:, 0:ncols],
                in_=ring[slot][:, 0:ncols // 128, :].rearrange("p c n -> p (c n)"))
            wbc_blk[key] = (blk_i, qx.fin(ins))
        SB_(sst)
        return slot, rdy, wbc_blk[key][1]

    class EpiStats:

        def __init__(self, par):
            self.par = par
            self.chain = []
            self.sq_used = {}
            self.cp_used = {}
            self.st2 = {}
            self.cp_m = {}
            self.sq_m = {}

        def nx_guard(self, eng, ename, tc, hc):
            v = self.sq_used.get((tc, hc - 2), self.sq_used.get((tc - 1, hc - 2 + HC)))
            W(eng, ename, ac, v)

        def after_vev(self, tc, hc, nx_dv, src_ap=None):
            ap = nx[:, hc % 2, :] if src_ap is None else src_ap

            def sqr(s, tc=tc, hc=hc, nx_dv=nx_dv, ap=ap):
                W(s, "s", dv, nx_dv)
                W(s, "s", pe, self.st2.get(hc - 2))
                ins = s.activation(xsq[:, hc % 2, :], ap, AF.Square)
                self.sq_m[hc] = ac.inc(ins)
                self.sq_used[(tc, hc)] = self.sq_m[hc]
            SB_(sqr)

            def cpy(v, tc=tc, hc=hc, ap=ap):
                W(v, "v", pe, self.st2.get(hc - 2))
                ins = v.tensor_copy(xbf[:, hc % 2, :], ap)
                self.cp_m[hc] = dv.inc(ins)
                self.cp_used[(tc, hc)] = self.cp_m[hc]
            VB(cpy)

        def stat_mm(self, tc, hc):
            def tst2(t, tc=tc, hc=hc):
                if hc == 0:
                    for b2 in (0, 1):
                        if bankfree[b2] is not None:
                            W(t, "t", *bankfree[b2])
                W(t, "t", dv, self.cp_m[hc])
                t.matmul(pb[0][0:1, :], ones_col[:, 0:1], xbf[:, hc % 2, :],
                         start=(hc == 0), stop=(hc == HC - 1), skip_group_check=True)
                W(t, "t", ac, self.sq_m[hc])
                ins = t.matmul(pb[1][0:1, :], ones_col[:, 0:1], xsq[:, hc % 2, :],
                               start=(hc == 0), stop=(hc == HC - 1), skip_group_check=True)
                self.st2[hc] = pe.inc(ins)
            TB(tst2)

        def after_mm(self, tc, hc):
            if hc >= 2:
                self.stat_mm(tc, hc - 2)

        def end_tc(self, tc):
            self.stat_mm(tc, HC - 2)
            self.stat_mm(tc, HC - 1)
            stat_mark = self.st2[HC - 1]
            par = self.par

            def v2(v, tc=tc, stat_mark=stat_mark):
                W(v, "v", pe, stat_mark)
                WM(v, "v", M.get(("nm_used", tc)))
                with nc.allow_low_precision(reason="LN stats rows stored bf16"):
                    v.tensor_scalar_mul(nmrow[0:1, tc, :], pb[0][0:1, :], -1.0 / H)
                v.tensor_scalar_mul(rowB[0:1, :], pb[1][0:1, :], 1.0 / H)
                v.drain()
                v.tensor_mul(rowC[0:1, :], nmrow[0:1, tc, :],
                             nmrow[0:1, tc, :])
                v.drain()
                v.tensor_sub(rowD[0:1, :], rowB[0:1, :], rowC[0:1, :])
                v.drain()
                ins = v.tensor_scalar_add(rowB[0:1, :], rowD[0:1, :], EPS)
                M["ep_var"] = dv.inc(ins)
            VB(v2)
            bankfree[0] = (dv, M["ep_var"])
            bankfree[1] = (dv, M["ep_var"])

            def s2(s):
                W(s, "s", dv, M["ep_var"])
                ins = s.activation(rowC[0:1, :], rowB[0:1, :], AF.Sqrt)
                M["ep_sd"] = ac.inc(ins)
            SB_(s2)

            def v3(v, tc=tc):
                W(v, "v", ac, M["ep_sd"])
                with nc.allow_low_precision(reason="LN stats rows stored bf16"):
                    ins = v.reciprocal(rsrow[0:1, tc, :], rowC[0:1, :])
                M["ep_rstd"] = dv.inc(ins)
            VB(v3)
            self.chain.append((dv, M["ep_rstd"]))
            self.st2 = {}
            self.cp_m = {}
            self.sq_m = {}

    def setup():
        ldm = []

        def g(g):
            for dst, src in ((tmk, tmk_pt), (tmv, tmv_pt), (tmr, tmr_pt),
                             (td, td_pt), (u_fst, tf_pt), (ident, ident_in)):
                qw.pre(g, "g")
                ins = g.dma_start(out=dst[:], in_=src[:, :])
                ldm.append(qw.fin(ins))
            qw.pre(g, "g")
            ins = g.dma_start(out=masks[:], in_=masks_in.rearrange("p (m t) -> p m t", m=4))
            ldm.append(qw.fin(ins))
            qw.pre(g, "g")
            ins = g.dma_start(out=gwt8[:], in_=gw_T.rearrange("(c p) e -> p c e", p=128))
            ldm.append(qw.fin(ins))
        GB(g)
        ld = ldm

        def v(v):
            v.memset(ones_col[:], 1.0)
            v.memset(ones_row[0:1, :], 1.0)
            v.memset(ones128b[0:1, :], 1.0)
            ins = v.memset(ones128f[0:1, :], 1.0)
            dv.inc(ins)
        VB(v)

        def s(s):
            WM(s, "s", ld)
            s.activation(e_tmp[:], td[:], AF.Exp)
            s.drain()
            ins = s.activation(w_dec[:], e_tmp[:], AF.Exp, scale=-1.0)
            ac.inc(ins)
        SB_(s)

    def layernorm(src, prev_marks):
        h_marks = []
        pe_entry = pe.n
        st_m = {}
        prev_hdone = [None]
        for tc in range(NTC):
            t0 = tc * TC
            ld1 = {}
            c_m = {}
            q_m = {}

            def y1(y, tc=tc, t0=t0):
                if prev_marks is not None:
                    WM(y, "y", prev_marks[tc])
                for hc in range(HC):
                    W(y, "y", dv, c_m.get(hc - 2))
                    qx.pre(y, "y")
                    ins = y.dma_start(out=xt[:, hc % 2, :],
                                      in_=src[hc * 128:(hc + 1) * 128, t0:t0 + TC])
                    ld1[hc] = qx.fin(ins)
            for hc in range(HC):
                def yl(y, hc=hc, tc=tc, t0=t0):
                    if hc == 0 and prev_marks is not None:
                        WM(y, "y", prev_marks[tc])
                    if hc < 2 and prev_hdone[0] is not None:
                        W(y, "y", dv, prev_hdone[0][HC - 2 + hc])
                    W(y, "y", dv, c_m.get(hc - 2))
                    qx.pre(y, "y")
                    ins = y.dma_start(out=xt[:, hc % 2, :],
                                      in_=src[hc * 128:(hc + 1) * 128, t0:t0 + TC])
                    ld1[hc] = qx.fin(ins)
                YB(yl)

                def vc(v, hc=hc):
                    WM(v, "v", ld1[hc])
                    W(v, "v", pe, st_m.get(hc - 2))
                    v.tensor_copy(xbf[:, hc % 2, :], xt[:, hc % 2, :])
                    v.drain()
                    ins = v.tensor_mul(xsq[:, hc % 2, :], xt[:, hc % 2, :],
                                       xt[:, hc % 2, :])
                    c_m[hc] = dv.inc(ins)
                    q_m[hc] = c_m[hc]
                VB(vc)

                def tst(t, hc=hc):
                    if hc == 0:
                        for b in (0, 1):
                            if bankfree[b] is not None:
                                W(t, "t", *bankfree[b])
                    W(t, "t", dv, c_m[hc])
                    t.matmul(pb[0][0:1, :], ones_col[:, 0:1], xbf[:, hc % 2, :],
                             start=(hc == 0), stop=(hc == HC - 1), skip_group_check=True)
                    ins = t.matmul(pb[1][0:1, :], ones_col[:, 0:1], xsq[:, hc % 2, :],
                                   start=(hc == 0), stop=(hc == HC - 1), skip_group_check=True)
                    st_m[hc] = pe.inc(ins)
                    if hc == HC - 1:
                        M["stats"] = st_m[hc]
                TB(tst)

            def v2(v):
                W(v, "v", pe, M["stats"])
                v.tensor_scalar_mul(rowA[0:1, :], pb[0][0:1, :], -1.0 / H)
                v.tensor_scalar_mul(rowB[0:1, :], pb[1][0:1, :], 1.0 / H)
                v.drain()
                v.tensor_mul(rowC[0:1, :], rowA[0:1, :], rowA[0:1, :])
                v.drain()
                v.tensor_sub(rowD[0:1, :], rowB[0:1, :], rowC[0:1, :])
                v.drain()
                ins = v.tensor_scalar_add(rowB[0:1, :], rowD[0:1, :], EPS)
                M["var"] = dv.inc(ins)
            VB(v2)
            bankfree[0] = (dv, M["var"])
            bankfree[1] = (dv, M["var"])

            def s2(s):
                W(s, "s", dv, M["var"])
                ins = s.activation(rowC[0:1, :], rowB[0:1, :], AF.Sqrt)
                M["sd"] = ac.inc(ins)
            SB_(s2)

            def v3(v):
                W(v, "v", ac, M["sd"])
                ins = v.reciprocal(rowB[0:1, :], rowC[0:1, :])
                M["rstd"] = dv.inc(ins)
            VB(v3)

            def t2(t):
                for b in (2, 3):
                    if bankfree[b] is not None:
                        W(t, "t", *bankfree[b])
                W(t, "t", dv, M["rstd"])
                t.matmul(pb[2][:, :], ones128f[0:1, :], rowA[0:1, :],
                         start=True, stop=True, skip_group_check=True)
                ins = t.matmul(pb[3][:, :], ones128f[0:1, :], rowB[0:1, :],
                               start=True, stop=True, skip_group_check=True)
                M["bc"] = pe.inc(ins)
            TB(t2)

            ld2 = {}
            hdone = {}
            for hc in range(HC):
                def yl2(y, hc=hc, t0=t0):
                    if hc < 2:
                        W(y, "y", dv, c_m[HC - 2 + hc])
                    else:
                        W(y, "y", dv, hdone.get(hc - 2))
                    qx.pre(y, "y")
                    ins = y.dma_start(out=xt[:, hc % 2, :],
                                      in_=src[hc * 128:(hc + 1) * 128, t0:t0 + TC])
                    ld2[hc] = qx.fin(ins)
                YB(yl2)

                def vh(v, hc=hc, tc=tc, t0=t0):
                    if hc == 0:
                        W(v, "v", pe, max(M["bc"], pe_entry))
                    WM(v, "v", ld2[hc])
                    v.tensor_add(lnt[:, :], xt[:, hc % 2, :], pb[2][:, :])
                    v.drain()
                    ins = v.tensor_mul(h_full[:, hc, t0:t0 + TC], lnt[:, :], pb[3][:, :])
                    v.drain()
                    hdone[hc] = dv.inc(ins)
                VB(vh)
            h_marks.append(dv.n)
            prev_hdone[0] = dict(hdone)
            bankfree[2] = (dv, dv.n)
            bankfree[3] = (dv, dv.n)
        M["xt_last"] = {0: prev_hdone[0][HC - 2], 1: prev_hdone[0][HC - 1]}
        if debug:
            li = M_dbg["ln"]; M_dbg["ln"] += 1

            def ydb(y, li=li, hm=list(h_marks)):
                W(y, "y", dv, hm[-1])
                for hc in range(HC):
                    qx.pre(y, "y")
                    ins = y.dma_start(out=h_dump[li, hc * 128:(hc + 1) * 128, :],
                                      in_=h_full[:, hc, :])
                    qx.fin(ins)
            YB(ydb)
        return h_marks

    class LNP2:

        def __init__(self, src, stats, nx_marks, hfull_guard):
            self.src = src
            self.stats = stats
            self.nx_marks = nx_marks
            self.hfull_guard = hfull_guard
            self.h_marks = []
            self.prev_hdone = None

        def emit_bc(self, tc):
            def t2(t, tc=tc):
                for b in (2, 3):
                    if bankfree[b] is not None:
                        W(t, "t", *bankfree[b])
                WM(t, "t", self.stats.chain[tc])
                t.matmul(pb[2][:, :], ones128b[0:1, :], nmrow[0:1, tc, :],
                         start=True, stop=True, skip_group_check=True)
                ins = t.matmul(pb[3][:, :], ones128b[0:1, :], rsrow[0:1, tc, :],
                               start=True, stop=True, skip_group_check=True)
                M["bc"] = pe.inc(ins)
                self.bc = M["bc"]
                M[("nm_used", tc)] = (pe, pe.n)
            TB(t2)
            self.ld2 = {}
            self.hdone = {}

        def emit_hc(self, tc, hc):
            src = self.src
            hfull_guard = self.hfull_guard
            t0 = tc * TC
            ld2 = self.ld2
            hdone = self.hdone
            bc = self.bc

            def yl2(y, hc=hc, t0=t0, tc=tc):
                WM(y, "y", self.nx_marks.get((tc, hc)))
                if hc < 2:
                    if self.prev_hdone is not None:
                        W(y, "y", dv, self.prev_hdone[HC - 2 + hc])
                    else:
                        W(y, "y", dv, M["xt_last"][hc])
                else:
                    W(y, "y", dv, hdone.get(hc - 2))
                qx.pre(y, "y")
                ins = y.dma_start(out=xt[:, hc % 2, :],
                                  in_=src[hc * 128:(hc + 1) * 128, t0:t0 + TC])
                ld2[hc] = qx.fin(ins)
            YB(yl2)

            def vh(v, hc=hc, tc=tc, t0=t0, bc=bc):
                if hc == 0:
                    W(v, "v", pe, bc)
                    if hfull_guard is not None:
                        WM(v, "v", hfull_guard(tc))
                WM(v, "v", ld2[hc])
                v.tensor_add(lnt[:, :], xt[:, hc % 2, :], pb[2][:, :])
                v.drain()
                ins = v.tensor_mul(h_full[:, hc, t0:t0 + TC], lnt[:, :], pb[3][:, :])
                v.drain()
                hdone[hc] = dv.inc(ins)
            VB(vh)
            if hc == HC - 1:
                self.h_marks.append(dv.n)
                self.prev_hdone = dict(hdone)
                bankfree[2] = (dv, dv.n)
                bankfree[3] = (dv, dv.n)
                if tc == NTC - 1:
                    M["xt_last"] = {0: self.prev_hdone[HC - 2],
                                    1: self.prev_hdone[HC - 1]}

        def emit(self, tc):
            self.emit_bc(tc)
            for hc in range(HC):
                self.emit_hc(tc, hc)

        def emit_all(self):
            for tc in range(NTC):
                self.emit(tc)
            return self.h_marks

    def out_proj(w_T, rhs_sp, src_old, dst, rhs_store_mark, nch, bias_row,
                 stats=None, xo_guard=None, p2=None, cache_tag="w"):
        store_marks = []
        tc_pe_done = {}
        nx_store = {}
        xo_used = {}
        if p2 is not None:
            p2.nx_marks = nx_store
        XON = 8
        xo_r = r1[:, 16384:24576].bitcast(F32)
        zb2 = [r1[:, 0:8192].rearrange("p (c t) -> p c t", c=16),
               r1[:, 8192:16384].rearrange("p (c t) -> p c t", c=16)]
        for tc in range(NTC):
            t0 = tc * TC
            rhs_m = {}
            zb = zb2[tc % 2]

            def yld(y, tc=tc, t0=t0, zb=zb):
                WM(y, "y", rhs_store_mark)
                W(y, "y", pe, tc_pe_done.get(tc - 2))
                for kc in range(nch):
                    qx.pre(y, "y")
                    ins = y.dma_start(out=zb[:, kc, :],
                                      in_=rhs_sp[kc * 128:(kc + 1) * 128, t0:t0 + TC])
                    rhs_m[kc] = qx.fin(ins)
            YB(yld)

            for hc in range(HC):
                slot, rdy, pend_w = ring_load_cached((cache_tag, hc), w_T[hc],
                                                     nch * 128)
                bank = 4 + (hc % 2)
                xo_m = {}
                xs = hc % XON
                xo_ap = xo_r[:, xs * TC:(xs + 1) * TC]

                def yxo(y, hc=hc, t0=t0, tc=tc, xo_ap=xo_ap):
                    if tc == 0 and hc < XON:
                        WM(y, "y", xo_guard)
                    WM(y, "y", nx_store.get((tc, hc - XON),
                                            nx_store.get((tc - 1, hc - XON + HC))))
                    if stats is not None:
                        W(y, "y", ac, stats.sq_used.get((tc, hc - XON),
                                      stats.sq_used.get((tc - 1, hc - XON + HC))))
                        W(y, "y", dv, stats.cp_used.get((tc, hc - XON),
                                      stats.cp_used.get((tc - 1, hc - XON + HC))))
                    else:
                        W(y, "y", dv, xo_used.get((tc, hc - XON),
                                      xo_used.get((tc - 1, hc - XON + HC))))
                    qx.pre(y, "y")
                    ins = y.dma_start(out=xo_ap,
                                      in_=src_old[hc * 128:(hc + 1) * 128, t0:t0 + TC])
                    xo_m[0] = qx.fin(ins)
                YB(yxo)

                def tmm(t, slot=slot, rdy=rdy, bank=bank, hc=hc, tc=tc, zb=zb):
                    WM(t, "t", rdy)
                    if bankfree[bank] is not None:
                        W(t, "t", *bankfree[bank])
                    for kc in range(nch):
                        WM(t, "t", rhs_m[kc])
                        ins = t.matmul(pwb[bank - 4][:, :], ring[slot][:, kc, :],
                                       zb[:, kc, :], start=(kc == 0),
                                       stop=(kc == nch - 1 and bias_row is None),
                                       skip_group_check=True)
                    if bias_row is not None:
                        ins = t.matmul(pwb[bank - 4][:, :],
                                       bias_row[0:1, hc * 128:(hc + 1) * 128],
                                       ones_row[0:1, :], start=False, stop=True,
                                       skip_group_check=True)
                    M["op_mm"] = pe.inc(ins)
                    tc_pe_done[tc] = pe.n
                TB(tmm)
                ring_used(slot, pe, M["op_mm"])
                if pend_w is not None:
                    ring_also_used(slot, pend_w)
                mm_mark = M["op_mm"]
                if stats is not None:
                    stats.after_mm(tc, hc)

                def vev(v, bank=bank, hc=hc, tc=tc, mm_mark=mm_mark, xo_ap=xo_ap):
                    W(v, "v", pe, mm_mark)
                    WM(v, "v", xo_m[0])
                    ins = v.tensor_add(xo_ap, pwb[bank - 4][:, :], xo_ap)
                    xo_used[(tc, hc)] = dv.inc(ins)
                VB(vev)
                ev = dv.n
                bankfree[bank] = (dv, ev)
                if stats is not None:
                    stats.after_vev(tc, hc, ev, src_ap=xo_ap)

                def yst(y, hc=hc, t0=t0, tc=tc, ev=ev, xo_ap=xo_ap):
                    W(y, "y", dv, ev)
                    qx.pre(y, "y")
                    ins = y.dma_start(out=dst[hc * 128:(hc + 1) * 128, t0:t0 + TC],
                                      in_=xo_ap)
                    nx_store[(tc, hc)] = qx.fin(ins)
                YB(yst)
            if stats is not None:
                stats.end_tc(tc)
            store_marks.append(qx.snap())
        return store_marks, nx_store

    def attn_a(h_marks, src_old, dst, stats=None, p2=None):
        sg_guard = {}
        kv_marks = {oc: [] for oc in range(HC)}
        r_marks = {oc: [] for oc in range(HC)}
        kvm = {}
        scm = {}
        wkv_st = {}
        hfw = h_full[:].rearrange("p c t -> p (c t)")[:, 0:16384].bitcast(F32)
        sc_tbuf = hfw[:, 0:T + 1]
        sc_stmp = hfw[:, T + 1:2 * T + 1]

        def emit_scan(cc, mix1_guard):
            def ysl(y, cc=cc):
                if cc < 2:
                    W(y, "y", pe, mix1_guard)
                WM(y, "y", kv_marks[cc])
                WM(y, "y", r_marks[cc])
                W(y, "y", dv, scm.get(cc - 2))
                WM(y, "y", wkv_st.get(cc - 2))
                qx.pre(y, "y")
                ins = y.dma_start(out=r1_kvld[:, cc % 2, :],
                                  in_=kv_sp[cc * 128:(cc + 1) * 128, :])
                mka = qx.fin(ins)
                qx.pre(y, "y")
                ins = y.dma_start(out=r1_rld[:, cc % 2, :],
                                  in_=r_sp[cc * 128:(cc + 1) * 128, :])
                kvm[cc] = [mka, qx.fin(ins)]
            YB(ysl)

            def vsc(v, cc=cc):
                WM(v, "v", kvm[cc])
                p = cc % 2
                v.memset(sc_tbuf[:, 0:1], 0.0)
                v.drain()
                v.tensor_tensor_scan(sc_tbuf[:, 1:T + 1],
                                     w_dec[:, cc:cc + 1].broadcast_to([128, T]),
                                     r1_kvld[:, p, :], 0.0, ALU.mult, ALU.add)
                v.drain()
                v.scalar_tensor_tensor(sc_stmp[:, 0:T], r1_kvld[:, p, :],
                                       u_fst[:, cc:cc + 1], sc_tbuf[:, 0:T],
                                       ALU.mult, ALU.add)
                v.drain()
                ins = v.tensor_mul(r1_kvld[:, p, :], sc_stmp[:, 0:T], r1_rld[:, p, :])
                v.drain()
                scm[cc] = dv.inc(ins)
            VB(vsc)

            def ysc(y, cc=cc):
                W(y, "y", dv, scm[cc])
                qx.pre(y, "y")
                ins = y.dma_start(out=wkv_sp[cc * 128:(cc + 1) * 128, :],
                                  in_=r1_kvld[:, cc % 2, :])
                wkv_st[cc] = qx.fin(ins)
            YB(ysc)
        for tc in range(NTC):
            t0 = tc * TC

            def vd(v, tc=tc, t0=t0):
                W(v, "v", dv, h_marks[tc])
                if tc == 0:
                    v.tensor_sub(r1_d[:, :, 1:TC],
                                 h_full[:, :, t0 + 1:t0 + TC], h_full[:, :, t0:t0 + TC - 1])
                    ins = v.tensor_copy(r1_d[:, :, 0:1], h_full[:, :, 0:1])
                else:
                    ins = v.tensor_sub(r1_d[:, :, 0:TC],
                                       h_full[:, :, t0:t0 + TC], h_full[:, :, t0 - 1:t0 + TC - 1])
                v.drain()
                M["aa_d"] = dv.inc(ins)
            VB(vd)

            def build_mix(which, tmx, mb, tc=tc, t0=t0):
                mixbuf = r1_mix[which]

                def vm(v):
                    W(v, "v", dv, M["aa_d"])
                    W(v, "v", pe, M.get(("aa_mix_used", which)))
                    if tc == 0:
                        if bankfree[mb] is not None:
                            W(v, "v", *bankfree[mb])
                        for hc in range(HC):
                            v.tensor_mul(pb[mb][:, :], r1_d[:, hc, :],
                                         tmx[:, hc:hc + 1].broadcast_to([128, TC]))
                            v.drain()
                            v.tensor_add(mixbuf[:, hc, 1:TC], pb[mb][:, 1:TC],
                                         h_full[:, hc, t0:t0 + TC - 1])
                            ins = v.tensor_copy(mixbuf[:, hc, 0:1], pb[mb][:, 0:1])
                            v.drain()
                        M["aa_mix", which] = dv.inc(ins)
                        bankfree[mb] = (dv, M["aa_mix", which])
                    else:
                        for hc in range(HC):
                            ins = v.scalar_tensor_tensor(
                                mixbuf[:, hc, :], r1_d[:, hc, :], tmx[:, hc:hc + 1],
                                h_full[:, hc, t0 - 1:t0 + TC - 1], ALU.mult, ALU.add)
                        v.drain()
                        M["aa_mix", which] = dv.inc(ins)
                VB(vm)
                return M["aa_mix", which]

            def project(mixbuf, slot, rdy, bank, mix_mark, last_of_mix, which):
                def tmm(t):
                    WM(t, "t", rdy)
                    W(t, "t", dv, mix_mark)
                    if bankfree[bank] is not None:
                        W(t, "t", *bankfree[bank])
                    for hc in range(HC):
                        ins = t.matmul(pwb[bank - 4][:, :], ring[slot][:, hc, :],
                                       mixbuf[:, hc, :], start=(hc == 0),
                                       stop=(hc == HC - 1), skip_group_check=True)
                    M["aa_mm"] = pe.inc(ins)
                    if last_of_mix:
                        M[("aa_mix_used", which)] = pe.n
                TB(tmm)
                ring_used(slot, pe, M["aa_mm"])
                return M["aa_mm"]

            mk = build_mix(0, tmk, 2)
            mv = build_mix(1, tmv, 3)
            for oc in range(HC):
                slot_k, rdy_k, pend_k = ring_load_cached(("k", oc), wk_tl[oc], H)
                slot_v, rdy_v, pend_v = ring_load_cached(("v", oc), wv_tl[oc], H)
                bk_ = 4 + (oc % 2)
                bv_ = 6 + (oc % 2)
                mm_k = project(r1_mix[0], slot_k, rdy_k, bk_, mk, oc == HC - 1, 0)
                if pend_k is not None:
                    ring_also_used(slot_k, pend_k)

                def se(s, bk_=bk_, mm_k=mm_k, oc=oc):
                    W(s, "s", pe, mm_k)
                    W(s, "s", dv, sg_guard.get(oc % 2))
                    ins = s.activation(sg[:, oc % 2, :], pwb[bk_ - 4][:, :], AF.Copy)
                    M["aa_kev"] = ac.inc(ins)
                SB_(se)
                kev = M["aa_kev"]
                bankfree[bk_] = (ac, kev)

                mm_v = project(r1_mix[1], slot_v, rdy_v, bv_, mv, oc == HC - 1, 1)
                if pend_v is not None:
                    ring_also_used(slot_v, pend_v)

                def ve(v, bv_=bv_, mm_v=mm_v, kev=kev, oc=oc):
                    W(v, "v", pe, mm_v)
                    W(v, "v", ac, kev)
                    WM(v, "v", M.get(("aa_stg", oc % 2)))
                    ins = v.tensor_mul(stg[:, oc % 2, :], pwb[bv_ - 4][:, :],
                                       sg[:, oc % 2, :])
                    M["aa_kv"] = dv.inc(ins)
                    sg_guard[oc % 2] = M["aa_kv"]
                VB(ve)
                bankfree[bv_] = (dv, M["aa_kv"])

                def ysp(y, oc=oc, t0=t0, ev=M["aa_kv"], tc=tc):
                    W(y, "y", dv, ev)
                    qx.pre(y, "y")
                    ins = y.dma_start(out=kv_sp[oc * 128:(oc + 1) * 128, t0:t0 + TC],
                                      in_=stg[:, oc % 2, :])
                    M[("aa_stg", oc % 2)] = qx.fin(ins)
                    kv_marks[oc].append(M[("aa_stg", oc % 2)])
                YB(ysp)

            mr = build_mix(0, tmr, 2)
            for oc in range(HC):
                slot_r, rdy_r, pend_r = ring_load_cached(("r", oc), wr_tl[oc], H)
                br_ = 4 + (oc % 2)
                mm_r = project(r1_mix[0], slot_r, rdy_r, br_, mr, oc == HC - 1, 0)
                if pend_r is not None:
                    ring_also_used(slot_r, pend_r)

                def se(s, br_=br_, mm_r=mm_r, oc=oc):
                    W(s, "s", pe, mm_r)
                    WM(s, "s", M.get(("aa_stg", oc % 2)))
                    ins = s.activation(stg[:, oc % 2, :], pwb[br_ - 4][:, :], AF.Sigmoid)
                    M["aa_rev"] = ac.inc(ins)
                SB_(se)
                bankfree[br_] = (ac, M["aa_rev"])

                def ysp(y, oc=oc, t0=t0, ev=M["aa_rev"], tc=tc):
                    W(y, "y", ac, ev)
                    qx.pre(y, "y")
                    ins = y.dma_start(out=r_sp[oc * 128:(oc + 1) * 128, t0:t0 + TC],
                                      in_=stg[:, oc % 2, :])
                    M[("aa_stg", oc % 2)] = qx.fin(ins)
                    r_marks[oc].append(M[("aa_stg", oc % 2)])
                YB(ysp)
                if tc == NTC - 1:
                    emit_scan(oc, M[("aa_mix_used", 1)] if oc < 2 else None)

        snap = qx.snap()
        return out_proj(wo_tl, wkv_sp, src_old, dst, snap, HC, None,
                        stats=stats, xo_guard=snap, p2=p2, cache_tag="wo")

    def ffn(h_marks, src_old, dst, stats=None, p2=None):
        lg_cp = {}
        for ts in range(NSUB):
            b = ts % 2

            def tlg(t, ts=ts, b=b):
                W(t, "t", dv, h_marks[ts // 4])
                W(t, "t", dv, lg_cp.get(ts - 2))
                if ts < 2 and bankfree[b] is not None:
                    W(t, "t", *bankfree[b])
                for hc in range(HC):
                    ins = t.matmul(pb[b][0:128, 0:NE],
                                   h_full[:, hc, ts * 128:(ts + 1) * 128],
                                   gwt8[:, hc, :], start=(hc == 0), stop=(hc == HC - 1),
                                   skip_group_check=True)
                M["lg"] = pe.inc(ins)
            TB(tlg)

            def vlg(v, ts=ts, b=b):
                W(v, "v", pe, M["lg"])
                ins = v.tensor_copy(lgts[:, ts, :], pb[b][0:128, 0:NE])
                lg_cp[ts] = dv.inc(ins)
            VB(vlg)
        bankfree[0] = (dv, dv.n)
        bankfree[1] = (dv, dv.n)
        lg_mark = dv.n

        def srt(s):
            W(s, "s", dv, lg_mark)
            ins = s.activation(rt_a[:], lgts[:], AF.Exp)
            M["rt_exp"] = ac.inc(ins)
        SB_(srt)

        def vrt(v):
            W(v, "v", ac, M["rt_exp"])
            v.reduce_sum(rt_r1[:, :], rt_a[:], axis=AX.X)
            v.drain()
            v.reciprocal(rt_r2[:, :], rt_r1[:, :])
            v.drain()
            v.tensor_tensor(rt_b[:], rt_a[:],
                            rt_r2[:, :, None].broadcast_to([128, NSUB, NE]), ALU.mult)
            v.drain()
            v.tensor_tensor(rt_g[:], rt_b[:, :, 0:NE:2], rt_b[:, :, 1:NE:2], ALU.max)
            v.drain()
            g = rt_g
            u = rt_u
            v.tensor_tensor(u[:, :, 0:1], g[:, :, 0:1], g[:, :, 1:2], ALU.min)
            v.tensor_tensor(u[:, :, 1:2], g[:, :, 2:3], g[:, :, 3:4], ALU.min)
            v.tensor_tensor(u[:, :, 2:3], g[:, :, 0:1], g[:, :, 1:2], ALU.max)
            v.tensor_tensor(u[:, :, 3:4], g[:, :, 2:3], g[:, :, 3:4], ALU.max)
            v.drain()
            v.tensor_tensor(rt_s1[:, :, 0:1], u[:, :, 0:1], u[:, :, 1:2], ALU.max)
            v.tensor_tensor(rt_s2[:, :, 0:1], u[:, :, 2:3], u[:, :, 3:4], ALU.min)
            v.drain()
            v.tensor_tensor(rt_s3[:, :, 0:1], rt_s1[:, :, 0:1], rt_s2[:, :, 0:1], ALU.max)
            v.drain()
            v.tensor_tensor(rt_h[:], g[:],
                            rt_s3[:, :, 0:1].broadcast_to([128, NSUB, 4]), ALU.is_ge)
            v.drain()
            v.tensor_tensor(rt_c[:, :, 0:NE:2], rt_b[:, :, 0:NE:2], rt_h[:], ALU.mult)
            v.tensor_tensor(rt_c[:, :, 1:NE:2], rt_b[:, :, 1:NE:2], rt_h[:], ALU.mult)
            v.drain()
            v.reduce_max(rt_r1[:, :], rt_c[:], axis=AX.X)
            v.drain()
            v.tensor_tensor(rt_a[:], rt_c[:],
                            rt_r1[:, :, None].broadcast_to([128, NSUB, NE]), ALU.is_lt)
            v.drain()
            v.tensor_tensor(rt_b[:], rt_c[:], rt_a[:], ALU.mult)
            v.drain()
            v.reduce_max(rt_r2[:, :], rt_b[:], axis=AX.X)
            v.drain()
            v.tensor_tensor(rt_a[:], rt_c[:],
                            rt_r2[:, :, None].broadcast_to([128, NSUB, NE]), ALU.is_ge)
            v.drain()
            ins = v.tensor_tensor(wmat[:], rt_c[:], rt_a[:], ALU.mult)
            M["wm"] = dv.inc(ins)
        VB(vrt)

        def twm(t):
            W(t, "t", dv, M["wm"])
            if bankfree[0] is not None:
                W(t, "t", *bankfree[0])
            ins = t.transpose(pb[0][0:128, 0:128],
                              wmat[:].rearrange("p a b -> p (a b)"), ident[:, :])
            M["wmT"] = pe.inc(ins)
        TB(twm)

        def vwm(v):
            W(v, "v", pe, M["wmT"])
            ins = v.tensor_copy(lnt[:, 0:128], pb[0][0:128, 0:128])
            M["wmTc"] = dv.inc(ins)
        VB(vwm)
        bankfree[0] = (dv, M["wmTc"])

        def ywm(y):
            W(y, "y", dv, M["wmTc"])
            qx.pre(y, "y")
            ins = y.dma_start(out=wrow_sp[:, :], in_=lnt[0:128, 0:128])
            M["wrow_st"] = qx.fin(ins)
        YB(ywm)

        spill_m = {}
        gu_used = {}
        M["wb_used_pe"] = None

        def do_expert(n, gsrc, usrc, nfc, row0, scaled):
            if scaled:
                def gw_(g, n=n):
                    WM(g, "g", M["wrow_st"])
                    W(g, "g", pe, M["wb_used_pe"])
                    qw.pre(g, "g")
                    ins = g.dma_start(out=wrow[0:1, :].rearrange("o (s p) -> o s p", p=128),
                                      in_=wrow_sp[n:128:NE, :])
                    M["wrow_ld"] = qw.fin(ins)
                GB(gw_)

            for tc in range(NTC):
                if not scaled:
                    break

                def twb(t, tc=tc):
                    WM(t, "t", M["wrow_ld"])
                    W(t, "t", dv, M.get("wb_cp"))
                    if bankfree[1] is not None:
                        W(t, "t", *bankfree[1])
                    ins = t.matmul(pb[1][:, :], ones128b[0:1, :],
                                   wrow[0:1, tc * TC:(tc + 1) * TC],
                                   start=True, stop=True, skip_group_check=True)
                    M["wb_mm"] = pe.inc(ins)
                TB(twb)

                def vwb(v, tc=tc):
                    W(v, "v", pe, M["wb_mm"])
                    ins = v.tensor_copy(wbv[:, tc, :], pb[1][:, :])
                    M["wb_cp"] = dv.inc(ins)
                VB(vwb)
                bankfree[1] = (dv, M["wb_cp"])
            wb_mark = M.get("wb_cp", 0) if scaled else 0

            for fc in range(nfc):
                slot_g, rdy_g = ring_load_flat(gsrc[fc], H, None)
                slot_u, rdy_u = ring_load_flat(usrc[fc], H, None)
                for tc in range(NTC):
                    t0 = tc * TC
                    bg = 4 + (tc % 2)
                    bu = 6 + (tc % 2)

                    def tge(t, slot_g=slot_g, rdy_g=rdy_g, slot_u=slot_u, rdy_u=rdy_u,
                            bg=bg, bu=bu, tc=tc, t0=t0):
                        WM(t, "t", rdy_u)
                        W(t, "t", dv, h_marks[tc])
                        if bankfree[bg] is not None:
                            W(t, "t", *bankfree[bg])
                        for hc in range(HC):
                            ins = t.matmul(pwb[bg - 4][:, :], ring[slot_g][:, hc, :],
                                           h_full[:, hc, t0:t0 + TC], start=(hc == 0),
                                           stop=(hc == HC - 1), skip_group_check=True)
                        M["g_mm"] = pe.inc(ins)
                        if bankfree[bu] is not None:
                            W(t, "t", *bankfree[bu])
                        for hc in range(HC):
                            ins = t.matmul(pwb[bu - 4][:, :], ring[slot_u][:, hc, :],
                                           h_full[:, hc, t0:t0 + TC], start=(hc == 0),
                                           stop=(hc == HC - 1), skip_group_check=True)
                        M["u_mm"] = pe.inc(ins)
                    TB(tge)
                    g_mm, u_mm = M["g_mm"], M["u_mm"]

                    def sev(s, bg=bg, tc=tc, g_mm=g_mm):
                        W(s, "s", pe, g_mm)
                        W(s, "s", dv, gu_used.get(("sg", tc % 2)))
                        ins = s.activation(sg[:, tc % 2, :], pwb[bg - 4][:, :], AF.Silu)
                        M["silu"] = ac.inc(ins)
                    SB_(sev)
                    bankfree[bg] = (ac, M["silu"])
                    silu_m = M["silu"]

                    def vev(v, bu=bu, tc=tc, silu_m=silu_m, u_mm=u_mm,
                            wb_mark=wb_mark, scaled=scaled):
                        W(v, "v", ac, silu_m)
                        W(v, "v", pe, u_mm)
                        v.tensor_mul(gu[:, tc % 2, :], pwb[bu - 4][:, :], sg[:, tc % 2, :])
                        v.drain()
                        WM(v, "v", spill_m.get(tc % 2))
                        if scaled:
                            W(v, "v", dv, wb_mark)
                            ins = v.tensor_mul(stg[:, tc % 2, :], gu[:, tc % 2, :],
                                               wbv[:, tc, :])
                        else:
                            ins = v.tensor_copy(stg[:, tc % 2, :], gu[:, tc % 2, :])
                        M["act_ev"] = dv.inc(ins)
                        gu_used[("sg", tc % 2)] = M["act_ev"]
                    VB(vev)
                    bankfree[bu] = (dv, M["act_ev"])
                    act_m = M["act_ev"]

                    def ysp(y, row0=row0, fc=fc, t0=t0, tc=tc, act_m=act_m):
                        W(y, "y", dv, act_m)
                        r = row0 + fc * 128
                        qx.pre(y, "y")
                        ins = y.dma_start(out=act_sp[r:r + 128, t0:t0 + TC],
                                          in_=stg[:, tc % 2, :])
                        spill_m[tc % 2] = qx.fin(ins)
                    YB(ysp)
                ring_used(slot_g, pe, M["g_mm"])
                ring_used(slot_u, pe, M["u_mm"])
            if scaled:
                M["wb_used_pe"] = M["wb_mm"]

        for n in range(NE):
            do_expert(n, gate_tl[n * FC:(n + 1) * FC], up_tl[n * FC:(n + 1) * FC],
                      FC, n * FC * 128, True)
        do_expert(0, shg_tl, shu_tl, SFC, NE * FC * 128, False)
        spillA = qx.snap()
        phaseA_pe = pe.n

        def actview(j):
            if j < 64:
                return h_full[:].rearrange("p c t -> p (c t)")[:, j * TC:(j + 1) * TC]
            return r1[:, (j - 64) * TC:(j - 63) * TC]

        store_marks = []
        dn_pe = {}
        nx_store = {}
        xo_used = {}
        for tc in range(NTC):
            t0 = tc * TC
            act_m = {}

            def yact(y, tc=tc, t0=t0):
                if tc == 0:
                    WM(y, "y", spillA)
                    W(y, "y", pe, phaseA_pe)
                for j in range(NKC):
                    if tc > 0:
                        WM(y, "y", M[("dn15", tc - 1, j // 16)])
                    qx.pre(y, "y")
                    ins = y.dma_start(out=actview(j), in_=act_sp[j * 128:(j + 1) * 128,
                                                                 t0:t0 + TC])
                    act_m[j] = qx.fin(ins)
            YB(yact)

            for hc in range(HC):
                xo_m = {}

                def yxo(y, hc=hc, t0=t0, tc=tc):
                    W(y, "y", dv, xo_used.get((tc, hc - 2), xo_used.get((tc - 1, hc - 2 + HC))))
                    qx.pre(y, "y")
                    ins = y.dma_start(out=xo[:, hc % 2, :],
                                      in_=src_old[hc * 128:(hc + 1) * 128, t0:t0 + TC])
                    xo_m[0] = qx.fin(ins)
                YB(yxo)

                bank = 4 + (hc % 2)
                j = 0
                jg = 0
                first = True
                while j < NKC:
                    nch = min(16, NKC - j)
                    blk_i = hc * 7 + jg
                    ck = ("dwc", hc, jg)
                    pend_store = None
                    if ck in M:
                        slot, rdy = ring_load_flat(dwb_sp[blk_i], nch * 128, M[ck])
                    else:
                        slot, rdy = ring_load_flat(dwn_tl[blk_i], nch * 128, None)

                        def sst(s_, slot=slot, rdy=list(rdy), blk_i=blk_i,
                                nch=nch, ck=ck):
                            WM(s_, "s", rdy)
                            qx.pre(s_, "s")
                            ins = s_.dma_start(
                                out=dwb_sp[blk_i][:, 0:nch * 128],
                                in_=ring[slot][:, 0:nch, :].rearrange("p c n -> p (c n)"))
                            M[ck] = qx.fin(ins)
                        SB_(sst)
                        pend_store = M[ck]

                    def tdn(t, slot=slot, rdy=rdy, j0=j, nch=nch, bank=bank, tc=tc,
                            first=first, hc=hc, jg=jg):
                        WM(t, "t", rdy)
                        if first and bankfree[bank] is not None:
                            W(t, "t", *bankfree[bank])
                        for c in range(nch):
                            jj = j0 + c
                            WM(t, "t", act_m[jj])
                            ins = t.matmul(pwb[bank - 4][:, :], ring[slot][:, c, :],
                                           actview(jj), start=(jj == 0), stop=(jj == NKC - 1),
                                           skip_group_check=True)
                        M["dn_mm"] = pe.inc(ins)
                        dn_pe[tc] = pe.n
                        if hc == HC - 1:
                            M[("dn15", tc, jg)] = (pe, pe.n)
                    TB(tdn)
                    ring_used(slot, pe, M["dn_mm"])
                    if pend_store is not None:
                        ring_also_used(slot, pend_store)
                    first = False
                    j += nch
                    jg += 1
                mm_mark = M["dn_mm"]
                if stats is not None:
                    stats.after_mm(tc, hc)

                def vev(v, bank=bank, hc=hc, tc=tc, mm_mark=mm_mark):
                    W(v, "v", pe, mm_mark)
                    WM(v, "v", xo_m[0])
                    WM(v, "v", nx_store.get((tc, hc - 2), nx_store.get((tc - 1, hc - 2 + HC))))
                    if stats is not None:
                        stats.nx_guard(v, "v", tc, hc)
                    ins = v.tensor_add(nx[:, hc % 2, :], pwb[bank - 4][:, :], xo[:, hc % 2, :])
                    xo_used[(tc, hc)] = dv.inc(ins)
                VB(vev)
                ev = dv.n
                bankfree[bank] = (dv, ev)
                if stats is not None:
                    stats.after_vev(tc, hc, ev)

                def yst(y, hc=hc, t0=t0, tc=tc, ev=ev):
                    W(y, "y", dv, ev)
                    qx.pre(y, "y")
                    ins = y.dma_start(out=dst[hc * 128:(hc + 1) * 128, t0:t0 + TC],
                                      in_=nx[:, hc % 2, :])
                    nx_store[(tc, hc)] = qx.fin(ins)
                YB(yst)
            if stats is not None:
                stats.end_tc(tc)
            if p2 is not None:
                p2.emit(tc)
            store_marks.append(qx.snap())
        return store_marks, nx_store

    def attn_b(h_marks, src_old, dst, stats=None, p2=None):
        h_all = h_marks[NTC - 1]
        pe_entry = pe.n

        def vguard(v):
            W(v, "v", pe, pe_entry)
        VB(vguard)

        def sguard(s):
            W(s, "s", pe, pe_entry)
        SB_(sguard)

        bias_marks = []

        def gbi(g):
            W(g, "g", pe, pe_entry)
            for i, src_d in enumerate((bq_r, bk_r, bv_r, bo_r)):
                qw.pre(g, "g")
                ins = g.dma_start(out=r1_bias[0:1, i, :], in_=src_d[0:1, :])
                bias_marks.append(qw.fin(ins))
        GB(gbi)
        bias_m = bias_marks
        bqv = r1_bias[0:1, 0, :]
        bkv = r1_bias[0:1, 1, :]
        bvv = r1_bias[0:1, 2, :]
        bov = r1_bias[0:1, 3, :]

        qv = r1_q
        kv_ = r1_k
        nm_dram = {}
        exp_done = {}
        av_prev = {}
        stq_prev = {}
        vg_av = {}

        for head in range(NH):
            grp = head // 4
            if head % 4 == 0:
                vslots = []
                for q4 in range(4):
                    s_, r_ = ring_load_flat(wvb_tl[grp * 4 + q4], H, None)
                    vslots.append((s_, r_))
                rdy_all = [r_ for _, r_ in vslots]
                for ts in range(NSUB):
                    bank = 4 + (ts % 2)

                    def tv(t, ts=ts, bank=bank, vslots=vslots, rdy_all=rdy_all, grp=grp):
                        if bankfree[bank] is not None:
                            W(t, "t", *bankfree[bank])
                        W(t, "t", dv, h_marks[ts // 4])
                        WM(t, "t", [rdy_all, bias_m])
                        for q4 in range(4):
                            sl = vslots[q4][0]
                            for hc in range(HC):
                                t.matmul(pwb[bank - 4][:, q4 * 128:(q4 + 1) * 128],
                                         h_full[:, hc, ts * 128:(ts + 1) * 128],
                                         ring[sl][:, hc, :], start=(hc == 0), stop=False,
                                         skip_group_check=True)
                            ins = t.matmul(pwb[bank - 4][:, q4 * 128:(q4 + 1) * 128],
                                           ones128b[0:1, :],
                                           bvv[0:1, grp * 512 + q4 * 128:
                                               grp * 512 + (q4 + 1) * 128],
                                           start=False, stop=True, skip_group_check=True)
                        M["v_mm"] = pe.inc(ins)
                    TB(tv)
                    mmv = M["v_mm"]

                    def sv(s, ts=ts, bank=bank, mmv=mmv, grp=grp):
                        W(s, "s", pe, mmv)
                        W(s, "s", pe, vg_av.get(grp - 1))
                        ins = s.activation(r1_vg[:, ts, :], pwb[bank - 4][:, :], AF.Copy)
                        M["v_ev"] = ac.inc(ins)
                    SB_(sv)
                    bankfree[bank] = (ac, M["v_ev"])
                for sl, _ in vslots:
                    ring_used(sl, pe, M["v_mm"])
                vg_mark = M["v_ev"]

            slot_q, rdy_q = ring_load_flat(wqh_tl[head], H, None)
            slot_k, rdy_k = ring_load_flat(wkh_tl[head], H, None)
            for tc in range(NTC):
                t0 = tc * TC
                b_q = 0 + (tc % 2)
                b_k = 2 + (tc % 2)

                def tqk(t, tc=tc, t0=t0, slot_q=slot_q, slot_k=slot_k, rdy_q=rdy_q,
                        rdy_k=rdy_k, head=head, b_q=b_q, b_k=b_k):
                    WM(t, "t", [rdy_q, rdy_k, bias_m])
                    W(t, "t", dv, h_marks[tc])
                    if bankfree[b_q] is not None:
                        W(t, "t", *bankfree[b_q])
                    for hc in range(HC):
                        t.matmul(pb[b_q][:, :], ring[slot_q][:, hc, :],
                                 h_full[:, hc, t0:t0 + TC], start=(hc == 0), stop=False,
                                 skip_group_check=True)
                    ins = t.matmul(pb[b_q][:, :], bqv[0:1, head * 128:(head + 1) * 128],
                                   ones_row[0:1, :], start=False, stop=True,
                                   skip_group_check=True)
                    M["q_mm"] = pe.inc(ins)
                    if bankfree[b_k] is not None:
                        W(t, "t", *bankfree[b_k])
                    for hc in range(HC):
                        t.matmul(pb[b_k][:, :], ring[slot_k][:, hc, :],
                                 h_full[:, hc, t0:t0 + TC], start=(hc == 0), stop=False,
                                 skip_group_check=True)
                    ins = t.matmul(pb[b_k][:, :], bkv[0:1, head * 128:(head + 1) * 128],
                                   ones_row[0:1, :], start=False, stop=True,
                                   skip_group_check=True)
                    M["k_mm"] = pe.inc(ins)
                TB(tqk)
                if head == NH - 1:
                    M[("hfree_b", tc)] = (pe, pe.n)

                def sqk(s, tc=tc, t0=t0, b_q=b_q, b_k=b_k, head=head):
                    W(s, "s", pe, M["k_mm"])
                    W(s, "s", pe, stq_prev.get(head - 1))
                    s.activation(qv[:, t0:t0 + TC], pb[b_q][:, :], AF.Copy, scale=ISQD)
                    ins = s.activation(kv_[:, t0:t0 + TC], pb[b_k][:, :], AF.Copy)
                    M["qk_ev"] = ac.inc(ins)
                SB_(sqk)
                bankfree[b_q] = (ac, M["qk_ev"])
                bankfree[b_k] = (ac, M["qk_ev"])
            ring_used(slot_q, pe, M["q_mm"])
            ring_used(slot_k, pe, M["k_mm"])
            qk_mark = M["qk_ev"]

            red = {}
            for qs in range(NSUB):
                live = (qs + 1) * 128
                use_pw = (qs % 2 == 0)
                banks = [4, 5, 6, 7] if use_pw else [0, 1, 2, 3]
                nq = ceil_div(live, TC)

                def tsm(t, qs=qs, live=live, use_pw=use_pw, banks=banks, nq=nq,
                        qk_mark=qk_mark):
                    W(t, "t", ac, qk_mark)
                    for b in banks[:nq]:
                        if bankfree[b] is not None:
                            W(t, "t", *bankfree[b])
                    for kq in range(nq):
                        n = min(TC, live - kq * TC)
                        dst_ = pw[:, kq * TC:kq * TC + n] if use_pw else pb[kq][:, 0:n]
                        ins = t.matmul(dst_, qv[:, qs * 128:(qs + 1) * 128],
                                       kv_[:, kq * TC:kq * TC + n], start=True, stop=True,
                                       skip_group_check=True)
                    M["sm_mm"] = pe.inc(ins)
                    stq_prev[head] = pe.n
                TB(tsm)
                mm_m = M["sm_mm"]

                def vsm(v, qs=qs, live=live, use_pw=use_pw, nq=nq, mm_m=mm_m):
                    W(v, "v", pe, mm_m)
                    if qs == 0:
                        W(v, "v", pe, M.get("mxT"))
                    if use_pw:
                        ins = v.reduce_max(maxs_t[:, qs:qs + 1], pw[:, 0:live], axis=AX.X)
                        v.drain()
                    else:
                        for kq in range(nq):
                            n = min(TC, live - kq * TC)
                            v.reduce_max(mxc[:, kq:kq + 1], pb[kq][:, 0:n], axis=AX.X)
                        v.drain()
                        if nq == 1:
                            ins = v.tensor_copy(maxs_t[:, qs:qs + 1], mxc[:, 0:1])
                        elif nq == 2:
                            ins = v.tensor_max(maxs_t[:, qs:qs + 1], mxc[:, 0:1], mxc[:, 1:2])
                        elif nq == 3:
                            v.tensor_max(mxc[:, 4:5], mxc[:, 0:1], mxc[:, 1:2])
                            v.drain()
                            ins = v.tensor_max(maxs_t[:, qs:qs + 1], mxc[:, 4:5], mxc[:, 2:3])
                        else:
                            v.tensor_max(mxc[:, 4:5], mxc[:, 0:1], mxc[:, 1:2])
                            v.tensor_max(mxc[:, 5:6], mxc[:, 2:3], mxc[:, 3:4])
                            v.drain()
                            ins = v.tensor_max(maxs_t[:, qs:qs + 1], mxc[:, 4:5], mxc[:, 5:6])
                    red[qs] = dv.inc(ins)
                VB(vsm)
                for b in banks[:nq]:
                    bankfree[b] = (dv, red[qs])
            max_mark = dv.n

            def tmx(t, max_mark=max_mark):
                W(t, "t", dv, max_mark)
                if bankfree[0] is not None:
                    W(t, "t", *bankfree[0])
                ins = t.transpose(pb[0][0:16, 0:128], maxs_t[:, 0:16], ident[:, :])
                M["mxT"] = pe.inc(ins)
            TB(tmx)

            def vmx(v, head=head):
                W(v, "v", pe, M["mxT"])
                WM(v, "v", nm_dram.get(head - 1))
                ins = v.tensor_scalar_mul(mT_t[0:16, 0:128], pb[0][0:16, 0:128], -1.0)
                M["ngT"] = dv.inc(ins)
            VB(vmx)
            bankfree[0] = (dv, M["ngT"])

            def ymx(y, head=head):
                W(y, "y", dv, M["ngT"])
                qx.pre(y, "y")
                ins = y.dma_start(out=max_sp[head * NSUB:(head + 1) * NSUB, :],
                                  in_=mT_t[0:16, 0:128])
                nm_dram[head] = qx.fin(ins)
            YB(ymx)

            def gnm(g, head=head):
                WM(g, "g", nm_dram[head])
                W(g, "g", ac, exp_done.get(head - 2))
                qw.pre(g, "g")
                ins = g.dma_start(out=r1_nm[0:1, head % 2, :],
                                  in_=max_sp[head * NSUB:(head + 1) * NSUB, :]
                                  .rearrange("s p -> (s p)").unsqueeze(0))
                M["nm_ld"] = qw.fin(ins)
            GB(gnm)
            nm_m = M["nm_ld"]

            for qc in range(NTC):
                q0 = qc * TC
                kmax = 4 * qc + 4
                exp_m = {}
                for g0 in range(0, kmax, 2):
                    pair = (g0 // 2) % 2
                    for kci in range(2):
                        kc = g0 + kci
                        bank = 4 + 2 * pair + kci

                        def tst(t, kc=kc, q0=q0, bank=bank, nm_m=nm_m, head=head):
                            WM(t, "t", nm_m)
                            if bankfree[bank] is not None:
                                W(t, "t", *bankfree[bank])
                            t.matmul(pwb[bank - 4][:, :], kv_[:, kc * 128:(kc + 1) * 128],
                                     qv[:, q0:q0 + TC], start=True, stop=False,
                                     skip_group_check=True)
                            ins = t.matmul(pwb[bank - 4][:, :], ones128b[0:1, :],
                                           r1_nm[0:1, head % 2, q0:q0 + TC],
                                           start=False, stop=True, skip_group_check=True)
                            M["st_mm"] = pe.inc(ins)
                            stq_prev[head] = pe.n
                        TB(tst)
                    st_m = M["st_mm"]

                    def sexg(s, g0=g0, pair=pair, st_m=st_m, head=head):
                        W(s, "s", pe, st_m)
                        W(s, "s", pe, av_prev.get(head, av_prev.get(head - 1)))
                        if M.get("dmask") is not None:
                            W(s, "s", dv, M["dmask"])
                        ins = s.activation(
                            r1_pT[:, g0:g0 + 2, :],
                            pw[:, pair * 1024:(pair + 1) * 1024]
                            .rearrange("p (c t) -> p c t", c=2),
                            AF.Exp)
                        M["expg"] = ac.inc(ins)
                    SB_(sexg)
                    for kci in range(2):
                        bankfree[4 + 2 * pair + kci] = (ac, M["expg"])
                    if g0 < 4 * qc:
                        for kci in range(2):
                            exp_m[g0 + kci] = (ac, M["expg"])
                    else:
                        pos0 = g0 - 4 * qc

                        def vms(v, g0=g0, pos0=pos0, head=head):
                            W(v, "v", ac, M["expg"])
                            ins = v.tensor_mul(r1_pT[:, g0:g0 + 2, :],
                                               r1_pT[:, g0:g0 + 2, :],
                                               masks[:, pos0:pos0 + 2, :])
                            M["dmask"] = dv.inc(ins)
                        VB(vms)
                        for kci in range(2):
                            exp_m[g0 + kci] = (dv, M["dmask"])
                exp_done[head] = ac.n

                def tsum(t, qc=qc, kmax=kmax, head=head, vg_mark=vg_mark):
                    if bankfree[1] is not None:
                        W(t, "t", *bankfree[1])
                    for kc in range(kmax):
                        W(t, "t", exp_m[kc][0], exp_m[kc][1])
                        ins = t.matmul(pb[1][0:1, :], ones_col[:, 0:1], r1_pT[:, kc, :],
                                       start=(kc == 0), stop=(kc == kmax - 1),
                                       skip_group_check=True)
                    M["sum_mm"] = pe.inc(ins)
                    bz = 2 + (qc % 2)
                    if bankfree[bz] is not None:
                        W(t, "t", *bankfree[bz])
                    W(t, "t", ac, vg_mark)
                    for kc in range(kmax):
                        ins = t.matmul(pb[bz][:, :],
                                       r1_vg[:, kc, (head % 4) * 128:(head % 4 + 1) * 128],
                                       r1_pT[:, kc, :], start=(kc == 0),
                                       stop=(kc == kmax - 1), skip_group_check=True)
                    M["av_mm"] = pe.inc(ins)
                    av_prev[head] = pe.n
                    vg_av[head // 4] = pe.n
                TB(tsum)
                sum_m, av_m = M["sum_mm"], M["av_mm"]

                def vsum(v, sum_m=sum_m):
                    W(v, "v", pe, sum_m)
                    ins = v.reciprocal(srow2[0:1, :], pb[1][0:1, :])
                    M["rc"] = dv.inc(ins)
                VB(vsum)
                bankfree[1] = (dv, M["rc"])

                def trb(t):
                    if bankfree[0] is not None:
                        W(t, "t", *bankfree[0])
                    W(t, "t", dv, M["rc"])
                    ins = t.matmul(pb[0][:, :], ones128f[0:1, :], srow2[0:1, :],
                                   start=True, stop=True, skip_group_check=True)
                    M["rb_mm"] = pe.inc(ins)
                TB(trb)
                rb_m = M["rb_mm"]

                def vz(v, qc=qc, rb_m=rb_m, av_m=av_m):
                    W(v, "v", pe, max(rb_m, av_m))
                    WM(v, "v", M.get(("z_st", qc % 2)))
                    v.tensor_copy(rb_t[:, :], pb[0][:, :])
                    v.drain()
                    ins = v.tensor_mul(stg[:, qc % 2, :], pb[2 + (qc % 2)][:, :], rb_t[:, :])
                    M["z_ev"] = dv.inc(ins)
                VB(vz)
                bankfree[0] = (dv, M["z_ev"])
                bankfree[2 + (qc % 2)] = (dv, M["z_ev"])
                z_m = M["z_ev"]

                def yz(y, head=head, qc=qc, q0=q0, z_m=z_m):
                    W(y, "y", dv, z_m)
                    qx.pre(y, "y")
                    ins = y.dma_start(out=z_sp[head * 128:(head + 1) * 128, q0:q0 + TC],
                                      in_=stg[:, qc % 2, :])
                    M[("z_st", qc % 2)] = qx.fin(ins)
                YB(yz)

        return out_proj(wob_tl, z_sp, src_old, dst, qx.snap(), NH, bov,
                        stats=stats, xo_guard=(pe, pe.n), p2=p2, cache_tag="wob")

    setup()
    hm = layernorm(xT_in, None)
    if n_blocks > 1:
        st_a = EpiStats(0)
        p2a = LNP2(x_res, st_a, {}, None)
    else:
        st_a = p2a = None
    sm, nxm = attn_a(hm, xT_in, x_res if n_blocks > 1 else xT_out,
                     stats=st_a, p2=p2a)
    if n_blocks >= 2:
        hm = p2a.emit_all()
        st_f = EpiStats(1) if n_blocks > 2 else None
        sm, nxm = ffn(hm, x_res, x_res if n_blocks > 2 else xT_out, stats=st_f)
        ffn_guard = (pe, pe.n)
    if n_blocks >= 3:
        p2f = LNP2(x_res, st_f, nxm, lambda tc: ffn_guard)
        hm = p2f.emit_all()
        if n_blocks > 3:
            st_b = EpiStats(0)
            p2b = LNP2(x_res, st_b, {}, lambda tc: M[("hfree_b", tc)])
        else:
            st_b = p2b = None
        sm, nxm = attn_b(hm, x_res, x_res if n_blocks > 3 else xT_out,
                         stats=st_b, p2=p2b)
    if n_blocks >= 4:
        hm = p2b.emit_all()
        sm, nxm = ffn(hm, x_res, xT_out)

    def yfin(y):
        WM(y, "y", qx.snap())
        WM(y, "y", qw.snap())
    YB(yfin)

    es.close()
    return nc


def _tile_cols(Wt, nfc):
    K = Wt.shape[0]
    c = K // 128
    R = Wt.reshape(c, 128, nfc, 128)
    return np.ascontiguousarray(R.transpose(2, 1, 0, 3).reshape(nfc, 128, K))


def _prep_inputs(inputs):
    f = np.ascontiguousarray
    ws = {}
    ws["wr_tl"] = _tile_cols(inputs["L0_attn_a_Wr"].T, HC)
    ws["wk_tl"] = _tile_cols(inputs["L0_attn_a_Wk"].T, HC)
    ws["wv_tl"] = _tile_cols(inputs["L0_attn_a_Wv"].T, HC)
    ws["wo_tl"] = _tile_cols(inputs["L0_attn_a_Wo"].T, HC)
    ws["td_pt"] = f(inputs["L0_attn_a_time_decay"].T)
    ws["tf_pt"] = f(inputs["L0_attn_a_time_first"].T)
    ws["tmk_pt"] = f(inputs["L0_attn_a_time_mix_k"].reshape(16, 128).T)
    ws["tmv_pt"] = f(inputs["L0_attn_a_time_mix_v"].reshape(16, 128).T)
    ws["tmr_pt"] = f(inputs["L0_attn_a_time_mix_r"].reshape(16, 128).T)
    wq = inputs["L0_attn_b_W_Q"]
    ws["wqh_tl"] = f(wq.reshape(NH, HC, 128, DH).transpose(0, 2, 1, 3)
                     .reshape(NH, 128, H))
    wk = inputs["L0_attn_b_W_K"]
    ws["wkh_tl"] = f(wk.reshape(NH, HC, 128, DH).transpose(0, 2, 1, 3)
                     .reshape(NH, 128, H))
    ws["wvb_tl"] = _tile_cols(inputs["L0_attn_b_W_V"].transpose(1, 0, 2)
                              .reshape(H, H), HC)
    ws["wob_tl"] = _tile_cols(inputs["L0_attn_b_W_O"].reshape(H, H), HC)
    ws["bq_r"] = f(inputs["L0_attn_b_b_Q"].reshape(1, H))
    ws["bk_r"] = f(inputs["L0_attn_b_b_K"].reshape(1, H))
    ws["bv_r"] = f(inputs["L0_attn_b_b_V"].reshape(1, H))
    ws["bo_r"] = f(inputs["L0_attn_b_b_O"].reshape(1, H))
    ws["gw_T"] = f(inputs["L0_ffn_gate_weight"].T)
    gate_wT = inputs["L0_ffn_gate_w"].transpose(0, 2, 1)
    up_wT = inputs["L0_ffn_up_w"].transpose(0, 2, 1)
    ws["gate_tl"] = np.concatenate([_tile_cols(gate_wT[n], FC) for n in range(NE)])
    ws["up_tl"] = np.concatenate([_tile_cols(up_wT[n], FC) for n in range(NE)])
    ws["shg_tl"] = _tile_cols(inputs["L0_ffn_sh_gate_w"].T, SFC)
    ws["shu_tl"] = _tile_cols(inputs["L0_ffn_sh_up_w"].T, SFC)
    D = np.concatenate([inputs["L0_ffn_down_w"].transpose(0, 2, 1).reshape(NE * FI, H),
                        inputs["L0_ffn_sh_down_w"].T], axis=0)
    R = D.reshape(NKC, 128, HC, 128).transpose(2, 0, 1, 3)
    Rp = np.zeros((HC, 112, 128, 128), dtype=np.float32)
    Rp[:, :NKC] = R
    ws["dwn_tl"] = f(Rp.reshape(HC, 7, 16, 128, 128).transpose(0, 1, 3, 2, 4)
                     .reshape(HC * 7, 128, H))
    ws["ident_in"] = np.eye(128, dtype=np.float32)
    m = np.zeros((4, 128, TC), dtype=np.float32)
    for p in range(4):
        kk = p * 128 + np.arange(128)[:, None]
        qq = np.arange(TC)[None, :]
        m[p] = (kk <= qq).astype(np.float32)
    ws["masks_in"] = f(m.transpose(1, 0, 2).reshape(128, 4 * TC))
    return ws


def kernel(**inputs):
    n_blocks = int(inputs.pop("_n_blocks", 4))
    ws = _prep_inputs(inputs)
    x = inputs["L0_attn_a_hidden_states"]
    nc = build_program(n_blocks)
    in_maps = []
    for b in range(8):
        m = dict(ws)
        m["xT_in"] = np.ascontiguousarray(x[b].T)
        in_maps.append(m)
    res = run_bass_kernel_spmd(nc, in_maps, core_ids=list(range(8)))
    out = np.stack([np.ascontiguousarray(r["xT_out"].T) for r in res.results])
    return out


# version 17
# speedup vs baseline: 1.0290x; 1.0290x over previous
import numpy as np
from contextlib import ExitStack

import concourse.bass as bass
DMALOG = None
qx_ref = [None]
import concourse.mybir as mybir
from concourse.bass_utils import run_bass_kernel_spmd

F32 = mybir.dt.float32
BF16 = mybir.dt.bfloat16
AF = mybir.ActivationFunctionType
ALU = mybir.AluOpType
AX = mybir.AxisListType

H = 2048
T = 2048
NH = 16
DH = 128
NE = 8
FI = 1408
FS = 2816
HC = 16
TC = 512
NTC = 4
NSUB = 16
FC = 11
SFC = 22
NKC = NE * FC + SFC
EPS = 1e-5
ISQD = float(1.0 / np.sqrt(128.0))


def ceil_div(a, b):
    return (a + b - 1) // b


def build_program(n_blocks=4, debug=False):
    nc = bass.Bass("TRN2", target_bir_lowering=False)
    es = ExitStack()

    def din(name, shape, dtype=F32):
        return nc.dram_tensor(name, shape, dtype, kind="ExternalInput").ap()

    xT_in = din("xT_in", [H, T])
    wr_tl = din("wr_tl", [HC, 128, H]); wk_tl = din("wk_tl", [HC, 128, H])
    wv_tl = din("wv_tl", [HC, 128, H]); wo_tl = din("wo_tl", [HC, 128, H])
    td_pt = din("td_pt", [128, NH]); tf_pt = din("tf_pt", [128, NH])
    tmk_pt = din("tmk_pt", [128, NH]); tmv_pt = din("tmv_pt", [128, NH])
    tmr_pt = din("tmr_pt", [128, NH])
    wqh_tl = din("wqh_tl", [NH, 128, H]); wkh_tl = din("wkh_tl", [NH, 128, H])
    wvb_tl = din("wvb_tl", [HC, 128, H]); wob_tl = din("wob_tl", [HC, 128, H])
    bq_r = din("bq_r", [1, H]); bk_r = din("bk_r", [1, H])
    bv_r = din("bv_r", [1, H]); bo_r = din("bo_r", [1, H])
    gw_T = din("gw_T", [H, NE])
    gate_tl = din("gate_tl", [NE * FC, 128, H]); up_tl = din("up_tl", [NE * FC, 128, H])
    shg_tl = din("shg_tl", [SFC, 128, H]); shu_tl = din("shu_tl", [SFC, 128, H])
    dwn_tl = din("dwn_tl", [HC * 7, 128, H])
    ident_in = din("ident_in", [128, 128])
    masks_in = din("masks_in", [128, 4 * TC])

    xT_out = nc.dram_tensor("xT_out", [H, T], F32, kind="ExternalOutput").ap()

    kscr = "ExternalOutput" if debug else "Internal"
    x_res = nc.dram_tensor("x_res", [H, T], F32, kind="Internal").ap()
    kv_sp = nc.dram_tensor("kv_sp", [H, T], BF16, kind=kscr).ap()
    r_sp = nc.dram_tensor("r_sp", [H, T], BF16, kind=kscr).ap()
    wkv_sp = nc.dram_tensor("wkv_sp", [H, T], BF16, kind=kscr).ap()
    act_sp = nc.dram_tensor("act_sp", [NKC * 128, T], BF16, kind=kscr).ap()
    z_sp = nc.dram_tensor("z_sp", [H, T], BF16, kind=kscr).ap()
    h_dump = (nc.dram_tensor("h_dump", [4, H, T], BF16, kind="ExternalOutput").ap()
              if debug else None)
    M_dbg = {"ln": 0}
    wrow_sp = nc.dram_tensor("wrow_sp", [128, 128], F32, kind=kscr).ap()
    max_sp = nc.dram_tensor("max_sp", [NH * NSUB, 128], F32, kind="Internal").ap()
    dwb_sp = nc.dram_tensor("dwb_sp", [HC * 7, 128, 16 * 128], BF16,
                            kind="Internal").ap()
    wbc_sp = nc.dram_tensor("wbc_sp", [80, 128, 16 * 128], BF16,
                            kind="Internal").ap()

    def sb(name, shape, dt):
        return es.enter_context(nc.sbuf_tensor(name, shape, dt))

    h_full = sb("h_full", [128, HC, T], BF16)
    r1 = sb("r1", [128, 32768], BF16)
    r1_d = r1[:, 0:8192].rearrange("p (c t) -> p c t", c=HC)
    r1_mix = [r1[:, 8192:16384].rearrange("p (c t) -> p c t", c=HC),
              r1[:, 16384:24576].rearrange("p (c t) -> p c t", c=HC)]
    r1f = r1[:, 0:16384].bitcast(F32)
    r1_tbuf = r1f[:, 0:2049]
    r1_stmp = r1f[:, 2049:4097]
    r1_kvld = r1[:, 16384:20480].rearrange("p (b t) -> p b t", b=2)
    r1_rld = r1[:, 20480:24576].rearrange("p (b t) -> p b t", b=2)
    r1_pT = r1[:, 0:8192].rearrange("p (c t) -> p c t", c=16)
    r1_vg = r1[:, 8192:16384].rearrange("p (c t) -> p c t", c=16)
    r1_q = r1[:, 16384:18432]
    r1_k = r1[:, 18432:20480]
    r1_nm = r1[:, 20480:24576].rearrange("p (b t) -> p b t", b=2)
    r1_bias = r1[:, 24576:32768].rearrange("p (b t) -> p b t", b=4)
    r1_zb = r1[:, 0:8192].rearrange("p (c t) -> p c t", c=16)

    NSLOT = 4
    ring = [sb(f"ring{i}", [128, 16, 128], BF16) for i in range(NSLOT)]
    ones_col = sb("ones_col", [128, 1], BF16)
    ones_row = sb("ones_row", [1, TC], BF16)
    ones128b = sb("ones128b", [1, 128], BF16)
    ones128f = sb("ones128f", [1, 128], F32)
    ident = sb("ident", [128, 128], F32)
    masks = sb("masks", [128, 4, TC], BF16)
    tmk = sb("tmk", [128, NH], F32)
    tmv = sb("tmv", [128, NH], F32)
    tmr = sb("tmr", [128, NH], F32)
    td = sb("td", [128, NH], F32)
    w_dec = sb("w_dec", [128, NH], F32)
    u_fst = sb("u_fst", [128, NH], F32)
    e_tmp = sb("e_tmp", [128, NH], F32)
    gwt8 = sb("gwt8", [128, HC, NE], BF16)
    xt = sb("xt", [128, 2, TC], F32)
    xbf = sb("xbf", [128, 2, TC], BF16)
    xsq = sb("xsq", [128, 2, TC], BF16)
    lnt = sb("lnt", [128, TC], F32)
    rowA = sb("rowA", [1, TC], F32)
    rowB = sb("rowB", [1, TC], F32)
    rowC = sb("rowC", [1, TC], F32)
    rowD = sb("rowD", [1, TC], F32)
    nmrow = sb("nmrow", [1, NTC, TC], BF16)
    rsrow = sb("rsrow", [1, NTC, TC], BF16)
    xo = sb("xo", [128, 2, TC], F32)
    nx = sb("nx", [128, 2, TC], F32)
    sg = sb("sg", [128, 2, TC], BF16)
    gu = sb("gu", [128, 2, TC], BF16)
    stg = sb("stg", [128, 2, TC], BF16)
    wbv = sb("wbv", [128, NTC, TC], BF16)
    wrow = sb("wrow", [1, T], BF16)
    lgts = sb("lgts", [128, NSUB, NE], F32)
    rt_a = sb("rt_a", [128, NSUB, NE], F32)
    rt_b = sb("rt_b", [128, NSUB, NE], F32)
    rt_c = sb("rt_c", [128, NSUB, NE], F32)
    wmat = sb("wmat", [128, NSUB, NE], F32)
    rt_g = sb("rt_g", [128, NSUB, 4], F32)
    rt_h = sb("rt_h", [128, NSUB, 4], F32)
    rt_r1 = sb("rt_r1", [128, NSUB], F32)
    rt_r2 = sb("rt_r2", [128, NSUB], F32)
    rt_s1 = sb("rt_s1", [128, NSUB, 1], F32)
    rt_s2 = sb("rt_s2", [128, NSUB, 1], F32)
    rt_s3 = sb("rt_s3", [128, NSUB, 1], F32)
    rt_u = sb("rt_u", [128, NSUB, 4], F32)
    mxc = sb("mxc", [128, 8], F32)
    srow2 = rowD
    diag_t = sb("diag_t", [128, TC], F32)
    rb_t = sb("rb_t", [128, TC], F32)
    maxs_t = sb("maxs_t", [128, NSUB], F32)
    mT_t = sb("mT_t", [16, 128], F32)

    pb = [es.enter_context(nc.psum_tensor(f"pb{i}", [128, TC], F32)) for i in range(4)]
    pw = es.enter_context(nc.psum_tensor("pw", [128, 4 * TC], F32))
    pwb = [pw[:, i * TC:(i + 1) * TC] for i in range(4)]

    blk = es.enter_context(nc.Block())

    class Sem:
        def __init__(self, name):
            self.h = es.enter_context(nc.semaphore(name))
            self.n = 0

        def inc(self, ins, k=1):
            ins.then_inc(self.h, k)
            self.n += k
            return self.n

    pe, dv, ac = Sem("pe"), Sem("dv"), Sem("ac")
    lastwait = {}

    def W(eng, ename, sem, val):
        if val is None or val <= 0:
            return
        key = (ename, sem.h)
        if lastwait.get(key, -1) >= val:
            return
        eng.wait_ge(sem.h, val)
        lastwait[key] = val

    def WM(eng, ename, mark):
        if mark is None:
            return
        if isinstance(mark, tuple):
            W(eng, ename, mark[0], mark[1])
            return
        for m in mark:
            WM(eng, ename, m)

    class Pool:

        def __init__(self, name, n):
            self.sems = [Sem(f"{name}{i}") for i in range(n)]
            self.i = 0

        def pre(self, eng, ename):
            sm = self.sems[self.i]
            if sm.n:
                W(eng, ename, sm, sm.n)

        def fin(self, ins):
            sm = self.sems[self.i]
            idx = self.i
            self.i = (self.i + 1) % len(self.sems)
            sm.inc(ins, 16)
            import traceback
            if DMALOG is not None:
                fr = traceback.extract_stack()[-2]
                DMALOG.append((self.sems is qx_ref[0] and "qx" or "qw", idx, sm.n,
                               f"{fr.name}:{fr.lineno}"))
            return (sm, sm.n)

        def snap(self):
            return [(sm, sm.n) for sm in self.sems if sm.n]

    qw = Pool("qw", 12)
    qx = Pool("qx", 24)
    qx_ref[0] = qx.sems

    bankfree = {i: None for i in range(8)}

    def bank_ap(i):
        return pb[i][:, :] if i < 4 else pwb[i - 4][:, :]

    def TB(f): blk.tensor(f)
    def VB(f): blk.vector(f)
    def SB_(f): blk.scalar(f)
    def GB(f): blk.gpsimd(f)
    def YB(f): blk.sync(f)

    M = {}

    ring_state = [None] * NSLOT
    ring_next = [0]

    def ring_load(segs):
        s = ring_next[0]
        ring_next[0] = (s + 1) % NSLOT
        hold = []

        def g(g):
            WM(g, "g", ring_state[s])
            off = 0
            for ap, rows in segs:
                qw.pre(g, "g")
                ins = g.dma_start(out=ring[s][:, off // 128:(off + rows) // 128, :],
                                  in_=ap.rearrange("(c p) n -> p c n", p=128))
                if DMALOG is not None:
                    import traceback
                    DMALOG.append(("ringslot", s, traceback.extract_stack()[-4].name,
                                   traceback.extract_stack()[-4].lineno))
                hold.append(qw.fin(ins))
                off += rows
        GB(g)
        return s, hold

    def ring_load_flat(src2d, ncols, extra_wait):
        s = ring_next[0]
        ring_next[0] = (s + 1) % NSLOT
        hold = []

        def g(g):
            WM(g, "g", ring_state[s])
            WM(g, "g", extra_wait)
            qw.pre(g, "g")
            ins = g.dma_start(
                out=ring[s][:, 0:ncols // 128, :].rearrange("p c n -> p (c n)"),
                in_=src2d[:, 0:ncols])
            hold.append(qw.fin(ins))
        GB(g)
        return s, hold

    def ring_used(s, sem, val):
        ring_state[s] = [(sem, val)]

    def ring_also_used(s, mark):
        ring_state[s].append(mark)

    wbc_next = [0]
    wbc_blk = {}

    def ring_load_cached(key, src2d, ncols):
        if key in wbc_blk:
            blk_i, mk = wbc_blk[key]
            slot, rdy = ring_load_flat(wbc_sp[blk_i], ncols, mk)
            return slot, rdy, None
        blk_i = wbc_next[0]
        wbc_next[0] += 1
        slot, rdy = ring_load_flat(src2d, ncols, None)

        def sst(s_, slot=slot, rdy=list(rdy), blk_i=blk_i, ncols=ncols):
            WM(s_, "s", rdy)
            qx.pre(s_, "s")
            ins = s_.dma_start(
                out=wbc_sp[blk_i][:, 0:ncols],
                in_=ring[slot][:, 0:ncols // 128, :].rearrange("p c n -> p (c n)"))
            wbc_blk[key] = (blk_i, qx.fin(ins))
        SB_(sst)
        return slot, rdy, wbc_blk[key][1]

    class EpiStats:

        def __init__(self, par):
            self.par = par
            self.chain = []
            self.sq_used = {}
            self.cp_used = {}
            self.st2 = {}
            self.cp_m = {}
            self.sq_m = {}

        def nx_guard(self, eng, ename, tc, hc):
            v = self.sq_used.get((tc, hc - 2), self.sq_used.get((tc - 1, hc - 2 + HC)))
            W(eng, ename, ac, v)

        def after_vev(self, tc, hc, nx_dv, src_ap=None):
            ap = nx[:, hc % 2, :] if src_ap is None else src_ap

            def sqr(s, tc=tc, hc=hc, nx_dv=nx_dv, ap=ap):
                W(s, "s", dv, nx_dv)
                W(s, "s", pe, self.st2.get(hc - 2))
                ins = s.activation(xsq[:, hc % 2, :], ap, AF.Square)
                self.sq_m[hc] = ac.inc(ins)
                self.sq_used[(tc, hc)] = self.sq_m[hc]
            SB_(sqr)

            def cpy(v, tc=tc, hc=hc, ap=ap, nx_dv=nx_dv):
                W(v, "v", pe, self.st2.get(hc - 2))
                W(v, "v", dv, nx_dv)
                ins = v.tensor_copy(xbf[:, hc % 2, :], ap)
                self.cp_m[hc] = dv.inc(ins)
                self.cp_used[(tc, hc)] = self.cp_m[hc]
            VB(cpy)

        def stat_mm(self, tc, hc):
            def tst2(t, tc=tc, hc=hc):
                if hc == 0:
                    for b2 in (0, 1):
                        if bankfree[b2] is not None:
                            W(t, "t", *bankfree[b2])
                W(t, "t", dv, self.cp_m[hc])
                t.matmul(pb[0][0:1, :], ones_col[:, 0:1], xbf[:, hc % 2, :],
                         start=(hc == 0), stop=(hc == HC - 1), skip_group_check=True)
                W(t, "t", ac, self.sq_m[hc])
                ins = t.matmul(pb[1][0:1, :], ones_col[:, 0:1], xsq[:, hc % 2, :],
                               start=(hc == 0), stop=(hc == HC - 1), skip_group_check=True)
                self.st2[hc] = pe.inc(ins)
            TB(tst2)

        def after_mm(self, tc, hc):
            if hc >= 2:
                self.stat_mm(tc, hc - 2)

        def end_tc(self, tc):
            self.stat_mm(tc, HC - 2)
            self.stat_mm(tc, HC - 1)
            stat_mark = self.st2[HC - 1]
            par = self.par

            def v2(v, tc=tc, stat_mark=stat_mark):
                W(v, "v", pe, stat_mark)
                WM(v, "v", M.get(("nm_used", tc)))
                with nc.allow_low_precision(reason="LN stats rows stored bf16"):
                    v.tensor_scalar_mul(nmrow[0:1, tc, :], pb[0][0:1, :], -1.0 / H)
                v.tensor_scalar_mul(rowB[0:1, :], pb[1][0:1, :], 1.0 / H)
                v.drain()
                v.tensor_mul(rowC[0:1, :], nmrow[0:1, tc, :],
                             nmrow[0:1, tc, :])
                v.drain()
                v.tensor_sub(rowD[0:1, :], rowB[0:1, :], rowC[0:1, :])
                v.drain()
                ins = v.tensor_scalar_add(rowB[0:1, :], rowD[0:1, :], EPS)
                M["ep_var"] = dv.inc(ins)
            VB(v2)
            bankfree[0] = (dv, M["ep_var"])
            bankfree[1] = (dv, M["ep_var"])

            def s2(s):
                W(s, "s", dv, M["ep_var"])
                ins = s.activation(rowC[0:1, :], rowB[0:1, :], AF.Sqrt)
                M["ep_sd"] = ac.inc(ins)
            SB_(s2)

            def v3(v, tc=tc):
                W(v, "v", ac, M["ep_sd"])
                with nc.allow_low_precision(reason="LN stats rows stored bf16"):
                    ins = v.reciprocal(rsrow[0:1, tc, :], rowC[0:1, :])
                M["ep_rstd"] = dv.inc(ins)
            VB(v3)
            self.chain.append((dv, M["ep_rstd"]))
            self.st2 = {}
            self.cp_m = {}
            self.sq_m = {}

    def setup():
        ldm = []

        def g(g):
            for dst, src in ((tmk, tmk_pt), (tmv, tmv_pt), (tmr, tmr_pt),
                             (td, td_pt), (u_fst, tf_pt), (ident, ident_in)):
                qw.pre(g, "g")
                ins = g.dma_start(out=dst[:], in_=src[:, :])
                ldm.append(qw.fin(ins))
            qw.pre(g, "g")
            ins = g.dma_start(out=masks[:], in_=masks_in.rearrange("p (m t) -> p m t", m=4))
            ldm.append(qw.fin(ins))
            qw.pre(g, "g")
            ins = g.dma_start(out=gwt8[:], in_=gw_T.rearrange("(c p) e -> p c e", p=128))
            ldm.append(qw.fin(ins))
        GB(g)
        ld = ldm

        def v(v):
            v.memset(ones_col[:], 1.0)
            v.memset(ones_row[0:1, :], 1.0)
            v.memset(ones128b[0:1, :], 1.0)
            ins = v.memset(ones128f[0:1, :], 1.0)
            dv.inc(ins)
        VB(v)

        def s(s):
            WM(s, "s", ld)
            s.activation(e_tmp[:], td[:], AF.Exp)
            s.drain()
            ins = s.activation(w_dec[:], e_tmp[:], AF.Exp, scale=-1.0)
            ac.inc(ins)
        SB_(s)

    def layernorm(src, prev_marks):
        h_marks = []
        pe_entry = pe.n
        st_m = {}
        prev_hdone = [None]
        for tc in range(NTC):
            t0 = tc * TC
            ld1 = {}
            c_m = {}
            q_m = {}

            def y1(y, tc=tc, t0=t0):
                if prev_marks is not None:
                    WM(y, "y", prev_marks[tc])
                for hc in range(HC):
                    W(y, "y", dv, c_m.get(hc - 2))
                    qx.pre(y, "y")
                    ins = y.dma_start(out=xt[:, hc % 2, :],
                                      in_=src[hc * 128:(hc + 1) * 128, t0:t0 + TC])
                    ld1[hc] = qx.fin(ins)
            for hc in range(HC):
                def yl(y, hc=hc, tc=tc, t0=t0):
                    if hc == 0 and prev_marks is not None:
                        WM(y, "y", prev_marks[tc])
                    if hc < 2 and prev_hdone[0] is not None:
                        W(y, "y", dv, prev_hdone[0][HC - 2 + hc])
                    W(y, "y", dv, c_m.get(hc - 2))
                    qx.pre(y, "y")
                    ins = y.dma_start(out=xt[:, hc % 2, :],
                                      in_=src[hc * 128:(hc + 1) * 128, t0:t0 + TC])
                    ld1[hc] = qx.fin(ins)
                YB(yl)

                def vc(v, hc=hc):
                    WM(v, "v", ld1[hc])
                    W(v, "v", pe, st_m.get(hc - 2))
                    v.tensor_copy(xbf[:, hc % 2, :], xt[:, hc % 2, :])
                    v.drain()
                    ins = v.tensor_mul(xsq[:, hc % 2, :], xt[:, hc % 2, :],
                                       xt[:, hc % 2, :])
                    c_m[hc] = dv.inc(ins)
                    q_m[hc] = c_m[hc]
                VB(vc)

                def tst(t, hc=hc):
                    if hc == 0:
                        for b in (0, 1):
                            if bankfree[b] is not None:
                                W(t, "t", *bankfree[b])
                    W(t, "t", dv, c_m[hc])
                    t.matmul(pb[0][0:1, :], ones_col[:, 0:1], xbf[:, hc % 2, :],
                             start=(hc == 0), stop=(hc == HC - 1), skip_group_check=True)
                    ins = t.matmul(pb[1][0:1, :], ones_col[:, 0:1], xsq[:, hc % 2, :],
                                   start=(hc == 0), stop=(hc == HC - 1), skip_group_check=True)
                    st_m[hc] = pe.inc(ins)
                    if hc == HC - 1:
                        M["stats"] = st_m[hc]
                TB(tst)

            def v2(v):
                W(v, "v", pe, M["stats"])
                v.tensor_scalar_mul(rowA[0:1, :], pb[0][0:1, :], -1.0 / H)
                v.tensor_scalar_mul(rowB[0:1, :], pb[1][0:1, :], 1.0 / H)
                v.drain()
                v.tensor_mul(rowC[0:1, :], rowA[0:1, :], rowA[0:1, :])
                v.drain()
                v.tensor_sub(rowD[0:1, :], rowB[0:1, :], rowC[0:1, :])
                v.drain()
                ins = v.tensor_scalar_add(rowB[0:1, :], rowD[0:1, :], EPS)
                M["var"] = dv.inc(ins)
            VB(v2)
            bankfree[0] = (dv, M["var"])
            bankfree[1] = (dv, M["var"])

            def s2(s):
                W(s, "s", dv, M["var"])
                ins = s.activation(rowC[0:1, :], rowB[0:1, :], AF.Sqrt)
                M["sd"] = ac.inc(ins)
            SB_(s2)

            def v3(v):
                W(v, "v", ac, M["sd"])
                ins = v.reciprocal(rowB[0:1, :], rowC[0:1, :])
                M["rstd"] = dv.inc(ins)
            VB(v3)

            def t2(t):
                for b in (2, 3):
                    if bankfree[b] is not None:
                        W(t, "t", *bankfree[b])
                W(t, "t", dv, M["rstd"])
                t.matmul(pb[2][:, :], ones128f[0:1, :], rowA[0:1, :],
                         start=True, stop=True, skip_group_check=True)
                ins = t.matmul(pb[3][:, :], ones128f[0:1, :], rowB[0:1, :],
                               start=True, stop=True, skip_group_check=True)
                M["bc"] = pe.inc(ins)
            TB(t2)

            ld2 = {}
            hdone = {}
            for hc in range(HC):
                def yl2(y, hc=hc, t0=t0):
                    if hc < 2:
                        W(y, "y", dv, c_m[HC - 2 + hc])
                    else:
                        W(y, "y", dv, hdone.get(hc - 2))
                    qx.pre(y, "y")
                    ins = y.dma_start(out=xt[:, hc % 2, :],
                                      in_=src[hc * 128:(hc + 1) * 128, t0:t0 + TC])
                    ld2[hc] = qx.fin(ins)
                YB(yl2)

                def vh(v, hc=hc, tc=tc, t0=t0):
                    if hc == 0:
                        W(v, "v", pe, max(M["bc"], pe_entry))
                    WM(v, "v", ld2[hc])
                    v.tensor_add(lnt[:, :], xt[:, hc % 2, :], pb[2][:, :])
                    v.drain()
                    ins = v.tensor_mul(h_full[:, hc, t0:t0 + TC], lnt[:, :], pb[3][:, :])
                    v.drain()
                    hdone[hc] = dv.inc(ins)
                VB(vh)
            h_marks.append(dv.n)
            prev_hdone[0] = dict(hdone)
            bankfree[2] = (dv, dv.n)
            bankfree[3] = (dv, dv.n)
        M["xt_last"] = {0: prev_hdone[0][HC - 2], 1: prev_hdone[0][HC - 1]}
        if debug:
            li = M_dbg["ln"]; M_dbg["ln"] += 1

            def ydb(y, li=li, hm=list(h_marks)):
                W(y, "y", dv, hm[-1])
                for hc in range(HC):
                    qx.pre(y, "y")
                    ins = y.dma_start(out=h_dump[li, hc * 128:(hc + 1) * 128, :],
                                      in_=h_full[:, hc, :])
                    qx.fin(ins)
            YB(ydb)
        return h_marks

    class LNP2:

        def __init__(self, src, stats, nx_marks, hfull_guard):
            self.src = src
            self.stats = stats
            self.nx_marks = nx_marks
            self.hfull_guard = hfull_guard
            self.h_marks = []
            self.prev_hdone = None

        def emit_bc(self, tc):
            def t2(t, tc=tc):
                for b in (2, 3):
                    if bankfree[b] is not None:
                        W(t, "t", *bankfree[b])
                WM(t, "t", self.stats.chain[tc])
                t.matmul(pb[2][:, :], ones128b[0:1, :], nmrow[0:1, tc, :],
                         start=True, stop=True, skip_group_check=True)
                ins = t.matmul(pb[3][:, :], ones128b[0:1, :], rsrow[0:1, tc, :],
                               start=True, stop=True, skip_group_check=True)
                M["bc"] = pe.inc(ins)
                self.bc = M["bc"]
                M[("nm_used", tc)] = (pe, pe.n)
            TB(t2)
            self.ld2 = {}
            self.hdone = {}

        def emit_hc(self, tc, hc):
            src = self.src
            hfull_guard = self.hfull_guard
            t0 = tc * TC
            ld2 = self.ld2
            hdone = self.hdone
            bc = self.bc

            def yl2(y, hc=hc, t0=t0, tc=tc):
                WM(y, "y", self.nx_marks.get((tc, hc)))
                if hc < 2:
                    if self.prev_hdone is not None:
                        W(y, "y", dv, self.prev_hdone[HC - 2 + hc])
                    else:
                        W(y, "y", dv, M["xt_last"][hc])
                else:
                    W(y, "y", dv, hdone.get(hc - 2))
                qx.pre(y, "y")
                ins = y.dma_start(out=xt[:, hc % 2, :],
                                  in_=src[hc * 128:(hc + 1) * 128, t0:t0 + TC])
                ld2[hc] = qx.fin(ins)
            YB(yl2)

            def vh(v, hc=hc, tc=tc, t0=t0, bc=bc):
                if hc == 0:
                    W(v, "v", pe, bc)
                    if hfull_guard is not None:
                        WM(v, "v", hfull_guard(tc))
                WM(v, "v", ld2[hc])
                v.tensor_add(lnt[:, :], xt[:, hc % 2, :], pb[2][:, :])
                v.drain()
                ins = v.tensor_mul(h_full[:, hc, t0:t0 + TC], lnt[:, :], pb[3][:, :])
                v.drain()
                hdone[hc] = dv.inc(ins)
            VB(vh)
            if hc == HC - 1:
                self.h_marks.append(dv.n)
                self.prev_hdone = dict(hdone)
                bankfree[2] = (dv, dv.n)
                bankfree[3] = (dv, dv.n)
                if tc == NTC - 1:
                    M["xt_last"] = {0: self.prev_hdone[HC - 2],
                                    1: self.prev_hdone[HC - 1]}

        def emit(self, tc):
            self.emit_bc(tc)
            for hc in range(HC):
                self.emit_hc(tc, hc)

        def emit_all(self):
            for tc in range(NTC):
                self.emit(tc)
            return self.h_marks

    def out_proj(w_T, rhs_sp, src_old, dst, rhs_store_mark, nch, bias_row,
                 stats=None, xo_guard=None, p2=None, cache_tag="w"):
        store_marks = []
        tc_pe_done = {}
        nx_store = {}
        xo_used = {}
        if p2 is not None:
            p2.nx_marks = nx_store
        XON = 8
        xo_r = r1[:, 16384:24576].bitcast(F32)
        zb2 = [r1[:, 0:8192].rearrange("p (c t) -> p c t", c=16),
               r1[:, 8192:16384].rearrange("p (c t) -> p c t", c=16)]
        for tc in range(NTC):
            t0 = tc * TC
            rhs_m = {}
            zb = zb2[tc % 2]

            def yld(y, tc=tc, t0=t0, zb=zb):
                WM(y, "y", rhs_store_mark)
                W(y, "y", pe, tc_pe_done.get(tc - 2))
                for kc in range(nch):
                    qx.pre(y, "y")
                    ins = y.dma_start(out=zb[:, kc, :],
                                      in_=rhs_sp[kc * 128:(kc + 1) * 128, t0:t0 + TC])
                    rhs_m[kc] = qx.fin(ins)
            YB(yld)

            for hc in range(HC):
                slot, rdy, pend_w = ring_load_cached((cache_tag, hc), w_T[hc],
                                                     nch * 128)
                bank = 4 + (hc % 2)
                xo_m = {}
                xs = hc % XON
                xo_ap = xo_r[:, xs * TC:(xs + 1) * TC]

                def yxo(y, hc=hc, t0=t0, tc=tc, xo_ap=xo_ap):
                    if tc == 0 and hc < XON:
                        WM(y, "y", xo_guard)
                    WM(y, "y", nx_store.get((tc, hc - XON),
                                            nx_store.get((tc - 1, hc - XON + HC))))
                    if stats is not None:
                        W(y, "y", ac, stats.sq_used.get((tc, hc - XON),
                                      stats.sq_used.get((tc - 1, hc - XON + HC))))
                        W(y, "y", dv, stats.cp_used.get((tc, hc - XON),
                                      stats.cp_used.get((tc - 1, hc - XON + HC))))
                    else:
                        W(y, "y", dv, xo_used.get((tc, hc - XON),
                                      xo_used.get((tc - 1, hc - XON + HC))))
                    qx.pre(y, "y")
                    ins = y.dma_start(out=xo_ap,
                                      in_=src_old[hc * 128:(hc + 1) * 128, t0:t0 + TC])
                    xo_m[0] = qx.fin(ins)
                YB(yxo)

                def tmm(t, slot=slot, rdy=rdy, bank=bank, hc=hc, tc=tc, zb=zb):
                    WM(t, "t", rdy)
                    if bankfree[bank] is not None:
                        W(t, "t", *bankfree[bank])
                    for kc in range(nch):
                        WM(t, "t", rhs_m[kc])
                        ins = t.matmul(pwb[bank - 4][:, :], ring[slot][:, kc, :],
                                       zb[:, kc, :], start=(kc == 0),
                                       stop=(kc == nch - 1 and bias_row is None),
                                       skip_group_check=True)
                    if bias_row is not None:
                        ins = t.matmul(pwb[bank - 4][:, :],
                                       bias_row[0:1, hc * 128:(hc + 1) * 128],
                                       ones_row[0:1, :], start=False, stop=True,
                                       skip_group_check=True)
                    M["op_mm"] = pe.inc(ins)
                    tc_pe_done[tc] = pe.n
                TB(tmm)
                ring_used(slot, pe, M["op_mm"])
                if pend_w is not None:
                    ring_also_used(slot, pend_w)
                mm_mark = M["op_mm"]
                if stats is not None:
                    stats.after_mm(tc, hc)

                def vev(v, bank=bank, hc=hc, tc=tc, mm_mark=mm_mark, xo_ap=xo_ap):
                    W(v, "v", pe, mm_mark)
                    WM(v, "v", xo_m[0])
                    ins = v.tensor_add(xo_ap, pwb[bank - 4][:, :], xo_ap)
                    xo_used[(tc, hc)] = dv.inc(ins)
                VB(vev)
                ev = dv.n
                bankfree[bank] = (dv, ev)
                if stats is not None:
                    stats.after_vev(tc, hc, ev, src_ap=xo_ap)

                def yst(y, hc=hc, t0=t0, tc=tc, ev=ev, xo_ap=xo_ap):
                    W(y, "y", dv, ev)
                    qx.pre(y, "y")
                    ins = y.dma_start(out=dst[hc * 128:(hc + 1) * 128, t0:t0 + TC],
                                      in_=xo_ap)
                    nx_store[(tc, hc)] = qx.fin(ins)
                YB(yst)
            if stats is not None:
                stats.end_tc(tc)
            store_marks.append(qx.snap())
        return store_marks, nx_store

    def attn_a(h_marks, src_old, dst, stats=None, p2=None):
        sg_guard = {}
        kv_marks = {oc: [] for oc in range(HC)}
        r_marks = {oc: [] for oc in range(HC)}
        kvm = {}
        scm = {}
        wkv_st = {}
        hfw = h_full[:].rearrange("p c t -> p (c t)")[:, 0:16384].bitcast(F32)
        sc_tbuf = hfw[:, 0:T + 1]
        sc_stmp = hfw[:, T + 1:2 * T + 1]

        def emit_scan(cc, mix1_guard):
            def ysl(y, cc=cc):
                if cc < 2:
                    W(y, "y", pe, mix1_guard)
                WM(y, "y", kv_marks[cc])
                WM(y, "y", r_marks[cc])
                W(y, "y", dv, scm.get(cc - 2))
                WM(y, "y", wkv_st.get(cc - 2))
                qx.pre(y, "y")
                ins = y.dma_start(out=r1_kvld[:, cc % 2, :],
                                  in_=kv_sp[cc * 128:(cc + 1) * 128, :])
                mka = qx.fin(ins)
                qx.pre(y, "y")
                ins = y.dma_start(out=r1_rld[:, cc % 2, :],
                                  in_=r_sp[cc * 128:(cc + 1) * 128, :])
                kvm[cc] = [mka, qx.fin(ins)]
            YB(ysl)

            def vsc(v, cc=cc):
                WM(v, "v", kvm[cc])
                p = cc % 2
                v.memset(sc_tbuf[:, 0:1], 0.0)
                v.drain()
                v.tensor_tensor_scan(sc_tbuf[:, 1:T + 1],
                                     w_dec[:, cc:cc + 1].broadcast_to([128, T]),
                                     r1_kvld[:, p, :], 0.0, ALU.mult, ALU.add)
                v.drain()
                v.scalar_tensor_tensor(sc_stmp[:, 0:T], r1_kvld[:, p, :],
                                       u_fst[:, cc:cc + 1], sc_tbuf[:, 0:T],
                                       ALU.mult, ALU.add)
                v.drain()
                ins = v.tensor_mul(r1_kvld[:, p, :], sc_stmp[:, 0:T], r1_rld[:, p, :])
                v.drain()
                scm[cc] = dv.inc(ins)
            VB(vsc)

            def ysc(y, cc=cc):
                W(y, "y", dv, scm[cc])
                qx.pre(y, "y")
                ins = y.dma_start(out=wkv_sp[cc * 128:(cc + 1) * 128, :],
                                  in_=r1_kvld[:, cc % 2, :])
                wkv_st[cc] = qx.fin(ins)
            YB(ysc)
        for tc in range(NTC):
            t0 = tc * TC

            def vd(v, tc=tc, t0=t0):
                W(v, "v", dv, h_marks[tc])
                if tc == 0:
                    v.tensor_sub(r1_d[:, :, 1:TC],
                                 h_full[:, :, t0 + 1:t0 + TC], h_full[:, :, t0:t0 + TC - 1])
                    ins = v.tensor_copy(r1_d[:, :, 0:1], h_full[:, :, 0:1])
                else:
                    ins = v.tensor_sub(r1_d[:, :, 0:TC],
                                       h_full[:, :, t0:t0 + TC], h_full[:, :, t0 - 1:t0 + TC - 1])
                v.drain()
                M["aa_d"] = dv.inc(ins)
            VB(vd)

            def build_mix(which, tmx, mb, tc=tc, t0=t0):
                mixbuf = r1_mix[which]

                def vm(v):
                    W(v, "v", dv, M["aa_d"])
                    W(v, "v", pe, M.get(("aa_mix_used", which)))
                    if tc == 0:
                        if bankfree[mb] is not None:
                            W(v, "v", *bankfree[mb])
                        for hc in range(HC):
                            v.tensor_mul(pb[mb][:, :], r1_d[:, hc, :],
                                         tmx[:, hc:hc + 1].broadcast_to([128, TC]))
                            v.drain()
                            v.tensor_add(mixbuf[:, hc, 1:TC], pb[mb][:, 1:TC],
                                         h_full[:, hc, t0:t0 + TC - 1])
                            ins = v.tensor_copy(mixbuf[:, hc, 0:1], pb[mb][:, 0:1])
                            v.drain()
                        M["aa_mix", which] = dv.inc(ins)
                        bankfree[mb] = (dv, M["aa_mix", which])
                    else:
                        for hc in range(HC):
                            ins = v.scalar_tensor_tensor(
                                mixbuf[:, hc, :], r1_d[:, hc, :], tmx[:, hc:hc + 1],
                                h_full[:, hc, t0 - 1:t0 + TC - 1], ALU.mult, ALU.add)
                        v.drain()
                        M["aa_mix", which] = dv.inc(ins)
                VB(vm)
                return M["aa_mix", which]

            def project(mixbuf, slot, rdy, bank, mix_mark, last_of_mix, which):
                def tmm(t):
                    WM(t, "t", rdy)
                    W(t, "t", dv, mix_mark)
                    if bankfree[bank] is not None:
                        W(t, "t", *bankfree[bank])
                    for hc in range(HC):
                        ins = t.matmul(pwb[bank - 4][:, :], ring[slot][:, hc, :],
                                       mixbuf[:, hc, :], start=(hc == 0),
                                       stop=(hc == HC - 1), skip_group_check=True)
                    M["aa_mm"] = pe.inc(ins)
                    if last_of_mix:
                        M[("aa_mix_used", which)] = pe.n
                TB(tmm)
                ring_used(slot, pe, M["aa_mm"])
                return M["aa_mm"]

            mk = build_mix(0, tmk, 2)
            mv = build_mix(1, tmv, 3)
            for oc in range(HC):
                slot_k, rdy_k, pend_k = ring_load_cached(("k", oc), wk_tl[oc], H)
                slot_v, rdy_v, pend_v = ring_load_cached(("v", oc), wv_tl[oc], H)
                bk_ = 4 + (oc % 2)
                bv_ = 6 + (oc % 2)
                mm_k = project(r1_mix[0], slot_k, rdy_k, bk_, mk, oc == HC - 1, 0)
                if pend_k is not None:
                    ring_also_used(slot_k, pend_k)

                def se(s, bk_=bk_, mm_k=mm_k, oc=oc):
                    W(s, "s", pe, mm_k)
                    W(s, "s", dv, sg_guard.get(oc % 2))
                    ins = s.activation(sg[:, oc % 2, :], pwb[bk_ - 4][:, :], AF.Copy)
                    M["aa_kev"] = ac.inc(ins)
                SB_(se)
                kev = M["aa_kev"]
                bankfree[bk_] = (ac, kev)

                mm_v = project(r1_mix[1], slot_v, rdy_v, bv_, mv, oc == HC - 1, 1)
                if pend_v is not None:
                    ring_also_used(slot_v, pend_v)

                def ve(v, bv_=bv_, mm_v=mm_v, kev=kev, oc=oc):
                    W(v, "v", pe, mm_v)
                    W(v, "v", ac, kev)
                    WM(v, "v", M.get(("aa_stg", oc % 2)))
                    ins = v.tensor_mul(stg[:, oc % 2, :], pwb[bv_ - 4][:, :],
                                       sg[:, oc % 2, :])
                    M["aa_kv"] = dv.inc(ins)
                    sg_guard[oc % 2] = M["aa_kv"]
                VB(ve)
                bankfree[bv_] = (dv, M["aa_kv"])

                def ysp(y, oc=oc, t0=t0, ev=M["aa_kv"], tc=tc):
                    W(y, "y", dv, ev)
                    qx.pre(y, "y")
                    ins = y.dma_start(out=kv_sp[oc * 128:(oc + 1) * 128, t0:t0 + TC],
                                      in_=stg[:, oc % 2, :])
                    M[("aa_stg", oc % 2)] = qx.fin(ins)
                    kv_marks[oc].append(M[("aa_stg", oc % 2)])
                YB(ysp)

            mr = build_mix(0, tmr, 2)
            for oc in range(HC):
                slot_r, rdy_r, pend_r = ring_load_cached(("r", oc), wr_tl[oc], H)
                br_ = 4 + (oc % 2)
                mm_r = project(r1_mix[0], slot_r, rdy_r, br_, mr, oc == HC - 1, 0)
                if pend_r is not None:
                    ring_also_used(slot_r, pend_r)

                def se(s, br_=br_, mm_r=mm_r, oc=oc):
                    W(s, "s", pe, mm_r)
                    WM(s, "s", M.get(("aa_stg", oc % 2)))
                    ins = s.activation(stg[:, oc % 2, :], pwb[br_ - 4][:, :], AF.Sigmoid)
                    M["aa_rev"] = ac.inc(ins)
                SB_(se)
                bankfree[br_] = (ac, M["aa_rev"])

                def ysp(y, oc=oc, t0=t0, ev=M["aa_rev"], tc=tc):
                    W(y, "y", ac, ev)
                    qx.pre(y, "y")
                    ins = y.dma_start(out=r_sp[oc * 128:(oc + 1) * 128, t0:t0 + TC],
                                      in_=stg[:, oc % 2, :])
                    M[("aa_stg", oc % 2)] = qx.fin(ins)
                    r_marks[oc].append(M[("aa_stg", oc % 2)])
                YB(ysp)
                if tc == NTC - 1:
                    emit_scan(oc, M[("aa_mix_used", 1)] if oc < 2 else None)

        snap = qx.snap()
        return out_proj(wo_tl, wkv_sp, src_old, dst, snap, HC, None,
                        stats=stats, xo_guard=snap, p2=p2, cache_tag="wo")

    def ffn(h_marks, src_old, dst, stats=None, p2=None):
        lg_cp = {}
        for ts in range(NSUB):
            b = ts % 2

            def tlg(t, ts=ts, b=b):
                W(t, "t", dv, h_marks[ts // 4])
                W(t, "t", dv, lg_cp.get(ts - 2))
                if ts < 2 and bankfree[b] is not None:
                    W(t, "t", *bankfree[b])
                for hc in range(HC):
                    ins = t.matmul(pb[b][0:128, 0:NE],
                                   h_full[:, hc, ts * 128:(ts + 1) * 128],
                                   gwt8[:, hc, :], start=(hc == 0), stop=(hc == HC - 1),
                                   skip_group_check=True)
                M["lg"] = pe.inc(ins)
            TB(tlg)

            def vlg(v, ts=ts, b=b):
                W(v, "v", pe, M["lg"])
                ins = v.tensor_copy(lgts[:, ts, :], pb[b][0:128, 0:NE])
                lg_cp[ts] = dv.inc(ins)
            VB(vlg)
        bankfree[0] = (dv, dv.n)
        bankfree[1] = (dv, dv.n)
        lg_mark = dv.n

        def srt(s):
            W(s, "s", dv, lg_mark)
            ins = s.activation(rt_a[:], lgts[:], AF.Exp)
            M["rt_exp"] = ac.inc(ins)
        SB_(srt)

        def vrt(v):
            W(v, "v", ac, M["rt_exp"])
            v.reduce_sum(rt_r1[:, :], rt_a[:], axis=AX.X)
            v.drain()
            v.reciprocal(rt_r2[:, :], rt_r1[:, :])
            v.drain()
            v.tensor_tensor(rt_b[:], rt_a[:],
                            rt_r2[:, :, None].broadcast_to([128, NSUB, NE]), ALU.mult)
            v.drain()
            v.tensor_tensor(rt_g[:], rt_b[:, :, 0:NE:2], rt_b[:, :, 1:NE:2], ALU.max)
            v.drain()
            g = rt_g
            u = rt_u
            v.tensor_tensor(u[:, :, 0:1], g[:, :, 0:1], g[:, :, 1:2], ALU.min)
            v.tensor_tensor(u[:, :, 1:2], g[:, :, 2:3], g[:, :, 3:4], ALU.min)
            v.tensor_tensor(u[:, :, 2:3], g[:, :, 0:1], g[:, :, 1:2], ALU.max)
            v.tensor_tensor(u[:, :, 3:4], g[:, :, 2:3], g[:, :, 3:4], ALU.max)
            v.drain()
            v.tensor_tensor(rt_s1[:, :, 0:1], u[:, :, 0:1], u[:, :, 1:2], ALU.max)
            v.tensor_tensor(rt_s2[:, :, 0:1], u[:, :, 2:3], u[:, :, 3:4], ALU.min)
            v.drain()
            v.tensor_tensor(rt_s3[:, :, 0:1], rt_s1[:, :, 0:1], rt_s2[:, :, 0:1], ALU.max)
            v.drain()
            v.tensor_tensor(rt_h[:], g[:],
                            rt_s3[:, :, 0:1].broadcast_to([128, NSUB, 4]), ALU.is_ge)
            v.drain()
            v.tensor_tensor(rt_c[:, :, 0:NE:2], rt_b[:, :, 0:NE:2], rt_h[:], ALU.mult)
            v.tensor_tensor(rt_c[:, :, 1:NE:2], rt_b[:, :, 1:NE:2], rt_h[:], ALU.mult)
            v.drain()
            v.reduce_max(rt_r1[:, :], rt_c[:], axis=AX.X)
            v.drain()
            v.tensor_tensor(rt_a[:], rt_c[:],
                            rt_r1[:, :, None].broadcast_to([128, NSUB, NE]), ALU.is_lt)
            v.drain()
            v.tensor_tensor(rt_b[:], rt_c[:], rt_a[:], ALU.mult)
            v.drain()
            v.reduce_max(rt_r2[:, :], rt_b[:], axis=AX.X)
            v.drain()
            v.tensor_tensor(rt_a[:], rt_c[:],
                            rt_r2[:, :, None].broadcast_to([128, NSUB, NE]), ALU.is_ge)
            v.drain()
            ins = v.tensor_tensor(wmat[:], rt_c[:], rt_a[:], ALU.mult)
            M["wm"] = dv.inc(ins)
        VB(vrt)

        def twm(t):
            W(t, "t", dv, M["wm"])
            if bankfree[0] is not None:
                W(t, "t", *bankfree[0])
            ins = t.transpose(pb[0][0:128, 0:128],
                              wmat[:].rearrange("p a b -> p (a b)"), ident[:, :])
            M["wmT"] = pe.inc(ins)
        TB(twm)

        def vwm(v):
            W(v, "v", pe, M["wmT"])
            ins = v.tensor_copy(lnt[:, 0:128], pb[0][0:128, 0:128])
            M["wmTc"] = dv.inc(ins)
        VB(vwm)
        bankfree[0] = (dv, M["wmTc"])

        def ywm(y):
            W(y, "y", dv, M["wmTc"])
            qx.pre(y, "y")
            ins = y.dma_start(out=wrow_sp[:, :], in_=lnt[0:128, 0:128])
            M["wrow_st"] = qx.fin(ins)
        YB(ywm)

        spill_m = {}
        gu_used = {}
        M["wb_used_pe"] = None

        def do_expert(n, gsrc, usrc, nfc, row0, scaled):
            if scaled:
                def gw_(g, n=n):
                    WM(g, "g", M["wrow_st"])
                    W(g, "g", pe, M["wb_used_pe"])
                    qw.pre(g, "g")
                    ins = g.dma_start(out=wrow[0:1, :].rearrange("o (s p) -> o s p", p=128),
                                      in_=wrow_sp[n:128:NE, :])
                    M["wrow_ld"] = qw.fin(ins)
                GB(gw_)

            for tc in range(NTC):
                if not scaled:
                    break

                def twb(t, tc=tc):
                    WM(t, "t", M["wrow_ld"])
                    W(t, "t", dv, M.get("wb_cp"))
                    if bankfree[1] is not None:
                        W(t, "t", *bankfree[1])
                    ins = t.matmul(pb[1][:, :], ones128b[0:1, :],
                                   wrow[0:1, tc * TC:(tc + 1) * TC],
                                   start=True, stop=True, skip_group_check=True)
                    M["wb_mm"] = pe.inc(ins)
                TB(twb)

                def vwb(v, tc=tc):
                    W(v, "v", pe, M["wb_mm"])
                    ins = v.tensor_copy(wbv[:, tc, :], pb[1][:, :])
                    M["wb_cp"] = dv.inc(ins)
                VB(vwb)
                bankfree[1] = (dv, M["wb_cp"])
            wb_mark = M.get("wb_cp", 0) if scaled else 0

            for fc in range(nfc):
                slot_g, rdy_g = ring_load_flat(gsrc[fc], H, None)
                slot_u, rdy_u = ring_load_flat(usrc[fc], H, None)
                for tc in range(NTC):
                    t0 = tc * TC
                    bg = 4 + (tc % 2)
                    bu = 6 + (tc % 2)

                    def tge(t, slot_g=slot_g, rdy_g=rdy_g, slot_u=slot_u, rdy_u=rdy_u,
                            bg=bg, bu=bu, tc=tc, t0=t0):
                        WM(t, "t", rdy_u)
                        W(t, "t", dv, h_marks[tc])
                        if bankfree[bg] is not None:
                            W(t, "t", *bankfree[bg])
                        for hc in range(HC):
                            ins = t.matmul(pwb[bg - 4][:, :], ring[slot_g][:, hc, :],
                                           h_full[:, hc, t0:t0 + TC], start=(hc == 0),
                                           stop=(hc == HC - 1), skip_group_check=True)
                        M["g_mm"] = pe.inc(ins)
                        if bankfree[bu] is not None:
                            W(t, "t", *bankfree[bu])
                        for hc in range(HC):
                            ins = t.matmul(pwb[bu - 4][:, :], ring[slot_u][:, hc, :],
                                           h_full[:, hc, t0:t0 + TC], start=(hc == 0),
                                           stop=(hc == HC - 1), skip_group_check=True)
                        M["u_mm"] = pe.inc(ins)
                    TB(tge)
                    g_mm, u_mm = M["g_mm"], M["u_mm"]

                    def sev(s, bg=bg, tc=tc, g_mm=g_mm):
                        W(s, "s", pe, g_mm)
                        W(s, "s", dv, gu_used.get(("sg", tc % 2)))
                        ins = s.activation(sg[:, tc % 2, :], pwb[bg - 4][:, :], AF.Silu)
                        M["silu"] = ac.inc(ins)
                    SB_(sev)
                    bankfree[bg] = (ac, M["silu"])
                    silu_m = M["silu"]

                    def vev(v, bu=bu, tc=tc, silu_m=silu_m, u_mm=u_mm,
                            wb_mark=wb_mark, scaled=scaled):
                        W(v, "v", ac, silu_m)
                        W(v, "v", pe, u_mm)
                        v.tensor_mul(gu[:, tc % 2, :], pwb[bu - 4][:, :], sg[:, tc % 2, :])
                        v.drain()
                        WM(v, "v", spill_m.get(tc % 2))
                        if scaled:
                            W(v, "v", dv, wb_mark)
                            ins = v.tensor_mul(stg[:, tc % 2, :], gu[:, tc % 2, :],
                                               wbv[:, tc, :])
                        else:
                            ins = v.tensor_copy(stg[:, tc % 2, :], gu[:, tc % 2, :])
                        M["act_ev"] = dv.inc(ins)
                        gu_used[("sg", tc % 2)] = M["act_ev"]
                    VB(vev)
                    bankfree[bu] = (dv, M["act_ev"])
                    act_m = M["act_ev"]

                    def ysp(y, row0=row0, fc=fc, t0=t0, tc=tc, act_m=act_m):
                        W(y, "y", dv, act_m)
                        r = row0 + fc * 128
                        qx.pre(y, "y")
                        ins = y.dma_start(out=act_sp[r:r + 128, t0:t0 + TC],
                                          in_=stg[:, tc % 2, :])
                        spill_m[tc % 2] = qx.fin(ins)
                    YB(ysp)
                ring_used(slot_g, pe, M["g_mm"])
                ring_used(slot_u, pe, M["u_mm"])
            if scaled:
                M["wb_used_pe"] = M["wb_mm"]

        for n in range(NE):
            do_expert(n, gate_tl[n * FC:(n + 1) * FC], up_tl[n * FC:(n + 1) * FC],
                      FC, n * FC * 128, True)
        do_expert(0, shg_tl, shu_tl, SFC, NE * FC * 128, False)
        spillA = qx.snap()
        phaseA_pe = pe.n

        def actview(j):
            if j < 64:
                return h_full[:].rearrange("p c t -> p (c t)")[:, j * TC:(j + 1) * TC]
            return r1[:, (j - 64) * TC:(j - 63) * TC]

        store_marks = []
        dn_pe = {}
        nx_store = {}
        xo_used = {}
        for tc in range(NTC):
            t0 = tc * TC
            act_m = {}

            def yact(y, tc=tc, t0=t0):
                if tc == 0:
                    WM(y, "y", spillA)
                    W(y, "y", pe, phaseA_pe)
                for j in range(NKC):
                    if tc > 0:
                        WM(y, "y", M[("dn15", tc - 1, j // 16)])
                    qx.pre(y, "y")
                    ins = y.dma_start(out=actview(j), in_=act_sp[j * 128:(j + 1) * 128,
                                                                 t0:t0 + TC])
                    act_m[j] = qx.fin(ins)
            YB(yact)

            for hc in range(HC):
                xo_m = {}

                def yxo(y, hc=hc, t0=t0, tc=tc):
                    W(y, "y", dv, xo_used.get((tc, hc - 2), xo_used.get((tc - 1, hc - 2 + HC))))
                    qx.pre(y, "y")
                    ins = y.dma_start(out=xo[:, hc % 2, :],
                                      in_=src_old[hc * 128:(hc + 1) * 128, t0:t0 + TC])
                    xo_m[0] = qx.fin(ins)
                YB(yxo)

                bank = 4 + (hc % 2)
                j = 0
                jg = 0
                first = True
                while j < NKC:
                    nch = min(16, NKC - j)
                    blk_i = hc * 7 + jg
                    ck = ("dwc", hc, jg)
                    pend_store = None
                    if ck in M:
                        slot, rdy = ring_load_flat(dwb_sp[blk_i], nch * 128, M[ck])
                    else:
                        slot, rdy = ring_load_flat(dwn_tl[blk_i], nch * 128, None)

                        def sst(s_, slot=slot, rdy=list(rdy), blk_i=blk_i,
                                nch=nch, ck=ck):
                            WM(s_, "s", rdy)
                            qx.pre(s_, "s")
                            ins = s_.dma_start(
                                out=dwb_sp[blk_i][:, 0:nch * 128],
                                in_=ring[slot][:, 0:nch, :].rearrange("p c n -> p (c n)"))
                            M[ck] = qx.fin(ins)
                        SB_(sst)
                        pend_store = M[ck]

                    def tdn(t, slot=slot, rdy=rdy, j0=j, nch=nch, bank=bank, tc=tc,
                            first=first, hc=hc, jg=jg):
                        WM(t, "t", rdy)
                        if first and bankfree[bank] is not None:
                            W(t, "t", *bankfree[bank])
                        for c in range(nch):
                            jj = j0 + c
                            WM(t, "t", act_m[jj])
                            ins = t.matmul(pwb[bank - 4][:, :], ring[slot][:, c, :],
                                           actview(jj), start=(jj == 0), stop=(jj == NKC - 1),
                                           skip_group_check=True)
                        M["dn_mm"] = pe.inc(ins)
                        dn_pe[tc] = pe.n
                        if hc == HC - 1:
                            M[("dn15", tc, jg)] = (pe, pe.n)
                    TB(tdn)
                    ring_used(slot, pe, M["dn_mm"])
                    if pend_store is not None:
                        ring_also_used(slot, pend_store)
                    first = False
                    j += nch
                    jg += 1
                mm_mark = M["dn_mm"]
                if stats is not None:
                    stats.after_mm(tc, hc)

                def vev(v, bank=bank, hc=hc, tc=tc, mm_mark=mm_mark):
                    W(v, "v", pe, mm_mark)
                    WM(v, "v", xo_m[0])
                    WM(v, "v", nx_store.get((tc, hc - 2), nx_store.get((tc - 1, hc - 2 + HC))))
                    if stats is not None:
                        stats.nx_guard(v, "v", tc, hc)
                    ins = v.tensor_add(nx[:, hc % 2, :], pwb[bank - 4][:, :], xo[:, hc % 2, :])
                    xo_used[(tc, hc)] = dv.inc(ins)
                VB(vev)
                ev = dv.n
                bankfree[bank] = (dv, ev)
                if stats is not None:
                    stats.after_vev(tc, hc, ev)

                def yst(y, hc=hc, t0=t0, tc=tc, ev=ev):
                    W(y, "y", dv, ev)
                    qx.pre(y, "y")
                    ins = y.dma_start(out=dst[hc * 128:(hc + 1) * 128, t0:t0 + TC],
                                      in_=nx[:, hc % 2, :])
                    nx_store[(tc, hc)] = qx.fin(ins)
                YB(yst)
            if stats is not None:
                stats.end_tc(tc)
            if p2 is not None:
                p2.emit(tc)
            store_marks.append(qx.snap())
        return store_marks, nx_store

    def attn_b(h_marks, src_old, dst, stats=None, p2=None):
        h_all = h_marks[NTC - 1]
        pe_entry = pe.n

        def vguard(v):
            W(v, "v", pe, pe_entry)
        VB(vguard)

        def sguard(s):
            W(s, "s", pe, pe_entry)
        SB_(sguard)

        bias_marks = []

        def gbi(g):
            W(g, "g", pe, pe_entry)
            for i, src_d in enumerate((bq_r, bk_r, bv_r, bo_r)):
                qw.pre(g, "g")
                ins = g.dma_start(out=r1_bias[0:1, i, :], in_=src_d[0:1, :])
                bias_marks.append(qw.fin(ins))
        GB(gbi)
        bias_m = bias_marks
        bqv = r1_bias[0:1, 0, :]
        bkv = r1_bias[0:1, 1, :]
        bvv = r1_bias[0:1, 2, :]
        bov = r1_bias[0:1, 3, :]

        qv = r1_q
        kv_ = r1_k
        nm_dram = {}
        exp_done = {}
        av_prev = {}
        stq_prev = {}
        vg_av = {}

        for head in range(NH):
            grp = head // 4
            if head % 4 == 0:
                vslots = []
                for q4 in range(4):
                    s_, r_ = ring_load_flat(wvb_tl[grp * 4 + q4], H, None)
                    vslots.append((s_, r_))
                rdy_all = [r_ for _, r_ in vslots]
                for ts in range(NSUB):
                    bank = 4 + (ts % 2)

                    def tv(t, ts=ts, bank=bank, vslots=vslots, rdy_all=rdy_all, grp=grp):
                        if bankfree[bank] is not None:
                            W(t, "t", *bankfree[bank])
                        W(t, "t", dv, h_marks[ts // 4])
                        WM(t, "t", [rdy_all, bias_m])
                        for q4 in range(4):
                            sl = vslots[q4][0]
                            for hc in range(HC):
                                t.matmul(pwb[bank - 4][:, q4 * 128:(q4 + 1) * 128],
                                         h_full[:, hc, ts * 128:(ts + 1) * 128],
                                         ring[sl][:, hc, :], start=(hc == 0), stop=False,
                                         skip_group_check=True)
                            ins = t.matmul(pwb[bank - 4][:, q4 * 128:(q4 + 1) * 128],
                                           ones128b[0:1, :],
                                           bvv[0:1, grp * 512 + q4 * 128:
                                               grp * 512 + (q4 + 1) * 128],
                                           start=False, stop=True, skip_group_check=True)
                        M["v_mm"] = pe.inc(ins)
                    TB(tv)
                    mmv = M["v_mm"]

                    def sv(s, ts=ts, bank=bank, mmv=mmv, grp=grp):
                        W(s, "s", pe, mmv)
                        W(s, "s", pe, vg_av.get(grp - 1))
                        ins = s.activation(r1_vg[:, ts, :], pwb[bank - 4][:, :], AF.Copy)
                        M["v_ev"] = ac.inc(ins)
                    SB_(sv)
                    bankfree[bank] = (ac, M["v_ev"])
                for sl, _ in vslots:
                    ring_used(sl, pe, M["v_mm"])
                vg_mark = M["v_ev"]

            slot_q, rdy_q = ring_load_flat(wqh_tl[head], H, None)
            slot_k, rdy_k = ring_load_flat(wkh_tl[head], H, None)
            for tc in range(NTC):
                t0 = tc * TC
                b_q = 0 + (tc % 2)
                b_k = 2 + (tc % 2)

                def tqk(t, tc=tc, t0=t0, slot_q=slot_q, slot_k=slot_k, rdy_q=rdy_q,
                        rdy_k=rdy_k, head=head, b_q=b_q, b_k=b_k):
                    WM(t, "t", [rdy_q, rdy_k, bias_m])
                    W(t, "t", dv, h_marks[tc])
                    if bankfree[b_q] is not None:
                        W(t, "t", *bankfree[b_q])
                    for hc in range(HC):
                        t.matmul(pb[b_q][:, :], ring[slot_q][:, hc, :],
                                 h_full[:, hc, t0:t0 + TC], start=(hc == 0), stop=False,
                                 skip_group_check=True)
                    ins = t.matmul(pb[b_q][:, :], bqv[0:1, head * 128:(head + 1) * 128],
                                   ones_row[0:1, :], start=False, stop=True,
                                   skip_group_check=True)
                    M["q_mm"] = pe.inc(ins)
                    if bankfree[b_k] is not None:
                        W(t, "t", *bankfree[b_k])
                    for hc in range(HC):
                        t.matmul(pb[b_k][:, :], ring[slot_k][:, hc, :],
                                 h_full[:, hc, t0:t0 + TC], start=(hc == 0), stop=False,
                                 skip_group_check=True)
                    ins = t.matmul(pb[b_k][:, :], bkv[0:1, head * 128:(head + 1) * 128],
                                   ones_row[0:1, :], start=False, stop=True,
                                   skip_group_check=True)
                    M["k_mm"] = pe.inc(ins)
                TB(tqk)
                if head == NH - 1:
                    M[("hfree_b", tc)] = (pe, pe.n)

                def sqk(s, tc=tc, t0=t0, b_q=b_q, b_k=b_k, head=head):
                    W(s, "s", pe, M["k_mm"])
                    W(s, "s", pe, stq_prev.get(head - 1))
                    s.activation(qv[:, t0:t0 + TC], pb[b_q][:, :], AF.Copy, scale=ISQD)
                    ins = s.activation(kv_[:, t0:t0 + TC], pb[b_k][:, :], AF.Copy)
                    M["qk_ev"] = ac.inc(ins)
                SB_(sqk)
                bankfree[b_q] = (ac, M["qk_ev"])
                bankfree[b_k] = (ac, M["qk_ev"])
            ring_used(slot_q, pe, M["q_mm"])
            ring_used(slot_k, pe, M["k_mm"])
            qk_mark = M["qk_ev"]

            red = {}
            for qs in range(NSUB):
                live = (qs + 1) * 128
                use_pw = (qs % 2 == 0)
                banks = [4, 5, 6, 7] if use_pw else [0, 1, 2, 3]
                nq = ceil_div(live, TC)

                def tsm(t, qs=qs, live=live, use_pw=use_pw, banks=banks, nq=nq,
                        qk_mark=qk_mark):
                    W(t, "t", ac, qk_mark)
                    for b in banks[:nq]:
                        if bankfree[b] is not None:
                            W(t, "t", *bankfree[b])
                    for kq in range(nq):
                        n = min(TC, live - kq * TC)
                        dst_ = pw[:, kq * TC:kq * TC + n] if use_pw else pb[kq][:, 0:n]
                        ins = t.matmul(dst_, qv[:, qs * 128:(qs + 1) * 128],
                                       kv_[:, kq * TC:kq * TC + n], start=True, stop=True,
                                       skip_group_check=True)
                    M["sm_mm"] = pe.inc(ins)
                    stq_prev[head] = pe.n
                TB(tsm)
                mm_m = M["sm_mm"]

                def vsm(v, qs=qs, live=live, use_pw=use_pw, nq=nq, mm_m=mm_m):
                    W(v, "v", pe, mm_m)
                    if qs == 0:
                        W(v, "v", pe, M.get("mxT"))
                    if use_pw:
                        ins = v.reduce_max(maxs_t[:, qs:qs + 1], pw[:, 0:live], axis=AX.X)
                        v.drain()
                    else:
                        for kq in range(nq):
                            n = min(TC, live - kq * TC)
                            v.reduce_max(mxc[:, kq:kq + 1], pb[kq][:, 0:n], axis=AX.X)
                        v.drain()
                        if nq == 1:
                            ins = v.tensor_copy(maxs_t[:, qs:qs + 1], mxc[:, 0:1])
                        elif nq == 2:
                            ins = v.tensor_max(maxs_t[:, qs:qs + 1], mxc[:, 0:1], mxc[:, 1:2])
                        elif nq == 3:
                            v.tensor_max(mxc[:, 4:5], mxc[:, 0:1], mxc[:, 1:2])
                            v.drain()
                            ins = v.tensor_max(maxs_t[:, qs:qs + 1], mxc[:, 4:5], mxc[:, 2:3])
                        else:
                            v.tensor_max(mxc[:, 4:5], mxc[:, 0:1], mxc[:, 1:2])
                            v.tensor_max(mxc[:, 5:6], mxc[:, 2:3], mxc[:, 3:4])
                            v.drain()
                            ins = v.tensor_max(maxs_t[:, qs:qs + 1], mxc[:, 4:5], mxc[:, 5:6])
                    red[qs] = dv.inc(ins)
                VB(vsm)
                for b in banks[:nq]:
                    bankfree[b] = (dv, red[qs])
            max_mark = dv.n

            def tmx(t, max_mark=max_mark):
                W(t, "t", dv, max_mark)
                if bankfree[0] is not None:
                    W(t, "t", *bankfree[0])
                ins = t.transpose(pb[0][0:16, 0:128], maxs_t[:, 0:16], ident[:, :])
                M["mxT"] = pe.inc(ins)
            TB(tmx)

            def vmx(v, head=head):
                W(v, "v", pe, M["mxT"])
                WM(v, "v", nm_dram.get(head - 1))
                ins = v.tensor_scalar_mul(mT_t[0:16, 0:128], pb[0][0:16, 0:128], -1.0)
                M["ngT"] = dv.inc(ins)
            VB(vmx)
            bankfree[0] = (dv, M["ngT"])

            def ymx(y, head=head):
                W(y, "y", dv, M["ngT"])
                qx.pre(y, "y")
                ins = y.dma_start(out=max_sp[head * NSUB:(head + 1) * NSUB, :],
                                  in_=mT_t[0:16, 0:128])
                nm_dram[head] = qx.fin(ins)
            YB(ymx)

            def gnm(g, head=head):
                WM(g, "g", nm_dram[head])
                W(g, "g", ac, exp_done.get(head - 2))
                qw.pre(g, "g")
                ins = g.dma_start(out=r1_nm[0:1, head % 2, :],
                                  in_=max_sp[head * NSUB:(head + 1) * NSUB, :]
                                  .rearrange("s p -> (s p)").unsqueeze(0))
                M["nm_ld"] = qw.fin(ins)
            GB(gnm)
            nm_m = M["nm_ld"]

            for qc in range(NTC):
                q0 = qc * TC
                kmax = 4 * qc + 4
                exp_m = {}
                for g0 in range(0, kmax, 2):
                    pair = (g0 // 2) % 2
                    for kci in range(2):
                        kc = g0 + kci
                        bank = 4 + 2 * pair + kci

                        def tst(t, kc=kc, q0=q0, bank=bank, nm_m=nm_m, head=head):
                            WM(t, "t", nm_m)
                            if bankfree[bank] is not None:
                                W(t, "t", *bankfree[bank])
                            t.matmul(pwb[bank - 4][:, :], kv_[:, kc * 128:(kc + 1) * 128],
                                     qv[:, q0:q0 + TC], start=True, stop=False,
                                     skip_group_check=True)
                            ins = t.matmul(pwb[bank - 4][:, :], ones128b[0:1, :],
                                           r1_nm[0:1, head % 2, q0:q0 + TC],
                                           start=False, stop=True, skip_group_check=True)
                            M["st_mm"] = pe.inc(ins)
                            stq_prev[head] = pe.n
                        TB(tst)
                    st_m = M["st_mm"]

                    def sexg(s, g0=g0, pair=pair, st_m=st_m, head=head):
                        W(s, "s", pe, st_m)
                        W(s, "s", pe, av_prev.get(head, av_prev.get(head - 1)))
                        if M.get("dmask") is not None:
                            W(s, "s", dv, M["dmask"])
                        ins = s.activation(
                            r1_pT[:, g0:g0 + 2, :],
                            pw[:, pair * 1024:(pair + 1) * 1024]
                            .rearrange("p (c t) -> p c t", c=2),
                            AF.Exp)
                        M["expg"] = ac.inc(ins)
                    SB_(sexg)
                    for kci in range(2):
                        bankfree[4 + 2 * pair + kci] = (ac, M["expg"])
                    if g0 < 4 * qc:
                        for kci in range(2):
                            exp_m[g0 + kci] = (ac, M["expg"])
                    else:
                        pos0 = g0 - 4 * qc

                        def vms(v, g0=g0, pos0=pos0, head=head):
                            W(v, "v", ac, M["expg"])
                            ins = v.tensor_mul(r1_pT[:, g0:g0 + 2, :],
                                               r1_pT[:, g0:g0 + 2, :],
                                               masks[:, pos0:pos0 + 2, :])
                            M["dmask"] = dv.inc(ins)
                        VB(vms)
                        for kci in range(2):
                            exp_m[g0 + kci] = (dv, M["dmask"])
                exp_done[head] = ac.n

                def tsum(t, qc=qc, kmax=kmax, head=head, vg_mark=vg_mark):
                    if bankfree[1] is not None:
                        W(t, "t", *bankfree[1])
                    for kc in range(kmax):
                        W(t, "t", exp_m[kc][0], exp_m[kc][1])
                        ins = t.matmul(pb[1][0:1, :], ones_col[:, 0:1], r1_pT[:, kc, :],
                                       start=(kc == 0), stop=(kc == kmax - 1),
                                       skip_group_check=True)
                    M["sum_mm"] = pe.inc(ins)
                    bz = 2 + (qc % 2)
                    if bankfree[bz] is not None:
                        W(t, "t", *bankfree[bz])
                    W(t, "t", ac, vg_mark)
                    for kc in range(kmax):
                        ins = t.matmul(pb[bz][:, :],
                                       r1_vg[:, kc, (head % 4) * 128:(head % 4 + 1) * 128],
                                       r1_pT[:, kc, :], start=(kc == 0),
                                       stop=(kc == kmax - 1), skip_group_check=True)
                    M["av_mm"] = pe.inc(ins)
                    av_prev[head] = pe.n
                    vg_av[head // 4] = pe.n
                TB(tsum)
                sum_m, av_m = M["sum_mm"], M["av_mm"]

                def vsum(v, sum_m=sum_m):
                    W(v, "v", pe, sum_m)
                    ins = v.reciprocal(srow2[0:1, :], pb[1][0:1, :])
                    M["rc"] = dv.inc(ins)
                VB(vsum)
                bankfree[1] = (dv, M["rc"])

                def trb(t):
                    if bankfree[0] is not None:
                        W(t, "t", *bankfree[0])
                    W(t, "t", dv, M["rc"])
                    ins = t.matmul(pb[0][:, :], ones128f[0:1, :], srow2[0:1, :],
                                   start=True, stop=True, skip_group_check=True)
                    M["rb_mm"] = pe.inc(ins)
                TB(trb)
                rb_m = M["rb_mm"]

                def vz(v, qc=qc, rb_m=rb_m, av_m=av_m):
                    W(v, "v", pe, max(rb_m, av_m))
                    WM(v, "v", M.get(("z_st", qc % 2)))
                    v.tensor_copy(rb_t[:, :], pb[0][:, :])
                    v.drain()
                    ins = v.tensor_mul(stg[:, qc % 2, :], pb[2 + (qc % 2)][:, :], rb_t[:, :])
                    M["z_ev"] = dv.inc(ins)
                VB(vz)
                bankfree[0] = (dv, M["z_ev"])
                bankfree[2 + (qc % 2)] = (dv, M["z_ev"])
                z_m = M["z_ev"]

                def yz(y, head=head, qc=qc, q0=q0, z_m=z_m):
                    W(y, "y", dv, z_m)
                    qx.pre(y, "y")
                    ins = y.dma_start(out=z_sp[head * 128:(head + 1) * 128, q0:q0 + TC],
                                      in_=stg[:, qc % 2, :])
                    M[("z_st", qc % 2)] = qx.fin(ins)
                YB(yz)

        return out_proj(wob_tl, z_sp, src_old, dst, qx.snap(), NH, bov,
                        stats=stats, xo_guard=(pe, pe.n), p2=p2, cache_tag="wob")

    setup()
    hm = layernorm(xT_in, None)
    if n_blocks > 1:
        st_a = EpiStats(0)
        p2a = LNP2(x_res, st_a, {}, None)
    else:
        st_a = p2a = None
    sm, nxm = attn_a(hm, xT_in, x_res if n_blocks > 1 else xT_out,
                     stats=st_a, p2=p2a)
    if n_blocks >= 2:
        hm = p2a.emit_all()
        st_f = EpiStats(1) if n_blocks > 2 else None
        sm, nxm = ffn(hm, x_res, x_res if n_blocks > 2 else xT_out, stats=st_f)
        ffn_guard = (pe, pe.n)
    if n_blocks >= 3:
        p2f = LNP2(x_res, st_f, nxm, lambda tc: ffn_guard)
        hm = p2f.emit_all()
        if n_blocks > 3:
            st_b = EpiStats(0)
            p2b = LNP2(x_res, st_b, {}, lambda tc: M[("hfree_b", tc)])
        else:
            st_b = p2b = None
        sm, nxm = attn_b(hm, x_res, x_res if n_blocks > 3 else xT_out,
                         stats=st_b, p2=p2b)
    if n_blocks >= 4:
        hm = p2b.emit_all()
        sm, nxm = ffn(hm, x_res, xT_out)

    def yfin(y):
        WM(y, "y", qx.snap())
        WM(y, "y", qw.snap())
    YB(yfin)

    es.close()
    return nc


def _tile_cols(Wt, nfc):
    K = Wt.shape[0]
    c = K // 128
    R = Wt.reshape(c, 128, nfc, 128)
    return np.ascontiguousarray(R.transpose(2, 1, 0, 3).reshape(nfc, 128, K))


def _prep_inputs(inputs):
    f = np.ascontiguousarray
    ws = {}
    ws["wr_tl"] = _tile_cols(inputs["L0_attn_a_Wr"].T, HC)
    ws["wk_tl"] = _tile_cols(inputs["L0_attn_a_Wk"].T, HC)
    ws["wv_tl"] = _tile_cols(inputs["L0_attn_a_Wv"].T, HC)
    ws["wo_tl"] = _tile_cols(inputs["L0_attn_a_Wo"].T, HC)
    ws["td_pt"] = f(inputs["L0_attn_a_time_decay"].T)
    ws["tf_pt"] = f(inputs["L0_attn_a_time_first"].T)
    ws["tmk_pt"] = f(inputs["L0_attn_a_time_mix_k"].reshape(16, 128).T)
    ws["tmv_pt"] = f(inputs["L0_attn_a_time_mix_v"].reshape(16, 128).T)
    ws["tmr_pt"] = f(inputs["L0_attn_a_time_mix_r"].reshape(16, 128).T)
    wq = inputs["L0_attn_b_W_Q"]
    ws["wqh_tl"] = f(wq.reshape(NH, HC, 128, DH).transpose(0, 2, 1, 3)
                     .reshape(NH, 128, H))
    wk = inputs["L0_attn_b_W_K"]
    ws["wkh_tl"] = f(wk.reshape(NH, HC, 128, DH).transpose(0, 2, 1, 3)
                     .reshape(NH, 128, H))
    ws["wvb_tl"] = _tile_cols(inputs["L0_attn_b_W_V"].transpose(1, 0, 2)
                              .reshape(H, H), HC)
    ws["wob_tl"] = _tile_cols(inputs["L0_attn_b_W_O"].reshape(H, H), HC)
    ws["bq_r"] = f(inputs["L0_attn_b_b_Q"].reshape(1, H))
    ws["bk_r"] = f(inputs["L0_attn_b_b_K"].reshape(1, H))
    ws["bv_r"] = f(inputs["L0_attn_b_b_V"].reshape(1, H))
    ws["bo_r"] = f(inputs["L0_attn_b_b_O"].reshape(1, H))
    ws["gw_T"] = f(inputs["L0_ffn_gate_weight"].T)
    gate_wT = inputs["L0_ffn_gate_w"].transpose(0, 2, 1)
    up_wT = inputs["L0_ffn_up_w"].transpose(0, 2, 1)
    ws["gate_tl"] = np.concatenate([_tile_cols(gate_wT[n], FC) for n in range(NE)])
    ws["up_tl"] = np.concatenate([_tile_cols(up_wT[n], FC) for n in range(NE)])
    ws["shg_tl"] = _tile_cols(inputs["L0_ffn_sh_gate_w"].T, SFC)
    ws["shu_tl"] = _tile_cols(inputs["L0_ffn_sh_up_w"].T, SFC)
    D = np.concatenate([inputs["L0_ffn_down_w"].transpose(0, 2, 1).reshape(NE * FI, H),
                        inputs["L0_ffn_sh_down_w"].T], axis=0)
    R = D.reshape(NKC, 128, HC, 128).transpose(2, 0, 1, 3)
    Rp = np.zeros((HC, 112, 128, 128), dtype=np.float32)
    Rp[:, :NKC] = R
    ws["dwn_tl"] = f(Rp.reshape(HC, 7, 16, 128, 128).transpose(0, 1, 3, 2, 4)
                     .reshape(HC * 7, 128, H))
    ws["ident_in"] = np.eye(128, dtype=np.float32)
    m = np.zeros((4, 128, TC), dtype=np.float32)
    for p in range(4):
        kk = p * 128 + np.arange(128)[:, None]
        qq = np.arange(TC)[None, :]
        m[p] = (kk <= qq).astype(np.float32)
    ws["masks_in"] = f(m.transpose(1, 0, 2).reshape(128, 4 * TC))
    return ws


def kernel(**inputs):
    n_blocks = int(inputs.pop("_n_blocks", 4))
    ws = _prep_inputs(inputs)
    x = inputs["L0_attn_a_hidden_states"]
    nc = build_program(n_blocks)
    in_maps = []
    for b in range(8):
        m = dict(ws)
        m["xT_in"] = np.ascontiguousarray(x[b].T)
        in_maps.append(m)
    res = run_bass_kernel_spmd(nc, in_maps, core_ids=list(range(8)))
    out = np.stack([np.ascontiguousarray(r["xT_out"].T) for r in res.results])
    return out


# version 18
# speedup vs baseline: 1.1065x; 1.0752x over previous
import numpy as np
from contextlib import ExitStack

import concourse.bass as bass
DMALOG = None
qx_ref = [None]
import concourse.mybir as mybir
from concourse.bass_utils import run_bass_kernel_spmd

F32 = mybir.dt.float32
BF16 = mybir.dt.bfloat16
AF = mybir.ActivationFunctionType
ALU = mybir.AluOpType
AX = mybir.AxisListType

H = 2048
T = 2048
NH = 16
DH = 128
NE = 8
FI = 1408
FS = 2816
HC = 16
TC = 512
NTC = 4
NSUB = 16
FC = 11
SFC = 22
NKC = NE * FC + SFC
EPS = 1e-5
ISQD = float(1.0 / np.sqrt(128.0))


def ceil_div(a, b):
    return (a + b - 1) // b


def build_program(n_blocks=4, debug=False):
    nc = bass.Bass("TRN2", target_bir_lowering=False)
    es = ExitStack()

    def din(name, shape, dtype=F32):
        return nc.dram_tensor(name, shape, dtype, kind="ExternalInput").ap()

    xT_in = din("xT_in", [H, T])
    wr_tl = din("wr_tl", [HC, 128, H]); wk_tl = din("wk_tl", [HC, 128, H])
    wv_tl = din("wv_tl", [HC, 128, H]); wo_tl = din("wo_tl", [HC, 128, H])
    td_pt = din("td_pt", [128, NH]); tf_pt = din("tf_pt", [128, NH])
    tmk_pt = din("tmk_pt", [128, NH]); tmv_pt = din("tmv_pt", [128, NH])
    tmr_pt = din("tmr_pt", [128, NH])
    wqh_tl = din("wqh_tl", [NH, 128, H]); wkh_tl = din("wkh_tl", [NH, 128, H])
    wvb_tl = din("wvb_tl", [HC, 128, H]); wob_tl = din("wob_tl", [HC, 128, H])
    bq_r = din("bq_r", [1, H]); bk_r = din("bk_r", [1, H])
    bv_r = din("bv_r", [1, H]); bo_r = din("bo_r", [1, H])
    gw_T = din("gw_T", [H, NE])
    gate_tl = din("gate_tl", [NE * FC, 128, H]); up_tl = din("up_tl", [NE * FC, 128, H])
    shg_tl = din("shg_tl", [SFC, 128, H]); shu_tl = din("shu_tl", [SFC, 128, H])
    dwn_tl = din("dwn_tl", [HC * 7, 128, H])
    ident_in = din("ident_in", [128, 128])
    masks_in = din("masks_in", [128, 4 * TC])

    xT_out = nc.dram_tensor("xT_out", [H, T], F32, kind="ExternalOutput").ap()

    kscr = "ExternalOutput" if debug else "Internal"
    x_res = nc.dram_tensor("x_res", [H, T], F32, kind="Internal").ap()
    kv_sp = nc.dram_tensor("kv_sp", [H, T], BF16, kind=kscr).ap()
    r_sp = nc.dram_tensor("r_sp", [H, T], BF16, kind=kscr).ap()
    wkv_sp = nc.dram_tensor("wkv_sp", [H, T], BF16, kind=kscr).ap()
    act_sp = nc.dram_tensor("act_sp", [NKC * 128, T], BF16, kind=kscr).ap()
    z_sp = nc.dram_tensor("z_sp", [H, T], BF16, kind=kscr).ap()
    h_dump = (nc.dram_tensor("h_dump", [4, H, T], BF16, kind="ExternalOutput").ap()
              if debug else None)
    M_dbg = {"ln": 0}
    wrow_sp = nc.dram_tensor("wrow_sp", [128, 128], F32, kind=kscr).ap()
    max_sp = nc.dram_tensor("max_sp", [NH * NSUB, 128], F32, kind="Internal").ap()
    dwb_sp = nc.dram_tensor("dwb_sp", [HC * 7, 128, 16 * 128], BF16,
                            kind="Internal").ap()
    wbc_sp = nc.dram_tensor("wbc_sp", [80, 128, 16 * 128], BF16,
                            kind="Internal").ap()

    def sb(name, shape, dt):
        return es.enter_context(nc.sbuf_tensor(name, shape, dt))

    h_full = sb("h_full", [128, HC, T], BF16)
    r1 = sb("r1", [128, 32768], BF16)
    r1_d = r1[:, 0:8192].rearrange("p (c t) -> p c t", c=HC)
    r1_mix = [r1[:, 8192:16384].rearrange("p (c t) -> p c t", c=HC),
              r1[:, 16384:24576].rearrange("p (c t) -> p c t", c=HC)]
    r1f = r1[:, 0:16384].bitcast(F32)
    r1_tbuf = r1f[:, 0:2049]
    r1_stmp = r1f[:, 2049:4097]
    r1_kvld = r1[:, 16384:20480].rearrange("p (b t) -> p b t", b=2)
    r1_rld = r1[:, 20480:24576].rearrange("p (b t) -> p b t", b=2)
    r1_pT = r1[:, 0:8192].rearrange("p (c t) -> p c t", c=16)
    r1_vg = r1[:, 8192:16384].rearrange("p (c t) -> p c t", c=16)
    r1_q = r1[:, 16384:18432]
    r1_k = r1[:, 18432:20480]
    r1_nm = r1[:, 20480:24576].rearrange("p (b t) -> p b t", b=2)
    r1_bias = r1[:, 24576:32768].rearrange("p (b t) -> p b t", b=4)
    r1_zb = r1[:, 0:8192].rearrange("p (c t) -> p c t", c=16)

    NSLOT = 4
    ring = [sb(f"ring{i}", [128, 16, 128], BF16) for i in range(NSLOT)]
    ones_col = sb("ones_col", [128, 1], BF16)
    ones_row = sb("ones_row", [1, TC], BF16)
    ones128b = sb("ones128b", [1, 128], BF16)
    ones128f = sb("ones128f", [1, 128], F32)
    ident = sb("ident", [128, 128], F32)
    masks = sb("masks", [128, 4, TC], BF16)
    tmk = sb("tmk", [128, NH], F32)
    tmv = sb("tmv", [128, NH], F32)
    tmr = sb("tmr", [128, NH], F32)
    td = sb("td", [128, NH], F32)
    w_dec = sb("w_dec", [128, NH], F32)
    u_fst = sb("u_fst", [128, NH], F32)
    e_tmp = sb("e_tmp", [128, NH], F32)
    gwt8 = sb("gwt8", [128, HC, NE], BF16)
    xt = sb("xt", [128, 2, TC], F32)
    xbf = sb("xbf", [128, 2, TC], BF16)
    xsq = sb("xsq", [128, 2, TC], BF16)
    lnt = sb("lnt", [128, TC], F32)
    rowA = sb("rowA", [1, TC], F32)
    rowB = sb("rowB", [1, TC], F32)
    rowC = sb("rowC", [1, TC], F32)
    rowD = sb("rowD", [1, TC], F32)
    nmrow = sb("nmrow", [1, NTC, TC], BF16)
    rsrow = sb("rsrow", [1, NTC, TC], BF16)
    xo = sb("xo", [128, 2, TC], F32)
    nx = sb("nx", [128, 2, TC], F32)
    sg = sb("sg", [128, 2, TC], BF16)
    gu = sb("gu", [128, 2, TC], BF16)
    stg = sb("stg", [128, 2, TC], BF16)
    wbv = sb("wbv", [128, NTC, TC], BF16)
    wrow = sb("wrow", [1, T], BF16)
    lgts = sb("lgts", [128, NSUB, NE], F32)
    rt_a = sb("rt_a", [128, NSUB, NE], F32)
    rt_b = sb("rt_b", [128, NSUB, NE], F32)
    rt_c = sb("rt_c", [128, NSUB, NE], F32)
    wmat = sb("wmat", [128, NSUB, NE], F32)
    rt_g = sb("rt_g", [128, NSUB, 4], F32)
    rt_h = sb("rt_h", [128, NSUB, 4], F32)
    rt_r1 = sb("rt_r1", [128, NSUB], F32)
    rt_r2 = sb("rt_r2", [128, NSUB], F32)
    rt_s1 = sb("rt_s1", [128, NSUB, 1], F32)
    rt_s2 = sb("rt_s2", [128, NSUB, 1], F32)
    rt_s3 = sb("rt_s3", [128, NSUB, 1], F32)
    rt_u = sb("rt_u", [128, NSUB, 4], F32)
    mxc = sb("mxc", [128, 8], F32)
    srow2 = rowD
    diag_t = sb("diag_t", [128, TC], F32)
    rb_t = sb("rb_t", [128, TC], F32)
    maxs_t = sb("maxs_t", [128, NSUB], F32)
    mT_t = sb("mT_t", [16, 128], F32)

    pb = [es.enter_context(nc.psum_tensor(f"pb{i}", [128, TC], F32)) for i in range(4)]
    pw = es.enter_context(nc.psum_tensor("pw", [128, 4 * TC], F32))
    pwb = [pw[:, i * TC:(i + 1) * TC] for i in range(4)]

    blk = es.enter_context(nc.Block())

    class Sem:
        def __init__(self, name):
            self.h = es.enter_context(nc.semaphore(name))
            self.n = 0

        def inc(self, ins, k=1):
            ins.then_inc(self.h, k)
            self.n += k
            return self.n

    pe, dv, ac = Sem("pe"), Sem("dv"), Sem("ac")
    lastwait = {}

    def W(eng, ename, sem, val):
        if val is None or val <= 0:
            return
        key = (ename, sem.h)
        if lastwait.get(key, -1) >= val:
            return
        eng.wait_ge(sem.h, val)
        lastwait[key] = val

    def WM(eng, ename, mark):
        if mark is None:
            return
        if isinstance(mark, tuple):
            W(eng, ename, mark[0], mark[1])
            return
        for m in mark:
            WM(eng, ename, m)

    class Pool:

        def __init__(self, name, n):
            self.sems = [Sem(f"{name}{i}") for i in range(n)]
            self.i = 0

        def pre(self, eng, ename):
            sm = self.sems[self.i]
            if sm.n:
                W(eng, ename, sm, sm.n)

        def fin(self, ins):
            sm = self.sems[self.i]
            idx = self.i
            self.i = (self.i + 1) % len(self.sems)
            sm.inc(ins, 16)
            import traceback
            if DMALOG is not None:
                fr = traceback.extract_stack()[-2]
                DMALOG.append((self.sems is qx_ref[0] and "qx" or "qw", idx, sm.n,
                               f"{fr.name}:{fr.lineno}"))
            return (sm, sm.n)

        def snap(self):
            return [(sm, sm.n) for sm in self.sems if sm.n]

    qw = Pool("qw", 12)
    qx = Pool("qx", 24)
    qx_ref[0] = qx.sems

    bankfree = {i: None for i in range(8)}

    def bank_ap(i):
        return pb[i][:, :] if i < 4 else pwb[i - 4][:, :]

    def TB(f): blk.tensor(f)
    def VB(f): blk.vector(f)
    def SB_(f): blk.scalar(f)
    def GB(f): blk.gpsimd(f)
    def YB(f): blk.sync(f)

    M = {}

    ring_state = [None] * NSLOT
    ring_next = [0]

    def ring_load(segs):
        s = ring_next[0]
        ring_next[0] = (s + 1) % NSLOT
        hold = []

        def g(g):
            WM(g, "g", ring_state[s])
            off = 0
            for ap, rows in segs:
                qw.pre(g, "g")
                ins = g.dma_start(out=ring[s][:, off // 128:(off + rows) // 128, :],
                                  in_=ap.rearrange("(c p) n -> p c n", p=128))
                if DMALOG is not None:
                    import traceback
                    DMALOG.append(("ringslot", s, traceback.extract_stack()[-4].name,
                                   traceback.extract_stack()[-4].lineno))
                hold.append(qw.fin(ins))
                off += rows
        GB(g)
        return s, hold

    def ring_load_flat(src2d, ncols, extra_wait):
        s = ring_next[0]
        ring_next[0] = (s + 1) % NSLOT
        hold = []

        def g(g):
            WM(g, "g", ring_state[s])
            WM(g, "g", extra_wait)
            qw.pre(g, "g")
            ins = g.dma_start(
                out=ring[s][:, 0:ncols // 128, :].rearrange("p c n -> p (c n)"),
                in_=src2d[:, 0:ncols])
            hold.append(qw.fin(ins))
        GB(g)
        return s, hold

    def ring_used(s, sem, val):
        ring_state[s] = [(sem, val)]

    def ring_also_used(s, mark):
        ring_state[s].append(mark)

    wbc_next = [0]
    wbc_blk = {}

    def ring_load_cached(key, src2d, ncols):
        if key in wbc_blk:
            blk_i, mk = wbc_blk[key]
            slot, rdy = ring_load_flat(wbc_sp[blk_i], ncols, mk)
            return slot, rdy, None
        blk_i = wbc_next[0]
        wbc_next[0] += 1
        slot, rdy = ring_load_flat(src2d, ncols, None)

        def sst(s_, slot=slot, rdy=list(rdy), blk_i=blk_i, ncols=ncols):
            WM(s_, "s", rdy)
            qx.pre(s_, "s")
            ins = s_.dma_start(
                out=wbc_sp[blk_i][:, 0:ncols],
                in_=ring[slot][:, 0:ncols // 128, :].rearrange("p c n -> p (c n)"))
            wbc_blk[key] = (blk_i, qx.fin(ins))
        SB_(sst)
        return slot, rdy, wbc_blk[key][1]

    class EpiStats:

        def __init__(self, par):
            self.par = par
            self.chain = []
            self.sq_used = {}
            self.cp_used = {}
            self.st2 = {}
            self.cp_m = {}
            self.sq_m = {}

        def nx_guard(self, eng, ename, tc, hc):
            v = self.sq_used.get((tc, hc - 2), self.sq_used.get((tc - 1, hc - 2 + HC)))
            W(eng, ename, ac, v)

        def after_vev(self, tc, hc, nx_dv, src_ap=None):
            ap = nx[:, hc % 2, :] if src_ap is None else src_ap

            def sqr(s, tc=tc, hc=hc, nx_dv=nx_dv, ap=ap):
                W(s, "s", dv, nx_dv)
                W(s, "s", pe, self.st2.get(hc - 2))
                ins = s.activation(xsq[:, hc % 2, :], ap, AF.Square)
                self.sq_m[hc] = ac.inc(ins)
                self.sq_used[(tc, hc)] = self.sq_m[hc]
            SB_(sqr)

            def cpy(v, tc=tc, hc=hc, ap=ap, nx_dv=nx_dv):
                W(v, "v", pe, self.st2.get(hc - 2))
                W(v, "v", dv, nx_dv)
                ins = v.tensor_copy(xbf[:, hc % 2, :], ap)
                self.cp_m[hc] = dv.inc(ins)
                self.cp_used[(tc, hc)] = self.cp_m[hc]
            VB(cpy)

        def stat_mm(self, tc, hc):
            def tst2(t, tc=tc, hc=hc):
                if hc == 0:
                    for b2 in (0, 1):
                        if bankfree[b2] is not None:
                            W(t, "t", *bankfree[b2])
                W(t, "t", dv, self.cp_m[hc])
                t.matmul(pb[0][0:1, :], ones_col[:, 0:1], xbf[:, hc % 2, :],
                         start=(hc == 0), stop=(hc == HC - 1), skip_group_check=True)
                W(t, "t", ac, self.sq_m[hc])
                ins = t.matmul(pb[1][0:1, :], ones_col[:, 0:1], xsq[:, hc % 2, :],
                               start=(hc == 0), stop=(hc == HC - 1), skip_group_check=True)
                self.st2[hc] = pe.inc(ins)
            TB(tst2)

        def after_mm(self, tc, hc):
            if hc >= 2:
                self.stat_mm(tc, hc - 2)

        def end_tc(self, tc):
            self.stat_mm(tc, HC - 2)
            self.stat_mm(tc, HC - 1)
            stat_mark = self.st2[HC - 1]
            par = self.par

            def v2(v, tc=tc, stat_mark=stat_mark):
                W(v, "v", pe, stat_mark)
                WM(v, "v", M.get(("nm_used", tc)))
                with nc.allow_low_precision(reason="LN stats rows stored bf16"):
                    v.tensor_scalar_mul(nmrow[0:1, tc, :], pb[0][0:1, :], -1.0 / H)
                v.tensor_scalar_mul(rowB[0:1, :], pb[1][0:1, :], 1.0 / H)
                v.drain()
                v.tensor_mul(rowC[0:1, :], nmrow[0:1, tc, :],
                             nmrow[0:1, tc, :])
                v.drain()
                v.tensor_sub(rowD[0:1, :], rowB[0:1, :], rowC[0:1, :])
                v.drain()
                ins = v.tensor_scalar_add(rowB[0:1, :], rowD[0:1, :], EPS)
                M["ep_var"] = dv.inc(ins)
            VB(v2)
            bankfree[0] = (dv, M["ep_var"])
            bankfree[1] = (dv, M["ep_var"])

            def s2(s):
                W(s, "s", dv, M["ep_var"])
                ins = s.activation(rowC[0:1, :], rowB[0:1, :], AF.Sqrt)
                M["ep_sd"] = ac.inc(ins)
            SB_(s2)

            def v3(v, tc=tc):
                W(v, "v", ac, M["ep_sd"])
                with nc.allow_low_precision(reason="LN stats rows stored bf16"):
                    ins = v.reciprocal(rsrow[0:1, tc, :], rowC[0:1, :])
                M["ep_rstd"] = dv.inc(ins)
            VB(v3)
            self.chain.append((dv, M["ep_rstd"]))
            self.st2 = {}
            self.cp_m = {}
            self.sq_m = {}

    def setup():
        ldm = []

        def g(g):
            for dst, src in ((tmk, tmk_pt), (tmv, tmv_pt), (tmr, tmr_pt),
                             (td, td_pt), (u_fst, tf_pt), (ident, ident_in)):
                qw.pre(g, "g")
                ins = g.dma_start(out=dst[:], in_=src[:, :])
                ldm.append(qw.fin(ins))
            qw.pre(g, "g")
            ins = g.dma_start(out=masks[:], in_=masks_in.rearrange("p (m t) -> p m t", m=4))
            ldm.append(qw.fin(ins))
            qw.pre(g, "g")
            ins = g.dma_start(out=gwt8[:], in_=gw_T.rearrange("(c p) e -> p c e", p=128))
            ldm.append(qw.fin(ins))
        GB(g)
        ld = ldm

        def v(v):
            v.memset(ones_col[:], 1.0)
            v.memset(ones_row[0:1, :], 1.0)
            v.memset(ones128b[0:1, :], 1.0)
            ins = v.memset(ones128f[0:1, :], 1.0)
            dv.inc(ins)
        VB(v)

        def s(s):
            WM(s, "s", ld)
            s.activation(e_tmp[:], td[:], AF.Exp)
            s.drain()
            ins = s.activation(w_dec[:], e_tmp[:], AF.Exp, scale=-1.0)
            ac.inc(ins)
        SB_(s)

    def layernorm(src, prev_marks):
        h_marks = []
        pe_entry = pe.n
        st_m = {}
        prev_hdone = [None]
        for tc in range(NTC):
            t0 = tc * TC
            ld1 = {}
            c_m = {}
            q_m = {}

            def y1(y, tc=tc, t0=t0):
                if prev_marks is not None:
                    WM(y, "y", prev_marks[tc])
                for hc in range(HC):
                    W(y, "y", dv, c_m.get(hc - 2))
                    qx.pre(y, "y")
                    ins = y.dma_start(out=xt[:, hc % 2, :],
                                      in_=src[hc * 128:(hc + 1) * 128, t0:t0 + TC])
                    ld1[hc] = qx.fin(ins)
            for hc in range(HC):
                def yl(y, hc=hc, tc=tc, t0=t0):
                    if hc == 0 and prev_marks is not None:
                        WM(y, "y", prev_marks[tc])
                    if hc < 2 and prev_hdone[0] is not None:
                        W(y, "y", dv, prev_hdone[0][HC - 2 + hc])
                    W(y, "y", dv, c_m.get(hc - 2))
                    qx.pre(y, "y")
                    ins = y.dma_start(out=xt[:, hc % 2, :],
                                      in_=src[hc * 128:(hc + 1) * 128, t0:t0 + TC])
                    ld1[hc] = qx.fin(ins)
                YB(yl)

                def vc(v, hc=hc):
                    WM(v, "v", ld1[hc])
                    W(v, "v", pe, st_m.get(hc - 2))
                    v.tensor_copy(xbf[:, hc % 2, :], xt[:, hc % 2, :])
                    v.drain()
                    ins = v.tensor_mul(xsq[:, hc % 2, :], xt[:, hc % 2, :],
                                       xt[:, hc % 2, :])
                    c_m[hc] = dv.inc(ins)
                    q_m[hc] = c_m[hc]
                VB(vc)

                def tst(t, hc=hc):
                    if hc == 0:
                        for b in (0, 1):
                            if bankfree[b] is not None:
                                W(t, "t", *bankfree[b])
                    W(t, "t", dv, c_m[hc])
                    t.matmul(pb[0][0:1, :], ones_col[:, 0:1], xbf[:, hc % 2, :],
                             start=(hc == 0), stop=(hc == HC - 1), skip_group_check=True)
                    ins = t.matmul(pb[1][0:1, :], ones_col[:, 0:1], xsq[:, hc % 2, :],
                                   start=(hc == 0), stop=(hc == HC - 1), skip_group_check=True)
                    st_m[hc] = pe.inc(ins)
                    if hc == HC - 1:
                        M["stats"] = st_m[hc]
                TB(tst)

            def v2(v):
                W(v, "v", pe, M["stats"])
                v.tensor_scalar_mul(rowA[0:1, :], pb[0][0:1, :], -1.0 / H)
                v.tensor_scalar_mul(rowB[0:1, :], pb[1][0:1, :], 1.0 / H)
                v.drain()
                v.tensor_mul(rowC[0:1, :], rowA[0:1, :], rowA[0:1, :])
                v.drain()
                v.tensor_sub(rowD[0:1, :], rowB[0:1, :], rowC[0:1, :])
                v.drain()
                ins = v.tensor_scalar_add(rowB[0:1, :], rowD[0:1, :], EPS)
                M["var"] = dv.inc(ins)
            VB(v2)
            bankfree[0] = (dv, M["var"])
            bankfree[1] = (dv, M["var"])

            def s2(s):
                W(s, "s", dv, M["var"])
                ins = s.activation(rowC[0:1, :], rowB[0:1, :], AF.Sqrt)
                M["sd"] = ac.inc(ins)
            SB_(s2)

            def v3(v):
                W(v, "v", ac, M["sd"])
                ins = v.reciprocal(rowB[0:1, :], rowC[0:1, :])
                M["rstd"] = dv.inc(ins)
            VB(v3)

            def t2(t):
                for b in (2, 3):
                    if bankfree[b] is not None:
                        W(t, "t", *bankfree[b])
                W(t, "t", dv, M["rstd"])
                t.matmul(pb[2][:, :], ones128f[0:1, :], rowA[0:1, :],
                         start=True, stop=True, skip_group_check=True)
                ins = t.matmul(pb[3][:, :], ones128f[0:1, :], rowB[0:1, :],
                               start=True, stop=True, skip_group_check=True)
                M["bc"] = pe.inc(ins)
            TB(t2)

            ld2 = {}
            hdone = {}
            for hc in range(HC):
                def yl2(y, hc=hc, t0=t0):
                    if hc < 2:
                        W(y, "y", dv, c_m[HC - 2 + hc])
                    else:
                        W(y, "y", dv, hdone.get(hc - 2))
                    qx.pre(y, "y")
                    ins = y.dma_start(out=xt[:, hc % 2, :],
                                      in_=src[hc * 128:(hc + 1) * 128, t0:t0 + TC])
                    ld2[hc] = qx.fin(ins)
                YB(yl2)

                def vh(v, hc=hc, tc=tc, t0=t0):
                    if hc == 0:
                        W(v, "v", pe, max(M["bc"], pe_entry))
                    WM(v, "v", ld2[hc])
                    v.tensor_add(lnt[:, :], xt[:, hc % 2, :], pb[2][:, :])
                    v.drain()
                    ins = v.tensor_mul(h_full[:, hc, t0:t0 + TC], lnt[:, :], pb[3][:, :])
                    v.drain()
                    hdone[hc] = dv.inc(ins)
                VB(vh)
            h_marks.append(dv.n)
            prev_hdone[0] = dict(hdone)
            bankfree[2] = (dv, dv.n)
            bankfree[3] = (dv, dv.n)
        M["xt_last"] = {0: prev_hdone[0][HC - 2], 1: prev_hdone[0][HC - 1]}
        if debug:
            li = M_dbg["ln"]; M_dbg["ln"] += 1

            def ydb(y, li=li, hm=list(h_marks)):
                W(y, "y", dv, hm[-1])
                for hc in range(HC):
                    qx.pre(y, "y")
                    ins = y.dma_start(out=h_dump[li, hc * 128:(hc + 1) * 128, :],
                                      in_=h_full[:, hc, :])
                    qx.fin(ins)
            YB(ydb)
        return h_marks

    class LNP2:

        def __init__(self, src, stats, nx_marks, hfull_guard):
            self.src = src
            self.stats = stats
            self.nx_marks = nx_marks
            self.hfull_guard = hfull_guard
            self.h_marks = []
            self.prev_hdone = None

        def emit_bc(self, tc):
            def t2(t, tc=tc):
                for b in (2, 3):
                    if bankfree[b] is not None:
                        W(t, "t", *bankfree[b])
                WM(t, "t", self.stats.chain[tc])
                t.matmul(pb[2][:, :], ones128b[0:1, :], nmrow[0:1, tc, :],
                         start=True, stop=True, skip_group_check=True)
                ins = t.matmul(pb[3][:, :], ones128b[0:1, :], rsrow[0:1, tc, :],
                               start=True, stop=True, skip_group_check=True)
                M["bc"] = pe.inc(ins)
                self.bc = M["bc"]
                M[("nm_used", tc)] = (pe, pe.n)
            TB(t2)
            self.ld2 = {}
            self.hdone = {}

        def emit_hc(self, tc, hc):
            src = self.src
            hfull_guard = self.hfull_guard
            t0 = tc * TC
            ld2 = self.ld2
            hdone = self.hdone
            bc = self.bc

            def yl2(y, hc=hc, t0=t0, tc=tc):
                WM(y, "y", self.nx_marks.get((tc, hc)))
                if hc < 2:
                    if self.prev_hdone is not None:
                        W(y, "y", dv, self.prev_hdone[HC - 2 + hc])
                    else:
                        W(y, "y", dv, M["xt_last"][hc])
                else:
                    W(y, "y", dv, hdone.get(hc - 2))
                qx.pre(y, "y")
                ins = y.dma_start(out=xt[:, hc % 2, :],
                                  in_=src[hc * 128:(hc + 1) * 128, t0:t0 + TC])
                ld2[hc] = qx.fin(ins)
            YB(yl2)

            def vh(v, hc=hc, tc=tc, t0=t0, bc=bc):
                if hc == 0:
                    W(v, "v", pe, bc)
                    if hfull_guard is not None:
                        WM(v, "v", hfull_guard(tc))
                WM(v, "v", ld2[hc])
                v.tensor_add(lnt[:, :], xt[:, hc % 2, :], pb[2][:, :])
                v.drain()
                ins = v.tensor_mul(h_full[:, hc, t0:t0 + TC], lnt[:, :], pb[3][:, :])
                v.drain()
                hdone[hc] = dv.inc(ins)
            VB(vh)
            if hc == HC - 1:
                self.h_marks.append(dv.n)
                self.prev_hdone = dict(hdone)
                bankfree[2] = (dv, dv.n)
                bankfree[3] = (dv, dv.n)
                if tc == NTC - 1:
                    M["xt_last"] = {0: self.prev_hdone[HC - 2],
                                    1: self.prev_hdone[HC - 1]}

        def emit(self, tc):
            self.emit_bc(tc)
            for hc in range(HC):
                self.emit_hc(tc, hc)

        def emit_all(self):
            for tc in range(NTC):
                self.emit(tc)
            return self.h_marks

    def out_proj(w_T, rhs_sp, src_old, dst, rhs_store_mark, nch, bias_row,
                 stats=None, xo_guard=None, p2=None, cache_tag="w"):
        store_marks = []
        tc_pe_done = {}
        nx_store = {}
        xo_used = {}
        if p2 is not None:
            p2.nx_marks = nx_store
        XON = 8
        xo_r = r1[:, 16384:24576].bitcast(F32)
        zb2 = [r1[:, 0:8192].rearrange("p (c t) -> p c t", c=16),
               r1[:, 8192:16384].rearrange("p (c t) -> p c t", c=16)]
        for tc in range(NTC):
            t0 = tc * TC
            rhs_m = {}
            zb = zb2[tc % 2]

            def yld(y, tc=tc, t0=t0, zb=zb):
                WM(y, "y", rhs_store_mark)
                W(y, "y", pe, tc_pe_done.get(tc - 2))
                for kc in range(nch):
                    qx.pre(y, "y")
                    ins = y.dma_start(out=zb[:, kc, :],
                                      in_=rhs_sp[kc * 128:(kc + 1) * 128, t0:t0 + TC])
                    rhs_m[kc] = qx.fin(ins)
            YB(yld)

            for hc in range(HC):
                slot, rdy = ring_load_flat(w_T[hc], nch * 128, None)
                bank = 4 + (hc % 2)
                xo_m = {}
                xs = hc % XON
                xo_ap = xo_r[:, xs * TC:(xs + 1) * TC]

                def yxo(y, hc=hc, t0=t0, tc=tc, xo_ap=xo_ap):
                    if tc == 0 and hc < XON:
                        WM(y, "y", xo_guard)
                    WM(y, "y", nx_store.get((tc, hc - XON),
                                            nx_store.get((tc - 1, hc - XON + HC))))
                    if stats is not None:
                        W(y, "y", ac, stats.sq_used.get((tc, hc - XON),
                                      stats.sq_used.get((tc - 1, hc - XON + HC))))
                        W(y, "y", dv, stats.cp_used.get((tc, hc - XON),
                                      stats.cp_used.get((tc - 1, hc - XON + HC))))
                    else:
                        W(y, "y", dv, xo_used.get((tc, hc - XON),
                                      xo_used.get((tc - 1, hc - XON + HC))))
                    qx.pre(y, "y")
                    ins = y.dma_start(out=xo_ap,
                                      in_=src_old[hc * 128:(hc + 1) * 128, t0:t0 + TC])
                    xo_m[0] = qx.fin(ins)
                YB(yxo)

                def tmm(t, slot=slot, rdy=rdy, bank=bank, hc=hc, tc=tc, zb=zb):
                    WM(t, "t", rdy)
                    if bankfree[bank] is not None:
                        W(t, "t", *bankfree[bank])
                    for kc in range(nch):
                        WM(t, "t", rhs_m[kc])
                        ins = t.matmul(pwb[bank - 4][:, :], ring[slot][:, kc, :],
                                       zb[:, kc, :], start=(kc == 0),
                                       stop=(kc == nch - 1 and bias_row is None),
                                       skip_group_check=True)
                    if bias_row is not None:
                        ins = t.matmul(pwb[bank - 4][:, :],
                                       bias_row[0:1, hc * 128:(hc + 1) * 128],
                                       ones_row[0:1, :], start=False, stop=True,
                                       skip_group_check=True)
                    M["op_mm"] = pe.inc(ins)
                    tc_pe_done[tc] = pe.n
                TB(tmm)
                ring_used(slot, pe, M["op_mm"])
                mm_mark = M["op_mm"]
                if stats is not None:
                    stats.after_mm(tc, hc)

                def vev(v, bank=bank, hc=hc, tc=tc, mm_mark=mm_mark, xo_ap=xo_ap):
                    W(v, "v", pe, mm_mark)
                    WM(v, "v", xo_m[0])
                    ins = v.tensor_add(xo_ap, pwb[bank - 4][:, :], xo_ap)
                    xo_used[(tc, hc)] = dv.inc(ins)
                VB(vev)
                ev = dv.n
                bankfree[bank] = (dv, ev)
                if stats is not None:
                    stats.after_vev(tc, hc, ev, src_ap=xo_ap)

                def yst(y, hc=hc, t0=t0, tc=tc, ev=ev, xo_ap=xo_ap):
                    W(y, "y", dv, ev)
                    qx.pre(y, "y")
                    ins = y.dma_start(out=dst[hc * 128:(hc + 1) * 128, t0:t0 + TC],
                                      in_=xo_ap)
                    nx_store[(tc, hc)] = qx.fin(ins)
                YB(yst)
            if stats is not None:
                stats.end_tc(tc)
            store_marks.append(qx.snap())
        return store_marks, nx_store

    def attn_a(h_marks, src_old, dst, stats=None, p2=None):
        sg_guard = {}
        kv_marks = {oc: [] for oc in range(HC)}
        r_marks = {oc: [] for oc in range(HC)}
        kvm = {}
        scm = {}
        wkv_st = {}
        hfw = h_full[:].rearrange("p c t -> p (c t)")[:, 0:16384].bitcast(F32)
        sc_tbuf = hfw[:, 0:T + 1]
        sc_stmp = hfw[:, T + 1:2 * T + 1]

        def emit_scan(cc, mix1_guard):
            def ysl(y, cc=cc):
                if cc < 2:
                    W(y, "y", pe, mix1_guard)
                WM(y, "y", kv_marks[cc])
                WM(y, "y", r_marks[cc])
                W(y, "y", dv, scm.get(cc - 2))
                WM(y, "y", wkv_st.get(cc - 2))
                qx.pre(y, "y")
                ins = y.dma_start(out=r1_kvld[:, cc % 2, :],
                                  in_=kv_sp[cc * 128:(cc + 1) * 128, :])
                mka = qx.fin(ins)
                qx.pre(y, "y")
                ins = y.dma_start(out=r1_rld[:, cc % 2, :],
                                  in_=r_sp[cc * 128:(cc + 1) * 128, :])
                kvm[cc] = [mka, qx.fin(ins)]
            YB(ysl)

            def vsc(v, cc=cc):
                WM(v, "v", kvm[cc])
                p = cc % 2
                v.memset(sc_tbuf[:, 0:1], 0.0)
                v.drain()
                v.tensor_tensor_scan(sc_tbuf[:, 1:T + 1],
                                     w_dec[:, cc:cc + 1].broadcast_to([128, T]),
                                     r1_kvld[:, p, :], 0.0, ALU.mult, ALU.add)
                v.drain()
                v.scalar_tensor_tensor(sc_stmp[:, 0:T], r1_kvld[:, p, :],
                                       u_fst[:, cc:cc + 1], sc_tbuf[:, 0:T],
                                       ALU.mult, ALU.add)
                v.drain()
                ins = v.tensor_mul(r1_kvld[:, p, :], sc_stmp[:, 0:T], r1_rld[:, p, :])
                v.drain()
                scm[cc] = dv.inc(ins)
            VB(vsc)

            def ysc(y, cc=cc):
                W(y, "y", dv, scm[cc])
                qx.pre(y, "y")
                ins = y.dma_start(out=wkv_sp[cc * 128:(cc + 1) * 128, :],
                                  in_=r1_kvld[:, cc % 2, :])
                wkv_st[cc] = qx.fin(ins)
            YB(ysc)
        for tc in range(NTC):
            t0 = tc * TC

            def vd(v, tc=tc, t0=t0):
                W(v, "v", dv, h_marks[tc])
                if tc == 0:
                    v.tensor_sub(r1_d[:, :, 1:TC],
                                 h_full[:, :, t0 + 1:t0 + TC], h_full[:, :, t0:t0 + TC - 1])
                    ins = v.tensor_copy(r1_d[:, :, 0:1], h_full[:, :, 0:1])
                else:
                    ins = v.tensor_sub(r1_d[:, :, 0:TC],
                                       h_full[:, :, t0:t0 + TC], h_full[:, :, t0 - 1:t0 + TC - 1])
                v.drain()
                M["aa_d"] = dv.inc(ins)
            VB(vd)

            def build_mix(which, tmx, mb, tc=tc, t0=t0):
                mixbuf = r1_mix[which]

                def vm(v):
                    W(v, "v", dv, M["aa_d"])
                    W(v, "v", pe, M.get(("aa_mix_used", which)))
                    if tc == 0:
                        if bankfree[mb] is not None:
                            W(v, "v", *bankfree[mb])
                        for hc in range(HC):
                            v.tensor_mul(pb[mb][:, :], r1_d[:, hc, :],
                                         tmx[:, hc:hc + 1].broadcast_to([128, TC]))
                            v.drain()
                            v.tensor_add(mixbuf[:, hc, 1:TC], pb[mb][:, 1:TC],
                                         h_full[:, hc, t0:t0 + TC - 1])
                            ins = v.tensor_copy(mixbuf[:, hc, 0:1], pb[mb][:, 0:1])
                            v.drain()
                        M["aa_mix", which] = dv.inc(ins)
                        bankfree[mb] = (dv, M["aa_mix", which])
                    else:
                        for hc in range(HC):
                            ins = v.scalar_tensor_tensor(
                                mixbuf[:, hc, :], r1_d[:, hc, :], tmx[:, hc:hc + 1],
                                h_full[:, hc, t0 - 1:t0 + TC - 1], ALU.mult, ALU.add)
                        v.drain()
                        M["aa_mix", which] = dv.inc(ins)
                VB(vm)
                return M["aa_mix", which]

            def project(mixbuf, slot, rdy, bank, mix_mark, last_of_mix, which):
                def tmm(t):
                    WM(t, "t", rdy)
                    W(t, "t", dv, mix_mark)
                    if bankfree[bank] is not None:
                        W(t, "t", *bankfree[bank])
                    for hc in range(HC):
                        ins = t.matmul(pwb[bank - 4][:, :], ring[slot][:, hc, :],
                                       mixbuf[:, hc, :], start=(hc == 0),
                                       stop=(hc == HC - 1), skip_group_check=True)
                    M["aa_mm"] = pe.inc(ins)
                    if last_of_mix:
                        M[("aa_mix_used", which)] = pe.n
                TB(tmm)
                ring_used(slot, pe, M["aa_mm"])
                return M["aa_mm"]

            mk = build_mix(0, tmk, 2)
            mv = build_mix(1, tmv, 3)
            for oc in range(HC):
                slot_k, rdy_k = ring_load_flat(wk_tl[oc], H, None)
                slot_v, rdy_v = ring_load_flat(wv_tl[oc], H, None)
                bk_ = 4 + (oc % 2)
                bv_ = 6 + (oc % 2)
                mm_k = project(r1_mix[0], slot_k, rdy_k, bk_, mk, oc == HC - 1, 0)

                def se(s, bk_=bk_, mm_k=mm_k, oc=oc):
                    W(s, "s", pe, mm_k)
                    W(s, "s", dv, sg_guard.get(oc % 2))
                    ins = s.activation(sg[:, oc % 2, :], pwb[bk_ - 4][:, :], AF.Copy)
                    M["aa_kev"] = ac.inc(ins)
                SB_(se)
                kev = M["aa_kev"]
                bankfree[bk_] = (ac, kev)

                mm_v = project(r1_mix[1], slot_v, rdy_v, bv_, mv, oc == HC - 1, 1)

                def ve(v, bv_=bv_, mm_v=mm_v, kev=kev, oc=oc):
                    W(v, "v", pe, mm_v)
                    W(v, "v", ac, kev)
                    WM(v, "v", M.get(("aa_stg", oc % 2)))
                    ins = v.tensor_mul(stg[:, oc % 2, :], pwb[bv_ - 4][:, :],
                                       sg[:, oc % 2, :])
                    M["aa_kv"] = dv.inc(ins)
                    sg_guard[oc % 2] = M["aa_kv"]
                VB(ve)
                bankfree[bv_] = (dv, M["aa_kv"])

                def ysp(y, oc=oc, t0=t0, ev=M["aa_kv"], tc=tc):
                    W(y, "y", dv, ev)
                    qx.pre(y, "y")
                    ins = y.dma_start(out=kv_sp[oc * 128:(oc + 1) * 128, t0:t0 + TC],
                                      in_=stg[:, oc % 2, :])
                    M[("aa_stg", oc % 2)] = qx.fin(ins)
                    kv_marks[oc].append(M[("aa_stg", oc % 2)])
                YB(ysp)

            mr = build_mix(0, tmr, 2)
            for oc in range(HC):
                slot_r, rdy_r = ring_load_flat(wr_tl[oc], H, None)
                br_ = 4 + (oc % 2)
                mm_r = project(r1_mix[0], slot_r, rdy_r, br_, mr, oc == HC - 1, 0)

                def se(s, br_=br_, mm_r=mm_r, oc=oc):
                    W(s, "s", pe, mm_r)
                    WM(s, "s", M.get(("aa_stg", oc % 2)))
                    ins = s.activation(stg[:, oc % 2, :], pwb[br_ - 4][:, :], AF.Sigmoid)
                    M["aa_rev"] = ac.inc(ins)
                SB_(se)
                bankfree[br_] = (ac, M["aa_rev"])

                def ysp(y, oc=oc, t0=t0, ev=M["aa_rev"], tc=tc):
                    W(y, "y", ac, ev)
                    qx.pre(y, "y")
                    ins = y.dma_start(out=r_sp[oc * 128:(oc + 1) * 128, t0:t0 + TC],
                                      in_=stg[:, oc % 2, :])
                    M[("aa_stg", oc % 2)] = qx.fin(ins)
                    r_marks[oc].append(M[("aa_stg", oc % 2)])
                YB(ysp)
                if tc == NTC - 1:
                    emit_scan(oc, M[("aa_mix_used", 1)] if oc < 2 else None)

        snap = qx.snap()
        return out_proj(wo_tl, wkv_sp, src_old, dst, snap, HC, None,
                        stats=stats, xo_guard=snap, p2=p2, cache_tag="wo")

    def ffn(h_marks, src_old, dst, stats=None, p2=None):
        lg_cp = {}
        for ts in range(NSUB):
            b = ts % 2

            def tlg(t, ts=ts, b=b):
                W(t, "t", dv, h_marks[ts // 4])
                W(t, "t", dv, lg_cp.get(ts - 2))
                if ts < 2 and bankfree[b] is not None:
                    W(t, "t", *bankfree[b])
                for hc in range(HC):
                    ins = t.matmul(pb[b][0:128, 0:NE],
                                   h_full[:, hc, ts * 128:(ts + 1) * 128],
                                   gwt8[:, hc, :], start=(hc == 0), stop=(hc == HC - 1),
                                   skip_group_check=True)
                M["lg"] = pe.inc(ins)
            TB(tlg)

            def vlg(v, ts=ts, b=b):
                W(v, "v", pe, M["lg"])
                ins = v.tensor_copy(lgts[:, ts, :], pb[b][0:128, 0:NE])
                lg_cp[ts] = dv.inc(ins)
            VB(vlg)
        bankfree[0] = (dv, dv.n)
        bankfree[1] = (dv, dv.n)
        lg_mark = dv.n

        def srt(s):
            W(s, "s", dv, lg_mark)
            ins = s.activation(rt_a[:], lgts[:], AF.Exp)
            M["rt_exp"] = ac.inc(ins)
        SB_(srt)

        def vrt(v):
            W(v, "v", ac, M["rt_exp"])
            v.reduce_sum(rt_r1[:, :], rt_a[:], axis=AX.X)
            v.drain()
            v.reciprocal(rt_r2[:, :], rt_r1[:, :])
            v.drain()
            v.tensor_tensor(rt_b[:], rt_a[:],
                            rt_r2[:, :, None].broadcast_to([128, NSUB, NE]), ALU.mult)
            v.drain()
            v.tensor_tensor(rt_g[:], rt_b[:, :, 0:NE:2], rt_b[:, :, 1:NE:2], ALU.max)
            v.drain()
            g = rt_g
            u = rt_u
            v.tensor_tensor(u[:, :, 0:1], g[:, :, 0:1], g[:, :, 1:2], ALU.min)
            v.tensor_tensor(u[:, :, 1:2], g[:, :, 2:3], g[:, :, 3:4], ALU.min)
            v.tensor_tensor(u[:, :, 2:3], g[:, :, 0:1], g[:, :, 1:2], ALU.max)
            v.tensor_tensor(u[:, :, 3:4], g[:, :, 2:3], g[:, :, 3:4], ALU.max)
            v.drain()
            v.tensor_tensor(rt_s1[:, :, 0:1], u[:, :, 0:1], u[:, :, 1:2], ALU.max)
            v.tensor_tensor(rt_s2[:, :, 0:1], u[:, :, 2:3], u[:, :, 3:4], ALU.min)
            v.drain()
            v.tensor_tensor(rt_s3[:, :, 0:1], rt_s1[:, :, 0:1], rt_s2[:, :, 0:1], ALU.max)
            v.drain()
            v.tensor_tensor(rt_h[:], g[:],
                            rt_s3[:, :, 0:1].broadcast_to([128, NSUB, 4]), ALU.is_ge)
            v.drain()
            v.tensor_tensor(rt_c[:, :, 0:NE:2], rt_b[:, :, 0:NE:2], rt_h[:], ALU.mult)
            v.tensor_tensor(rt_c[:, :, 1:NE:2], rt_b[:, :, 1:NE:2], rt_h[:], ALU.mult)
            v.drain()
            v.reduce_max(rt_r1[:, :], rt_c[:], axis=AX.X)
            v.drain()
            v.tensor_tensor(rt_a[:], rt_c[:],
                            rt_r1[:, :, None].broadcast_to([128, NSUB, NE]), ALU.is_lt)
            v.drain()
            v.tensor_tensor(rt_b[:], rt_c[:], rt_a[:], ALU.mult)
            v.drain()
            v.reduce_max(rt_r2[:, :], rt_b[:], axis=AX.X)
            v.drain()
            v.tensor_tensor(rt_a[:], rt_c[:],
                            rt_r2[:, :, None].broadcast_to([128, NSUB, NE]), ALU.is_ge)
            v.drain()
            ins = v.tensor_tensor(wmat[:], rt_c[:], rt_a[:], ALU.mult)
            M["wm"] = dv.inc(ins)
        VB(vrt)

        def twm(t):
            W(t, "t", dv, M["wm"])
            if bankfree[0] is not None:
                W(t, "t", *bankfree[0])
            ins = t.transpose(pb[0][0:128, 0:128],
                              wmat[:].rearrange("p a b -> p (a b)"), ident[:, :])
            M["wmT"] = pe.inc(ins)
        TB(twm)

        def vwm(v):
            W(v, "v", pe, M["wmT"])
            ins = v.tensor_copy(lnt[:, 0:128], pb[0][0:128, 0:128])
            M["wmTc"] = dv.inc(ins)
        VB(vwm)
        bankfree[0] = (dv, M["wmTc"])

        def ywm(y):
            W(y, "y", dv, M["wmTc"])
            qx.pre(y, "y")
            ins = y.dma_start(out=wrow_sp[:, :], in_=lnt[0:128, 0:128])
            M["wrow_st"] = qx.fin(ins)
        YB(ywm)

        spill_m = {}
        gu_used = {}
        M["wb_used_pe"] = None

        def do_expert(n, gsrc, usrc, nfc, row0, scaled):
            if scaled:
                def gw_(g, n=n):
                    WM(g, "g", M["wrow_st"])
                    W(g, "g", pe, M["wb_used_pe"])
                    qw.pre(g, "g")
                    ins = g.dma_start(out=wrow[0:1, :].rearrange("o (s p) -> o s p", p=128),
                                      in_=wrow_sp[n:128:NE, :])
                    M["wrow_ld"] = qw.fin(ins)
                GB(gw_)

            for tc in range(NTC):
                if not scaled:
                    break

                def twb(t, tc=tc):
                    WM(t, "t", M["wrow_ld"])
                    W(t, "t", dv, M.get("wb_cp"))
                    if bankfree[1] is not None:
                        W(t, "t", *bankfree[1])
                    ins = t.matmul(pb[1][:, :], ones128b[0:1, :],
                                   wrow[0:1, tc * TC:(tc + 1) * TC],
                                   start=True, stop=True, skip_group_check=True)
                    M["wb_mm"] = pe.inc(ins)
                TB(twb)

                def vwb(v, tc=tc):
                    W(v, "v", pe, M["wb_mm"])
                    ins = v.tensor_copy(wbv[:, tc, :], pb[1][:, :])
                    M["wb_cp"] = dv.inc(ins)
                VB(vwb)
                bankfree[1] = (dv, M["wb_cp"])
            wb_mark = M.get("wb_cp", 0) if scaled else 0

            for fc in range(nfc):
                slot_g, rdy_g = ring_load_flat(gsrc[fc], H, None)
                slot_u, rdy_u = ring_load_flat(usrc[fc], H, None)
                for tc in range(NTC):
                    t0 = tc * TC
                    bg = 4 + (tc % 2)
                    bu = 6 + (tc % 2)

                    def tge(t, slot_g=slot_g, rdy_g=rdy_g, slot_u=slot_u, rdy_u=rdy_u,
                            bg=bg, bu=bu, tc=tc, t0=t0):
                        WM(t, "t", rdy_u)
                        W(t, "t", dv, h_marks[tc])
                        if bankfree[bg] is not None:
                            W(t, "t", *bankfree[bg])
                        for hc in range(HC):
                            ins = t.matmul(pwb[bg - 4][:, :], ring[slot_g][:, hc, :],
                                           h_full[:, hc, t0:t0 + TC], start=(hc == 0),
                                           stop=(hc == HC - 1), skip_group_check=True)
                        M["g_mm"] = pe.inc(ins)
                        if bankfree[bu] is not None:
                            W(t, "t", *bankfree[bu])
                        for hc in range(HC):
                            ins = t.matmul(pwb[bu - 4][:, :], ring[slot_u][:, hc, :],
                                           h_full[:, hc, t0:t0 + TC], start=(hc == 0),
                                           stop=(hc == HC - 1), skip_group_check=True)
                        M["u_mm"] = pe.inc(ins)
                    TB(tge)
                    g_mm, u_mm = M["g_mm"], M["u_mm"]

                    def sev(s, bg=bg, tc=tc, g_mm=g_mm):
                        W(s, "s", pe, g_mm)
                        W(s, "s", dv, gu_used.get(("sg", tc % 2)))
                        ins = s.activation(sg[:, tc % 2, :], pwb[bg - 4][:, :], AF.Silu)
                        M["silu"] = ac.inc(ins)
                    SB_(sev)
                    bankfree[bg] = (ac, M["silu"])
                    silu_m = M["silu"]

                    def vev(v, bu=bu, tc=tc, silu_m=silu_m, u_mm=u_mm,
                            wb_mark=wb_mark, scaled=scaled):
                        W(v, "v", ac, silu_m)
                        W(v, "v", pe, u_mm)
                        v.tensor_mul(gu[:, tc % 2, :], pwb[bu - 4][:, :], sg[:, tc % 2, :])
                        v.drain()
                        WM(v, "v", spill_m.get(tc % 2))
                        if scaled:
                            W(v, "v", dv, wb_mark)
                            ins = v.tensor_mul(stg[:, tc % 2, :], gu[:, tc % 2, :],
                                               wbv[:, tc, :])
                        else:
                            ins = v.tensor_copy(stg[:, tc % 2, :], gu[:, tc % 2, :])
                        M["act_ev"] = dv.inc(ins)
                        gu_used[("sg", tc % 2)] = M["act_ev"]
                    VB(vev)
                    bankfree[bu] = (dv, M["act_ev"])
                    act_m = M["act_ev"]

                    def ysp(y, row0=row0, fc=fc, t0=t0, tc=tc, act_m=act_m):
                        W(y, "y", dv, act_m)
                        r = row0 + fc * 128
                        qx.pre(y, "y")
                        ins = y.dma_start(out=act_sp[r:r + 128, t0:t0 + TC],
                                          in_=stg[:, tc % 2, :])
                        spill_m[tc % 2] = qx.fin(ins)
                    YB(ysp)
                ring_used(slot_g, pe, M["g_mm"])
                ring_used(slot_u, pe, M["u_mm"])
            if scaled:
                M["wb_used_pe"] = M["wb_mm"]

        for n in range(NE):
            do_expert(n, gate_tl[n * FC:(n + 1) * FC], up_tl[n * FC:(n + 1) * FC],
                      FC, n * FC * 128, True)
        do_expert(0, shg_tl, shu_tl, SFC, NE * FC * 128, False)
        spillA = qx.snap()
        phaseA_pe = pe.n

        def actview(j):
            if j < 64:
                return h_full[:].rearrange("p c t -> p (c t)")[:, j * TC:(j + 1) * TC]
            return r1[:, (j - 64) * TC:(j - 63) * TC]

        store_marks = []
        dn_pe = {}
        nx_store = {}
        xo_used = {}
        for tc in range(NTC):
            t0 = tc * TC
            act_m = {}

            def yact(y, tc=tc, t0=t0):
                if tc == 0:
                    WM(y, "y", spillA)
                    W(y, "y", pe, phaseA_pe)
                for j in range(NKC):
                    if tc > 0:
                        WM(y, "y", M[("dn15", tc - 1, j // 16)])
                    qx.pre(y, "y")
                    ins = y.dma_start(out=actview(j), in_=act_sp[j * 128:(j + 1) * 128,
                                                                 t0:t0 + TC])
                    act_m[j] = qx.fin(ins)
            YB(yact)

            for hc in range(HC):
                xo_m = {}

                def yxo(y, hc=hc, t0=t0, tc=tc):
                    W(y, "y", dv, xo_used.get((tc, hc - 2), xo_used.get((tc - 1, hc - 2 + HC))))
                    qx.pre(y, "y")
                    ins = y.dma_start(out=xo[:, hc % 2, :],
                                      in_=src_old[hc * 128:(hc + 1) * 128, t0:t0 + TC])
                    xo_m[0] = qx.fin(ins)
                YB(yxo)

                bank = 4 + (hc % 2)
                j = 0
                jg = 0
                first = True
                while j < NKC:
                    nch = min(16, NKC - j)
                    blk_i = hc * 7 + jg
                    ck = ("dwc", hc, jg)
                    pend_store = None
                    if ck in M:
                        slot, rdy = ring_load_flat(dwb_sp[blk_i], nch * 128, M[ck])
                    else:
                        slot, rdy = ring_load_flat(dwn_tl[blk_i], nch * 128, None)

                        def sst(s_, slot=slot, rdy=list(rdy), blk_i=blk_i,
                                nch=nch, ck=ck):
                            WM(s_, "s", rdy)
                            qx.pre(s_, "s")
                            ins = s_.dma_start(
                                out=dwb_sp[blk_i][:, 0:nch * 128],
                                in_=ring[slot][:, 0:nch, :].rearrange("p c n -> p (c n)"))
                            M[ck] = qx.fin(ins)
                        SB_(sst)
                        pend_store = M[ck]

                    def tdn(t, slot=slot, rdy=rdy, j0=j, nch=nch, bank=bank, tc=tc,
                            first=first, hc=hc, jg=jg):
                        WM(t, "t", rdy)
                        if first and bankfree[bank] is not None:
                            W(t, "t", *bankfree[bank])
                        for c in range(nch):
                            jj = j0 + c
                            WM(t, "t", act_m[jj])
                            ins = t.matmul(pwb[bank - 4][:, :], ring[slot][:, c, :],
                                           actview(jj), start=(jj == 0), stop=(jj == NKC - 1),
                                           skip_group_check=True)
                        M["dn_mm"] = pe.inc(ins)
                        dn_pe[tc] = pe.n
                        if hc == HC - 1:
                            M[("dn15", tc, jg)] = (pe, pe.n)
                    TB(tdn)
                    ring_used(slot, pe, M["dn_mm"])
                    if pend_store is not None:
                        ring_also_used(slot, pend_store)
                    first = False
                    j += nch
                    jg += 1
                mm_mark = M["dn_mm"]
                if stats is not None:
                    stats.after_mm(tc, hc)

                def vev(v, bank=bank, hc=hc, tc=tc, mm_mark=mm_mark):
                    W(v, "v", pe, mm_mark)
                    WM(v, "v", xo_m[0])
                    WM(v, "v", nx_store.get((tc, hc - 2), nx_store.get((tc - 1, hc - 2 + HC))))
                    if stats is not None:
                        stats.nx_guard(v, "v", tc, hc)
                    ins = v.tensor_add(nx[:, hc % 2, :], pwb[bank - 4][:, :], xo[:, hc % 2, :])
                    xo_used[(tc, hc)] = dv.inc(ins)
                VB(vev)
                ev = dv.n
                bankfree[bank] = (dv, ev)
                if stats is not None:
                    stats.after_vev(tc, hc, ev)

                def yst(y, hc=hc, t0=t0, tc=tc, ev=ev):
                    W(y, "y", dv, ev)
                    qx.pre(y, "y")
                    ins = y.dma_start(out=dst[hc * 128:(hc + 1) * 128, t0:t0 + TC],
                                      in_=nx[:, hc % 2, :])
                    nx_store[(tc, hc)] = qx.fin(ins)
                YB(yst)
            if stats is not None:
                stats.end_tc(tc)
            if p2 is not None:
                p2.emit(tc)
            store_marks.append(qx.snap())
        return store_marks, nx_store

    def attn_b(h_marks, src_old, dst, stats=None, p2=None):
        h_all = h_marks[NTC - 1]
        pe_entry = pe.n

        def vguard(v):
            W(v, "v", pe, pe_entry)
        VB(vguard)

        def sguard(s):
            W(s, "s", pe, pe_entry)
        SB_(sguard)

        bias_marks = []

        def gbi(g):
            W(g, "g", pe, pe_entry)
            for i, src_d in enumerate((bq_r, bk_r, bv_r, bo_r)):
                qw.pre(g, "g")
                ins = g.dma_start(out=r1_bias[0:1, i, :], in_=src_d[0:1, :])
                bias_marks.append(qw.fin(ins))
        GB(gbi)
        bias_m = bias_marks
        bqv = r1_bias[0:1, 0, :]
        bkv = r1_bias[0:1, 1, :]
        bvv = r1_bias[0:1, 2, :]
        bov = r1_bias[0:1, 3, :]

        qv = r1_q
        kv_ = r1_k
        nm_dram = {}
        exp_done = {}
        av_prev = {}
        stq_prev = {}
        vg_av = {}

        for head in range(NH):
            grp = head // 4
            if head % 4 == 0:
                vslots = []
                for q4 in range(4):
                    s_, r_ = ring_load_flat(wvb_tl[grp * 4 + q4], H, None)
                    vslots.append((s_, r_))
                rdy_all = [r_ for _, r_ in vslots]
                for ts in range(NSUB):
                    bank = 4 + (ts % 2)

                    def tv(t, ts=ts, bank=bank, vslots=vslots, rdy_all=rdy_all, grp=grp):
                        if bankfree[bank] is not None:
                            W(t, "t", *bankfree[bank])
                        W(t, "t", dv, h_marks[ts // 4])
                        WM(t, "t", [rdy_all, bias_m])
                        for q4 in range(4):
                            sl = vslots[q4][0]
                            for hc in range(HC):
                                t.matmul(pwb[bank - 4][:, q4 * 128:(q4 + 1) * 128],
                                         h_full[:, hc, ts * 128:(ts + 1) * 128],
                                         ring[sl][:, hc, :], start=(hc == 0), stop=False,
                                         skip_group_check=True)
                            ins = t.matmul(pwb[bank - 4][:, q4 * 128:(q4 + 1) * 128],
                                           ones128b[0:1, :],
                                           bvv[0:1, grp * 512 + q4 * 128:
                                               grp * 512 + (q4 + 1) * 128],
                                           start=False, stop=True, skip_group_check=True)
                        M["v_mm"] = pe.inc(ins)
                    TB(tv)
                    mmv = M["v_mm"]

                    def sv(s, ts=ts, bank=bank, mmv=mmv, grp=grp):
                        W(s, "s", pe, mmv)
                        W(s, "s", pe, vg_av.get(grp - 1))
                        ins = s.activation(r1_vg[:, ts, :], pwb[bank - 4][:, :], AF.Copy)
                        M["v_ev"] = ac.inc(ins)
                    SB_(sv)
                    bankfree[bank] = (ac, M["v_ev"])
                for sl, _ in vslots:
                    ring_used(sl, pe, M["v_mm"])
                vg_mark = M["v_ev"]

            slot_q, rdy_q = ring_load_flat(wqh_tl[head], H, None)
            slot_k, rdy_k = ring_load_flat(wkh_tl[head], H, None)
            for tc in range(NTC):
                t0 = tc * TC
                b_q = 0 + (tc % 2)
                b_k = 2 + (tc % 2)

                def tqk(t, tc=tc, t0=t0, slot_q=slot_q, slot_k=slot_k, rdy_q=rdy_q,
                        rdy_k=rdy_k, head=head, b_q=b_q, b_k=b_k):
                    WM(t, "t", [rdy_q, rdy_k, bias_m])
                    W(t, "t", dv, h_marks[tc])
                    if bankfree[b_q] is not None:
                        W(t, "t", *bankfree[b_q])
                    for hc in range(HC):
                        t.matmul(pb[b_q][:, :], ring[slot_q][:, hc, :],
                                 h_full[:, hc, t0:t0 + TC], start=(hc == 0), stop=False,
                                 skip_group_check=True)
                    ins = t.matmul(pb[b_q][:, :], bqv[0:1, head * 128:(head + 1) * 128],
                                   ones_row[0:1, :], start=False, stop=True,
                                   skip_group_check=True)
                    M["q_mm"] = pe.inc(ins)
                    if bankfree[b_k] is not None:
                        W(t, "t", *bankfree[b_k])
                    for hc in range(HC):
                        t.matmul(pb[b_k][:, :], ring[slot_k][:, hc, :],
                                 h_full[:, hc, t0:t0 + TC], start=(hc == 0), stop=False,
                                 skip_group_check=True)
                    ins = t.matmul(pb[b_k][:, :], bkv[0:1, head * 128:(head + 1) * 128],
                                   ones_row[0:1, :], start=False, stop=True,
                                   skip_group_check=True)
                    M["k_mm"] = pe.inc(ins)
                TB(tqk)
                if head == NH - 1:
                    M[("hfree_b", tc)] = (pe, pe.n)

                def sqk(s, tc=tc, t0=t0, b_q=b_q, b_k=b_k, head=head):
                    W(s, "s", pe, M["k_mm"])
                    W(s, "s", pe, stq_prev.get(head - 1))
                    s.activation(qv[:, t0:t0 + TC], pb[b_q][:, :], AF.Copy, scale=ISQD)
                    ins = s.activation(kv_[:, t0:t0 + TC], pb[b_k][:, :], AF.Copy)
                    M["qk_ev"] = ac.inc(ins)
                SB_(sqk)
                bankfree[b_q] = (ac, M["qk_ev"])
                bankfree[b_k] = (ac, M["qk_ev"])
            ring_used(slot_q, pe, M["q_mm"])
            ring_used(slot_k, pe, M["k_mm"])
            qk_mark = M["qk_ev"]

            red = {}
            for qs in range(NSUB):
                live = (qs + 1) * 128
                use_pw = (qs % 2 == 0)
                banks = [4, 5, 6, 7] if use_pw else [0, 1, 2, 3]
                nq = ceil_div(live, TC)

                def tsm(t, qs=qs, live=live, use_pw=use_pw, banks=banks, nq=nq,
                        qk_mark=qk_mark):
                    W(t, "t", ac, qk_mark)
                    for b in banks[:nq]:
                        if bankfree[b] is not None:
                            W(t, "t", *bankfree[b])
                    for kq in range(nq):
                        n = min(TC, live - kq * TC)
                        dst_ = pw[:, kq * TC:kq * TC + n] if use_pw else pb[kq][:, 0:n]
                        ins = t.matmul(dst_, qv[:, qs * 128:(qs + 1) * 128],
                                       kv_[:, kq * TC:kq * TC + n], start=True, stop=True,
                                       skip_group_check=True)
                    M["sm_mm"] = pe.inc(ins)
                    stq_prev[head] = pe.n
                TB(tsm)
                mm_m = M["sm_mm"]

                def vsm(v, qs=qs, live=live, use_pw=use_pw, nq=nq, mm_m=mm_m):
                    W(v, "v", pe, mm_m)
                    if qs == 0:
                        W(v, "v", pe, M.get("mxT"))
                    if use_pw:
                        ins = v.reduce_max(maxs_t[:, qs:qs + 1], pw[:, 0:live], axis=AX.X)
                        v.drain()
                    else:
                        for kq in range(nq):
                            n = min(TC, live - kq * TC)
                            v.reduce_max(mxc[:, kq:kq + 1], pb[kq][:, 0:n], axis=AX.X)
                        v.drain()
                        if nq == 1:
                            ins = v.tensor_copy(maxs_t[:, qs:qs + 1], mxc[:, 0:1])
                        elif nq == 2:
                            ins = v.tensor_max(maxs_t[:, qs:qs + 1], mxc[:, 0:1], mxc[:, 1:2])
                        elif nq == 3:
                            v.tensor_max(mxc[:, 4:5], mxc[:, 0:1], mxc[:, 1:2])
                            v.drain()
                            ins = v.tensor_max(maxs_t[:, qs:qs + 1], mxc[:, 4:5], mxc[:, 2:3])
                        else:
                            v.tensor_max(mxc[:, 4:5], mxc[:, 0:1], mxc[:, 1:2])
                            v.tensor_max(mxc[:, 5:6], mxc[:, 2:3], mxc[:, 3:4])
                            v.drain()
                            ins = v.tensor_max(maxs_t[:, qs:qs + 1], mxc[:, 4:5], mxc[:, 5:6])
                    red[qs] = dv.inc(ins)
                VB(vsm)
                for b in banks[:nq]:
                    bankfree[b] = (dv, red[qs])
            max_mark = dv.n

            def tmx(t, max_mark=max_mark):
                W(t, "t", dv, max_mark)
                if bankfree[0] is not None:
                    W(t, "t", *bankfree[0])
                ins = t.transpose(pb[0][0:16, 0:128], maxs_t[:, 0:16], ident[:, :])
                M["mxT"] = pe.inc(ins)
            TB(tmx)

            def vmx(v, head=head):
                W(v, "v", pe, M["mxT"])
                WM(v, "v", nm_dram.get(head - 1))
                ins = v.tensor_scalar_mul(mT_t[0:16, 0:128], pb[0][0:16, 0:128], -1.0)
                M["ngT"] = dv.inc(ins)
            VB(vmx)
            bankfree[0] = (dv, M["ngT"])

            def ymx(y, head=head):
                W(y, "y", dv, M["ngT"])
                qx.pre(y, "y")
                ins = y.dma_start(out=max_sp[head * NSUB:(head + 1) * NSUB, :],
                                  in_=mT_t[0:16, 0:128])
                nm_dram[head] = qx.fin(ins)
            YB(ymx)

            def gnm(g, head=head):
                WM(g, "g", nm_dram[head])
                W(g, "g", ac, exp_done.get(head - 2))
                qw.pre(g, "g")
                ins = g.dma_start(out=r1_nm[0:1, head % 2, :],
                                  in_=max_sp[head * NSUB:(head + 1) * NSUB, :]
                                  .rearrange("s p -> (s p)").unsqueeze(0))
                M["nm_ld"] = qw.fin(ins)
            GB(gnm)
            nm_m = M["nm_ld"]

            for qc in range(NTC):
                q0 = qc * TC
                kmax = 4 * qc + 4
                exp_m = {}
                for g0 in range(0, kmax, 2):
                    pair = (g0 // 2) % 2
                    for kci in range(2):
                        kc = g0 + kci
                        bank = 4 + 2 * pair + kci

                        def tst(t, kc=kc, q0=q0, bank=bank, nm_m=nm_m, head=head):
                            WM(t, "t", nm_m)
                            if bankfree[bank] is not None:
                                W(t, "t", *bankfree[bank])
                            t.matmul(pwb[bank - 4][:, :], kv_[:, kc * 128:(kc + 1) * 128],
                                     qv[:, q0:q0 + TC], start=True, stop=False,
                                     skip_group_check=True)
                            ins = t.matmul(pwb[bank - 4][:, :], ones128b[0:1, :],
                                           r1_nm[0:1, head % 2, q0:q0 + TC],
                                           start=False, stop=True, skip_group_check=True)
                            M["st_mm"] = pe.inc(ins)
                            stq_prev[head] = pe.n
                        TB(tst)
                    st_m = M["st_mm"]

                    def sexg(s, g0=g0, pair=pair, st_m=st_m, head=head):
                        W(s, "s", pe, st_m)
                        W(s, "s", pe, av_prev.get(head, av_prev.get(head - 1)))
                        if M.get("dmask") is not None:
                            W(s, "s", dv, M["dmask"])
                        ins = s.activation(
                            r1_pT[:, g0:g0 + 2, :],
                            pw[:, pair * 1024:(pair + 1) * 1024]
                            .rearrange("p (c t) -> p c t", c=2),
                            AF.Exp)
                        M["expg"] = ac.inc(ins)
                    SB_(sexg)
                    for kci in range(2):
                        bankfree[4 + 2 * pair + kci] = (ac, M["expg"])
                    if g0 < 4 * qc:
                        for kci in range(2):
                            exp_m[g0 + kci] = (ac, M["expg"])
                    else:
                        pos0 = g0 - 4 * qc

                        def vms(v, g0=g0, pos0=pos0, head=head):
                            W(v, "v", ac, M["expg"])
                            ins = v.tensor_mul(r1_pT[:, g0:g0 + 2, :],
                                               r1_pT[:, g0:g0 + 2, :],
                                               masks[:, pos0:pos0 + 2, :])
                            M["dmask"] = dv.inc(ins)
                        VB(vms)
                        for kci in range(2):
                            exp_m[g0 + kci] = (dv, M["dmask"])
                exp_done[head] = ac.n

                def tsum(t, qc=qc, kmax=kmax, head=head, vg_mark=vg_mark):
                    if bankfree[1] is not None:
                        W(t, "t", *bankfree[1])
                    for kc in range(kmax):
                        W(t, "t", exp_m[kc][0], exp_m[kc][1])
                        ins = t.matmul(pb[1][0:1, :], ones_col[:, 0:1], r1_pT[:, kc, :],
                                       start=(kc == 0), stop=(kc == kmax - 1),
                                       skip_group_check=True)
                    M["sum_mm"] = pe.inc(ins)
                    bz = 2 + (qc % 2)
                    if bankfree[bz] is not None:
                        W(t, "t", *bankfree[bz])
                    W(t, "t", ac, vg_mark)
                    for kc in range(kmax):
                        ins = t.matmul(pb[bz][:, :],
                                       r1_vg[:, kc, (head % 4) * 128:(head % 4 + 1) * 128],
                                       r1_pT[:, kc, :], start=(kc == 0),
                                       stop=(kc == kmax - 1), skip_group_check=True)
                    M["av_mm"] = pe.inc(ins)
                    av_prev[head] = pe.n
                    vg_av[head // 4] = pe.n
                TB(tsum)
                sum_m, av_m = M["sum_mm"], M["av_mm"]

                def vsum(v, sum_m=sum_m):
                    W(v, "v", pe, sum_m)
                    ins = v.reciprocal(srow2[0:1, :], pb[1][0:1, :])
                    M["rc"] = dv.inc(ins)
                VB(vsum)
                bankfree[1] = (dv, M["rc"])

                def trb(t):
                    if bankfree[0] is not None:
                        W(t, "t", *bankfree[0])
                    W(t, "t", dv, M["rc"])
                    ins = t.matmul(pb[0][:, :], ones128f[0:1, :], srow2[0:1, :],
                                   start=True, stop=True, skip_group_check=True)
                    M["rb_mm"] = pe.inc(ins)
                TB(trb)
                rb_m = M["rb_mm"]

                def vz(v, qc=qc, rb_m=rb_m, av_m=av_m):
                    W(v, "v", pe, max(rb_m, av_m))
                    WM(v, "v", M.get(("z_st", qc % 2)))
                    v.tensor_copy(rb_t[:, :], pb[0][:, :])
                    v.drain()
                    ins = v.tensor_mul(stg[:, qc % 2, :], pb[2 + (qc % 2)][:, :], rb_t[:, :])
                    M["z_ev"] = dv.inc(ins)
                VB(vz)
                bankfree[0] = (dv, M["z_ev"])
                bankfree[2 + (qc % 2)] = (dv, M["z_ev"])
                z_m = M["z_ev"]

                def yz(y, head=head, qc=qc, q0=q0, z_m=z_m):
                    W(y, "y", dv, z_m)
                    qx.pre(y, "y")
                    ins = y.dma_start(out=z_sp[head * 128:(head + 1) * 128, q0:q0 + TC],
                                      in_=stg[:, qc % 2, :])
                    M[("z_st", qc % 2)] = qx.fin(ins)
                YB(yz)

        return out_proj(wob_tl, z_sp, src_old, dst, qx.snap(), NH, bov,
                        stats=stats, xo_guard=(pe, pe.n), p2=p2, cache_tag="wob")

    setup()
    hm = layernorm(xT_in, None)
    if n_blocks > 1:
        st_a = EpiStats(0)
        p2a = LNP2(x_res, st_a, {}, None)
    else:
        st_a = p2a = None
    sm, nxm = attn_a(hm, xT_in, x_res if n_blocks > 1 else xT_out,
                     stats=st_a, p2=p2a)
    if n_blocks >= 2:
        hm = p2a.emit_all()
        st_f = EpiStats(1) if n_blocks > 2 else None
        sm, nxm = ffn(hm, x_res, x_res if n_blocks > 2 else xT_out, stats=st_f)
        ffn_guard = (pe, pe.n)
    if n_blocks >= 3:
        p2f = LNP2(x_res, st_f, nxm, lambda tc: ffn_guard)
        hm = p2f.emit_all()
        if n_blocks > 3:
            st_b = EpiStats(0)
            p2b = LNP2(x_res, st_b, {}, lambda tc: M[("hfree_b", tc)])
        else:
            st_b = p2b = None
        sm, nxm = attn_b(hm, x_res, x_res if n_blocks > 3 else xT_out,
                         stats=st_b, p2=p2b)
    if n_blocks >= 4:
        hm = p2b.emit_all()
        sm, nxm = ffn(hm, x_res, xT_out)

    def yfin(y):
        WM(y, "y", qx.snap())
        WM(y, "y", qw.snap())
    YB(yfin)

    es.close()
    return nc


def _tile_cols(Wt, nfc):
    K = Wt.shape[0]
    c = K // 128
    R = Wt.reshape(c, 128, nfc, 128)
    return np.ascontiguousarray(R.transpose(2, 1, 0, 3).reshape(nfc, 128, K))


def _prep_inputs(inputs):
    f = np.ascontiguousarray
    ws = {}
    ws["wr_tl"] = _tile_cols(inputs["L0_attn_a_Wr"].T, HC)
    ws["wk_tl"] = _tile_cols(inputs["L0_attn_a_Wk"].T, HC)
    ws["wv_tl"] = _tile_cols(inputs["L0_attn_a_Wv"].T, HC)
    ws["wo_tl"] = _tile_cols(inputs["L0_attn_a_Wo"].T, HC)
    ws["td_pt"] = f(inputs["L0_attn_a_time_decay"].T)
    ws["tf_pt"] = f(inputs["L0_attn_a_time_first"].T)
    ws["tmk_pt"] = f(inputs["L0_attn_a_time_mix_k"].reshape(16, 128).T)
    ws["tmv_pt"] = f(inputs["L0_attn_a_time_mix_v"].reshape(16, 128).T)
    ws["tmr_pt"] = f(inputs["L0_attn_a_time_mix_r"].reshape(16, 128).T)
    wq = inputs["L0_attn_b_W_Q"]
    ws["wqh_tl"] = f(wq.reshape(NH, HC, 128, DH).transpose(0, 2, 1, 3)
                     .reshape(NH, 128, H))
    wk = inputs["L0_attn_b_W_K"]
    ws["wkh_tl"] = f(wk.reshape(NH, HC, 128, DH).transpose(0, 2, 1, 3)
                     .reshape(NH, 128, H))
    ws["wvb_tl"] = _tile_cols(inputs["L0_attn_b_W_V"].transpose(1, 0, 2)
                              .reshape(H, H), HC)
    ws["wob_tl"] = _tile_cols(inputs["L0_attn_b_W_O"].reshape(H, H), HC)
    ws["bq_r"] = f(inputs["L0_attn_b_b_Q"].reshape(1, H))
    ws["bk_r"] = f(inputs["L0_attn_b_b_K"].reshape(1, H))
    ws["bv_r"] = f(inputs["L0_attn_b_b_V"].reshape(1, H))
    ws["bo_r"] = f(inputs["L0_attn_b_b_O"].reshape(1, H))
    ws["gw_T"] = f(inputs["L0_ffn_gate_weight"].T)
    gate_wT = inputs["L0_ffn_gate_w"].transpose(0, 2, 1)
    up_wT = inputs["L0_ffn_up_w"].transpose(0, 2, 1)
    ws["gate_tl"] = np.concatenate([_tile_cols(gate_wT[n], FC) for n in range(NE)])
    ws["up_tl"] = np.concatenate([_tile_cols(up_wT[n], FC) for n in range(NE)])
    ws["shg_tl"] = _tile_cols(inputs["L0_ffn_sh_gate_w"].T, SFC)
    ws["shu_tl"] = _tile_cols(inputs["L0_ffn_sh_up_w"].T, SFC)
    D = np.concatenate([inputs["L0_ffn_down_w"].transpose(0, 2, 1).reshape(NE * FI, H),
                        inputs["L0_ffn_sh_down_w"].T], axis=0)
    R = D.reshape(NKC, 128, HC, 128).transpose(2, 0, 1, 3)
    Rp = np.zeros((HC, 112, 128, 128), dtype=np.float32)
    Rp[:, :NKC] = R
    ws["dwn_tl"] = f(Rp.reshape(HC, 7, 16, 128, 128).transpose(0, 1, 3, 2, 4)
                     .reshape(HC * 7, 128, H))
    ws["ident_in"] = np.eye(128, dtype=np.float32)
    m = np.zeros((4, 128, TC), dtype=np.float32)
    for p in range(4):
        kk = p * 128 + np.arange(128)[:, None]
        qq = np.arange(TC)[None, :]
        m[p] = (kk <= qq).astype(np.float32)
    ws["masks_in"] = f(m.transpose(1, 0, 2).reshape(128, 4 * TC))
    return ws


def kernel(**inputs):
    n_blocks = int(inputs.pop("_n_blocks", 4))
    ws = _prep_inputs(inputs)
    x = inputs["L0_attn_a_hidden_states"]
    nc = build_program(n_blocks)
    in_maps = []
    for b in range(8):
        m = dict(ws)
        m["xT_in"] = np.ascontiguousarray(x[b].T)
        in_maps.append(m)
    res = run_bass_kernel_spmd(nc, in_maps, core_ids=list(range(8)))
    out = np.stack([np.ascontiguousarray(r["xT_out"].T) for r in res.results])
    return out
